# version 45
# baseline (speedup 1.0000x reference)
import numpy as np
import ml_dtypes

import concourse.bass as bass
import concourse.mybir as mybir
from concourse.bass_utils import run_bass_kernel_spmd

F32 = mybir.dt.float32
BF16 = mybir.dt.bfloat16
AF = mybir.ActivationFunctionType
ALU = mybir.AluOpType
AX = mybir.AxisListType
bfdt = ml_dtypes.bfloat16

D = 1024
H = 16
NP = H // 2
RANK = 16

ENGS = ("sync", "gpsimd", "tensor", "scalar", "vector")


class Sched:
    def __init__(self):
        self.events = {e: [] for e in ENGS}
        self.cnt = {}

    def sem(self, name):
        self.cnt.setdefault(name, 0)

    def ev(self, eng, fn, waits=(), incs=()):
        out = {}
        for s, a in incs:
            self.cnt[s] += a
            out[s] = self.cnt[s]
        self.events[eng].append((fn, list(waits), list(incs)))
        return out

    def val(self, s):
        return self.cnt[s]


def _host_constants(inp, S):
    f32 = np.float32
    W_uq = np.asarray(inp["L0_attn_W_uq"], f32)
    W_uk = np.asarray(inp["L0_attn_W_uk"], f32)
    b_uq = np.asarray(inp["L0_attn_b_uq"], f32)
    b_uk = np.asarray(inp["L0_attn_b_uk"], f32)
    W_qr = np.asarray(inp["L0_attn_W_qr"], f32)
    W_kr = np.asarray(inp["L0_attn_W_kr"], f32)
    b_qr = np.asarray(inp["L0_attn_b_qr"], f32)
    b_kr = np.asarray(inp["L0_attn_b_kr"], f32)

    def cat_nope(W, b):
        Wc = np.zeros((W.shape[0], 128 * NP), f32)
        bc = np.zeros((128, NP), f32)
        for p in range(NP):
            Wc[:, 128 * p + 0:128 * p + 48] = W[:, 96 * p:96 * p + 48]
            Wc[:, 128 * p + 64:128 * p + 112] = W[:, 96 * p + 48:96 * p + 96]
            bc[0:48, p] = b[96 * p:96 * p + 48]
            bc[64:112, p] = b[96 * p + 48:96 * p + 96]
        return Wc, bc

    Wq_cat, bq_cat = cat_nope(W_uq, b_uq)
    Wk_cat, bk_cat = cat_nope(W_uk, b_uk)

    def jm(Wr, br):
        nin = Wr.shape[0]
        Wj = np.zeros((nin, 384), f32)
        bj = np.zeros((128, 3), f32)
        for h in range(16):
            for j in range(8):
                Wj[:, 0 * 128 + 16 * j + h] = Wr[:, 16 * h + j]
                bj[16 * j + h, 0] = br[16 * h + j]
                js = j + 4 if j < 4 else j - 4
                Wj[:, 1 * 128 + 16 * j + h] = Wr[:, 16 * h + js]
                bj[16 * j + h, 1] = br[16 * h + js]
                Wj[:, 2 * 128 + 16 * j + h] = Wr[:, 16 * h + 8 + j]
                bj[16 * j + h, 2] = br[16 * h + 8 + j]
        return Wj, bj

    Wqr_jm, bqr_jm = jm(W_qr, b_qr)
    Wkr_jm, bkr_jm = jm(W_kr, b_kr)

    inv_freq = (1.0 / (10000.0 ** (np.arange(0, 8, 2, dtype=f32) / np.float32(8.0)))).astype(f32)
    t = (np.arange(S, dtype=f32) / np.float32(40.0)).astype(f32)
    freqs = (t[:, None] * inv_freq[None, :]).astype(f32)
    cos4 = np.cos(freqs).astype(f32)
    sin4 = np.sin(freqs).astype(f32)
    cosT = np.zeros((128, S), f32)
    sinT = np.zeros((128, S), f32)
    for h in range(16):
        for j in range(8):
            r = 16 * j + h
            cosT[r] = cos4[:, j % 4]
            sinT[r] = (-sin4[:, j % 4]) if j < 4 else sin4[:, j % 4]

    bpad = np.zeros((32, 1024), f32)
    bpad[0:16] = np.asarray(inp["L0_ffn_b"], f32)
    ones32 = np.zeros((32, 128), f32)
    ones32[0] = 1.0

    return dict(
        Wq_cat=Wq_cat, bq_cat=bq_cat, Wk_cat=Wk_cat, bk_cat=bk_cat,
        Wqr_jm=Wqr_jm, bqr_jm=bqr_jm, Wkr_jm=Wkr_jm, bkr_jm=bkr_jm,
        cosT=cosT.astype(bfdt), sinT=sinT.astype(bfdt), bpad=bpad, ones32=ones32,
        ident=np.eye(128, dtype=f32),
        bdkv=np.asarray(inp["L0_attn_b_dkv"], f32).reshape(128, 1),
        bdq=np.asarray(inp["L0_attn_b_dq"], f32).reshape(128, 1),
        b_uv=np.asarray(inp["L0_attn_b_uv"], f32).reshape(1, 1024),
        b_o=np.asarray(inp["L0_attn_b_o"], f32).reshape(1, 1024),
        b_ffn=np.asarray(inp["L0_ffn_bias"], f32).reshape(1, 1024),
        W_dkv=np.asarray(inp["L0_attn_W_dkv"], f32),
        W_dq=np.asarray(inp["L0_attn_W_dq"], f32),
        W_uv=np.asarray(inp["L0_attn_W_uv"], f32),
        W_o=np.asarray(inp["L0_attn_W_o"], f32),
        W_ffn=np.asarray(inp["L0_ffn_kernel"], f32),
        a_l=np.asarray(inp["L0_ffn_a"], f32),
    )


def build_program(S):
    NQT = S // 512
    NKC = S // 128
    NSP = S // 512
    nc = bass.Bass("TRN2", target_bir_lowering=False)

    dr = {}
    def din(name, shape, dt=F32):
        dr[name] = nc.dram_tensor(name, list(shape), dt, kind="ExternalInput").ap()
    din("h", (S, D))
    din("W_dkv", (D, 128)); din("W_dq", (D, 128))
    din("Wq_cat", (128, 1024)); din("Wk_cat", (128, 1024))
    din("Wqr_jm", (128, 384)); din("Wkr_jm", (D, 384))
    din("W_uv", (128, 1024)); din("W_o", (D, D)); din("W_ffn", (D, D))
    din("a_l", (D, RANK)); din("bpad", (32, 1024))
    din("cosT", (128, S), BF16); din("sinT", (128, S), BF16)
    din("ident", (128, 128)); din("ones32", (32, 128))
    din("bdkv", (128, 1)); din("bdq", (128, 1))
    din("bq_cat", (128, NP)); din("bk_cat", (128, NP))
    din("bqr_jm", (128, 3)); din("bkr_jm", (128, 3))
    din("b_uv", (1, 1024)); din("b_o", (1, 1024)); din("b_ffn", (1, 1024))
    out_d = nc.dram_tensor("out", [S, D], F32, kind="ExternalOutput").ap()
    rden = nc.dram_tensor("rden", [NQT * NP, 1024], F32, kind="Internal").ap()
    o_dram = nc.dram_tensor("o_dram", [NQT * NP * 128, 512], BF16, kind="Internal").ap()
    rotk_scr = nc.dram_tensor("rotk_scr", [256, S], BF16, kind="Internal").ap()
    rotq_scr = nc.dram_tensor("rotq_scr", [256, 1024], BF16, kind="Internal").ap()
    cq_scr = nc.dram_tensor("cq_scr", [128, S], BF16, kind="Internal").ap()
    x1_scr = nc.dram_tensor("x1_scr", [512, D], F32, kind="Internal").ap()
    wo_p = nc.dram_tensor("wo_p", [128, 8 * D], BF16, kind="Internal").ap()
    wf_p = nc.dram_tensor("wf_p", [128, 8 * D], BF16, kind="Internal").ap()
    p0_scr = nc.dram_tensor("p0_scr", [1, 192], F32, kind="Internal").ap()

    sbt = nc.alloc_sbuf_tensor
    Ksb = sbt("Ksb", [128, NP * S], BF16)
    Vsb = sbt("Vsb", [128, NKC * 1040], BF16)
    wq_sb = sbt("wq_sb", [128, 1024], BF16)
    wqr_sb = sbt("wqr_sb", [128, 384], BF16)
    wkr_sb = sbt("wkr_sb", [128, 3072], BF16)
    a_sb = sbt("a_sb", [128, 128], BF16)
    bpad_sb = sbt("bpad_sb", [32, 1024], BF16)
    ident_sb = sbt("ident_sb", [128, 128], F32)
    ones1_sb = sbt("ones1_sb", [128, 1], BF16)
    bdkv_sb = sbt("bdkv_sb", [128, 1], F32)
    bdq_sb = sbt("bdq_sb", [128, 1], F32)
    bqc_sb = sbt("bqc_sb", [128, NP], F32)
    bkc_sb = sbt("bkc_sb", [128, NP], F32)
    bqr_sb = sbt("bqr_sb", [128, 3], F32)
    bkr_sb = sbt("bkr_sb", [128, 3], F32)
    eps_sb = sbt("eps_sb", [128, 1], F32)
    scl_sb = sbt("scl_sb", [128, 1], F32)
    rscl_sb = sbt("rscl_sb", [128, 1], F32)
    s32_sb = sbt("s32_sb", [32, 2], F32)
    gmax_sb = sbt("gmax_sb", [128, 9], F32)
    Qsb = sbt("Qsb", [128, NP * 512], BF16)
    _ = None
    Psb = sbt("Psb", [128, 3072], BF16)
    osb = sbt("osb", [128, NP * 512], BF16)
    cos_sb = sbt("cos_sb", [128, 512], BF16)
    sin_sb = sbt("sin_sb", [128, 512], BF16)
    cq_buf = sbt("cq_buf", [128, 1024], BF16)
    bcast_sb = sbt("bcast_sb", [128, 512], F32)
    c0b_sb = sbt("c0b_sb", [128, 512], BF16)
    c0s_sb = sbt("c0s_sb", [128, 512], BF16)
    wbuf = sbt("wbuf", [128, 8192], BF16)
    bbuf = sbt("bbuf", [128, 1024], BF16)
    h2 = sbt("h2", [128, 2048], F32)
    tA = sbt("tA", [128, 1024], F32)
    x1T2 = sbt("x1T2", [128, 2048], BF16)
    zb_sb = sbt("zb_sb", [32, 128], BF16)
    st1 = sbt("st1", [128, 1], F32)
    st2 = sbt("st2", [128, 1], F32)
    stm = sbt("stm", [128, 1], F32)
    stv = sbt("stv", [128, 1], F32)
    str_ = sbt("str_", [128, 1], F32)
    oun_sb = sbt("oun_sb", [128, 1024], BF16)
    assert nc.sbuf_bytes_remaining >= 0

    PS = nc.alloc_psum_tensor("PS", [128, 4096], F32)

    hT = Qsb
    wkcat_sb = osb[:, 0:1024]
    wdkv_sb = osb[:, 1024:2048]
    wdq_sb = osb[:, 2048:3072]
    wuv_sb = osb[:, 3072:4096]
    wbufF = (wbuf.ap() if hasattr(wbuf, "ap") else wbuf).bitcast(F32)
    wstage = (tA, wbufF[:, 0:1024])
    wstage_out = x1T2[:, 0:1024]
    qacc = x1T2[:, 1024:2048]
    qcmp = Psb[:, 0:1024]
    qsgn = Psb[:, 1024:2048]
    PSB4 = [Psb[:, 0:1024], Psb[:, 1024:2048], Psb[:, 2048:3072],
            wkr_sb[:, 0:1024]]
    qabs = wbufF[:, 1024:2048]

    S_ = Sched()
    for s in ("qSm", "qW0", "qW1", "qT", "qC0", "qC1", "qR", "qWsc", "qKb",
              "qCw", "qX", "qO", "qH0", "qH1", "qTH0", "qTH1", "qB0", "qB1", "qB2", "qB3", "qBB", "qBC", "qOD", "qOL",
              "qRp0", "qRp1", "qRp2", "qRp3", "qRp4", "qRp5", "qRp6", "qRp7",
              "qCVa", "qCVb", "qCVc", "qCVd", "qCVe",
              "sPE", "sDV", "sAC", "sGP"):
        S_.sem(s)
    qH = ("qH0", "qH1")
    qW = ("qW0", "qW1")
    qB = ("qB0", "qB1", "qB2", "qB3")
    qTH = ("qTH0", "qTH1")
    qC = ("qC0", "qC1")

    def dma(q, out, in_, waits=(), sem="qSm", extra=()):
        return S_.ev(q, (lambda e, o=out, i=in_: e.dma_start(out=o, in_=i)),
                     waits, [(sem, 16)] + list(extra))

    def mm(out, lhsT, rhs, waits=(), start=True, stop=True, tp=(0, 0), skip=False,
           extra=()):
        return S_.ev("tensor",
                     (lambda e, o=out, l=lhsT, r=rhs, s1=start, s2=stop, t=tp, k=skip:
                      e.matmul(o, l, r, start=s1, stop=s2, tile_position=t,
                               skip_group_check=k)),
                     waits, [("sPE", 1)] + list(extra))

    def tr(out, in_, waits=(), extra=()):
        return S_.ev("tensor",
                     (lambda e, o=out, i=in_: e.transpose(o, i, ident_sb[:, :])),
                     waits, [("sPE", 1)] + list(extra))

    def act(out, in_, func, waits=(), scale=1.0, bias=0.0, extra=()):
        return S_.ev("scalar",
                     (lambda e, o=out, i=in_, f=func, sc=scale, b=bias:
                      e.activation(o, i, f, scale=sc, bias=b)),
                     waits, [("sAC", 1)] + list(extra))

    def tt(out, in0, in1, op, waits=(), extra=()):
        return S_.ev("vector",
                     (lambda e, o=out, x=in0, y=in1, p=op: e.tensor_tensor(o, x, y, op=p)),
                     waits, [("sDV", 1)] + list(extra))

    def ts(out, in0, s1, op0, waits=(), extra=()):
        return S_.ev("vector",
                     (lambda e, o=out, i=in0, a=s1, p=op0:
                      e.tensor_scalar(o, i, a, None, op0=p)),
                     waits, [("sDV", 1)] + list(extra))

    def ts2(out, in0, s1, op0, s2, op1, waits=(), extra=()):
        return S_.ev("vector",
                     (lambda e, o=out, i=in0, a=s1, p=op0, b=s2, q=op1:
                      e.tensor_scalar(o, i, a, b, op0=p, op1=q)),
                     waits, [("sDV", 1)] + list(extra))

    def red(out, in_, op, absval=False, waits=(), extra=()):
        return S_.ev("vector",
                     (lambda e, o=out, i=in_, p=op, av=absval:
                      e.tensor_reduce(o, i, axis=AX.X, op=p, apply_absolute_value=av)),
                     waits, [("sDV", 1)] + list(extra))

    def cp(out, in_, waits=(), extra=()):
        return S_.ev("vector", (lambda e, o=out, i=in_: e.tensor_copy(o, i)),
                     waits, [("sDV", 1)] + list(extra))

    def recip(out, in_, waits=(), extra=()):
        return S_.ev("vector", (lambda e, o=out, i=in_: e.reciprocal(o, i)),
                     waits, [("sDV", 1)] + list(extra))

    def mset(ap, v, waits=()):
        return S_.ev("gpsimd", (lambda e, a=ap, x=v: e.memset(a, x)),
                     waits, [("sGP", 1)])

    def wonly(eng, waits):
        S_.ev(eng, None, waits, [])

    V = S_.val

    mset(ones1_sb[:, :], 1.0)
    mset(Vsb[:, :], 1.0)
    mset(eps_sb[:, :], 1e-5)
    mset(s32_sb[:, :], 0.0)
    mset(zb_sb[:, :], 0.0)
    g0 = V("sGP")

    for nm, dst in (("ident", ident_sb), ("bdkv", bdkv_sb),
                    ("bdq", bdq_sb), ("bq_cat", bqc_sb), ("bk_cat", bkc_sb),
                    ("bqr_jm", bqr_sb), ("bkr_jm", bkr_sb)):
        dma("sync", dst[:, :], dr[nm])
    buv = dma("gpsimd", bbuf[:, :], dr["b_uv"].partition_broadcast(128), sem="qBB")
    smalls_done = V("qSm")

    qcv = {}
    qcv["dkv"] = ("qCVa", dma("gpsimd", wdkv_sb,
                     dr["W_dkv"].rearrange("(c p) n -> p c n", p=128),
                     sem="qCVa")["qCVa"])
    qcv["dq"] = ("qCVb", dma("gpsimd", wdq_sb,
                    dr["W_dq"].rearrange("(c p) n -> p c n", p=128),
                    sem="qCVb")["qCVb"])
    qcv["kr"] = ("qCVc", dma("gpsimd",
                    wkr_sb[:, 0:3072].rearrange("p (c k) -> p c k", k=384),
                    dr["Wkr_jm"].rearrange("(c p) k -> p c k", p=128),
                    sem="qCVc")["qCVc"])
    qcv["kcat"] = ("qCVd", dma("gpsimd", wkcat_sb, dr["Wk_cat"], sem="qCVd")["qCVd"])
    qcv["uv"] = ("qCVe", dma("gpsimd", wuv_sb, dr["W_uv"], sem="qCVe")["qCVe"])
    conv_jobs = [
        (dr["Wq_cat"], wq_sb[:, 0:1024], 128, 1024),
        (dr["Wqr_jm"], wqr_sb[:, 0:384], 128, 384),
        (dr["a_l"].rearrange("(c p) n -> p c n", p=128), a_sb[:, 0:128], 128, 128),
        (dr["bpad"], bpad_sb[0:32, 0:1024], 32, 1024),
    ]
    stage_free = [None, None]

    for half in range(2):
        dma("gpsimd",
            wo_p[:, 4096 * half:4096 * (half + 1)].rearrange(
                "p (j k) -> p j k", k=1024),
            dr["W_o"][512 * half:512 * (half + 1), :].rearrange(
                "(j p) k -> p j k", p=128),
            sem="qWsc")

    wsc_l = [0]

    def p0_gen():
        for i, (src_ap, dst, pp, w) in enumerate(conv_jobs):
            par = i % 2
            ld = dma("sync", wstage[par][0:pp, 0:w], src_ap, sem=qW[par],
                     waits=([stage_free[par]] if stage_free[par] else []))
            yield
            r = cp(dst, wstage[par][0:pp, 0:w], waits=[(qW[par], ld[qW[par]])])
            stage_free[par] = ("sDV", r["sDV"])
            yield
        for c in range(8):
            par = c % 2
            ld = dma("sync", wstage[par][:, :], dr["W_ffn"][128 * c:128 * (c + 1), :],
                     sem=qW[par], waits=[stage_free[par]])
            yield
            r = red(gmax_sb[:, c:c + 1], wstage[par][:, :], ALU.max, absval=True,
                    waits=[(qW[par], ld[qW[par]])])
            stage_free[par] = ("sDV", r["sDV"])
            yield
        r = red(gmax_sb[:, 8:9], gmax_sb[:, 0:8], ALU.max,
                waits=[("sDV", r["sDV"])])
        st0 = dma("sync", p0_scr[0:1, 0:128], gmax_sb[:, 8:9],
                  waits=[("sDV", r["sDV"])])
        yield
        ld0 = dma("sync", bcast_sb[0:1, 0:128], p0_scr[0:1, 0:128],
                  waits=[("qSm", st0["qSm"])])
        yield
        r = red(stm[0:1, 0:1], bcast_sb[0:1, 0:128], ALU.max,
                waits=[("qSm", ld0["qSm"])])
        r2 = ts(stv[0:1, 0:1], stm[0:1, 0:1], float(np.float32(1.0 / 7.0)),
                ALU.mult, waits=[("sDV", r["sDV"])])
        r3 = recip(str_[0:1, 0:1], stv[0:1, 0:1], waits=[("sDV", r2["sDV"])])
        r4 = cp(s32_sb[0:1, 0:1], stv[0:1, 0:1],
                waits=[("sDV", r3["sDV"]), ("sGP", g0)])
        r5 = cp(s32_sb[0:1, 1:2], str_[0:1, 0:1], waits=[("sDV", r4["sDV"])])
        st1 = dma("sync", p0_scr[0:1, 128:130], s32_sb[0:1, 0:2],
                  waits=[("sDV", r5["sDV"])])
        yield
        dma("sync", scl_sb[:, :], p0_scr[0:1, 128:129].partition_broadcast(128),
            waits=[("qSm", st1["qSm"])])
        blr = dma("sync", rscl_sb[:, :],
                  p0_scr[0:1, 129:130].partition_broadcast(128))
        qsc_q = blr["qSm"]
        yield
        for c in range(8):
            par = c % 2
            ld = dma("sync", wstage[par][:, :], dr["W_ffn"][128 * c:128 * (c + 1), :],
                     sem=qW[par], waits=[stage_free[par]])
            yield
            t = ts(wstage[par][:, :], wstage[par][:, :], rscl_sb[:, 0:1], ALU.mult,
                   waits=[(qW[par], ld[qW[par]]), ("qSm", qsc_q)])
            u = act(qabs, wstage[par][:, :], AF.Abs, waits=[("sDV", t["sDV"])])
            yield
            r = ts(qacc, qabs, 0.5, ALU.is_ge, waits=[("sAC", u["sAC"])])
            for j in range(1, 7):
                r1 = ts(qcmp, qabs, float(j) + 0.5, ALU.is_ge,
                        waits=[("sDV", r["sDV"])])
                r = tt(qacc, qacc, qcmp, ALU.add, waits=[("sDV", r1["sDV"])])
                if j % 2 == 0:
                    yield
            g1 = ts(qsgn, wstage[par][:, :], 0.0, ALU.is_ge, waits=[("sDV", r["sDV"])])
            stage_free[par] = ("sDV", g1["sDV"])
            g2 = ts2(qsgn, qsgn, 2.0, ALU.mult, -1.0, ALU.add,
                     waits=[("sDV", g1["sDV"])])
            yield
            qq = tt(qacc, qacc, qsgn, ALU.mult, waits=[("sDV", g2["sDV"])])
            wq_ = ts(wstage_out, qacc, scl_sb[:, 0:1], ALU.mult,
                     waits=[("sDV", qq["sDV"]), ("qWsc", wsc_l[0])])
            st = dma("gpsimd", wf_p[:, 1024 * c:1024 * (c + 1)], wstage_out,
                     sem="qWsc", waits=[("sDV", wq_["sDV"])])
            wsc_l[0] = st["qWsc"]
            yield

    p0q = [p0_gen()]

    def p0_step(n=1):
        for _ in range(n):
            if not p0q:
                return
            try:
                next(p0q[0])
            except StopIteration:
                p0q.pop(0)

    hload_gate = [0, 0]
    hload_seen = [0, 0]
    tbl_free = 0
    ckv_free = 0
    rope_done = 0
    kb_cnt = 0
    cw_cnt = 0
    last_knope_evac = 0
    last_pa_pe = 0
    hT_prev_consumed = 0
    for s in range(NSP):
        tau = 512 * s
        tl = dma("sync", cos_sb[:, :], dr["cosT"][:, tau:tau + 512], sem="qT",
                 waits=([("sDV", tbl_free)] if tbl_free else []))
        tl2 = dma("sync", sin_sb[:, :], dr["sinT"][:, tau:tau + 512], sem="qT")
        tbl_val = tl2["qT"]

        ev_done = [0] * 4
        for i in range(4):
            par = i % 2
            w = [("sPE", hload_seen[par])] if hload_seen[par] else []
            if hload_gate[par]:
                w.append(("sDV", hload_gate[par]))
                hload_gate[par] = 0
            ld = dma("sync", h2[:, 1024 * par:1024 * (par + 1)],
                     dr["h"][tau + 128 * i:tau + 128 * (i + 1), :], sem=qH[par],
                     waits=w)
            ldv = ld[qH[par]]
            for c in range(8):
                bk = 2 * par + c // 4
                w2 = []
                if c == 0:
                    w2 = [(qH[par], ldv)]
                    if s == 0 and i == 0:
                        w2.append(("qSm", smalls_done))
                    if i >= 2:
                        w2 += [("sAC", ev_done[2 * par]), ("sAC", ev_done[2 * par + 1])]
                    elif s > 0:
                        w2 += [("sDV", hT_prev_consumed), ("sAC", hT_prev_ac)]
                last = tr(PS[:, 512 * bk + 128 * (c % 4):512 * bk + 128 * (c % 4) + 128],
                          h2[:, 1024 * par + 128 * c:1024 * par + 128 * (c + 1)],
                          waits=w2)
                if c == 3 or c == 7:
                    dst = hT[:, 0:4096].rearrange("p (c t) -> p c t", t=512)[
                        :, 4 * (c // 4):4 * (c // 4) + 4, 128 * i:128 * (i + 1)]
                    e = act(dst, PS[:, 512 * bk:512 * (bk + 1)].rearrange(
                                "p (c t) -> p c t", t=128),
                            AF.Copy, waits=[("sPE", last["sPE"])])
                    ev_done[bk] = e["sAC"]
            hload_seen[par] = last["sPE"]
        hT_ready = V("sAC")
        p0_step(3)

        for c in range(8):
            w = []
            if c == 0:
                w = [("sAC", hT_ready)]
                if s == 0:
                    w.append(qcv["dkv"])
                if ckv_free:
                    w.append(("sDV", ckv_free))
            mm(PS[:, 2048:2560], wdkv_sb[:, 128 * c:128 * (c + 1)],
               hT[:, 512 * c:512 * (c + 1)], waits=w, start=(c == 0), stop=(c == 7))
        pe_ckv = V("sPE")
        for c in range(8):
            mm(PS[:, 2560:3072], wdq_sb[:, 128 * c:128 * (c + 1)],
               hT[:, 512 * c:512 * (c + 1)],
               waits=([qcv["dq"]] if s == 0 and c == 0 else []),
               start=(c == 0), stop=(c == 7))
        pe_cq = V("sPE")
        e1 = act(cq_buf[:, 0:512], PS[:, 2048:2560], AF.Identity, bias=bdkv_sb[:, 0:1],
                 waits=[("sPE", pe_ckv)])
        ckv_sb = cq_buf[:, 0:512]
        e2 = act(cq_buf[:, 512:1024], PS[:, 2560:3072], AF.Identity, bias=bdq_sb[:, 0:1],
                 waits=[("sPE", pe_cq)] + ([("qCw", cw_cnt)] if cw_cnt else []))
        ckv_ready = e2["sAC"]
        cw = dma("sync", cq_scr[:, tau:tau + 512], cq_buf[:, 512:1024],
                 sem="qCw", waits=[("sAC", e2["sAC"])])
        cw_cnt = cw["qCw"]
        p0_step(3)

        for blk, bk in enumerate((6, 7, 0)):
            for c in range(8):
                w = []
                if c == 0:
                    if blk == 0 and rope_done:
                        w = [("sDV", rope_done)]
                    if blk == 0 and s == 0:
                        w.append(qcv["kr"])
                    if blk == 2:
                        w = [("sAC", ev_done[0])]
                mm(PS[:, 512 * bk:512 * (bk + 1)],
                   wkr_sb[:, 384 * c + 128 * blk:384 * c + 128 * (blk + 1)],
                   hT[:, 512 * c:512 * (c + 1)], waits=w,
                   start=(c == 0), stop=(c == 7))
        pe_krot = V("sPE")
        c0c = Psb[:, 2048:2560]
        e4 = act(c0s_sb[:, :], PS[:, 512 * 7:512 * 8], AF.Identity, bias=bkr_sb[:, 1:2],
                 waits=[("sPE", pe_krot - 8)])
        e5 = act(c0c, PS[:, 0:512], AF.Identity, bias=bkr_sb[:, 2:3],
                 waits=[("sPE", pe_krot)] + ([("qKb", kb_cnt)] if kb_cnt else []))
        kb1 = dma("sync", rotk_scr[128:256, tau:tau + 512], c0c,
                  sem="qKb", waits=[("sAC", e5["sAC"])])
        e3b = act(c0b_sb[:, :], PS[:, 512 * 6:512 * 7], AF.Identity, bias=bkr_sb[:, 0:1],
                  waits=[("sAC", e4["sAC"])] + ([("qKb", kb_cnt)] if kb_cnt else []))
        e6 = tt(c0b_sb[:, :], c0b_sb[:, :], cos_sb[:, :], ALU.mult,
                waits=[("sAC", e3b["sAC"]), ("qT", tbl_val)])
        e7 = tt(c0s_sb[:, :], c0s_sb[:, :], sin_sb[:, :], ALU.mult,
                waits=[("sDV", e6["sDV"])])
        e8 = tt(c0b_sb[:, :], c0b_sb[:, :], c0s_sb[:, :], ALU.add,
                waits=[("sDV", e7["sDV"])])
        rope_done = e8["sDV"]
        tbl_free = rope_done
        kb2 = dma("sync", rotk_scr[0:128, tau:tau + 512], c0b_sb[:, :],
                  sem="qKb", waits=[("sDV", e8["sDV"])])
        kb_cnt = kb2["qKb"]
        p0_step(3)

        kn_evac = [0, 0]
        for p in range(NP):
            bk = 1 + p % 2
            w = []
            if p < 2:
                w = [("sAC", ckv_ready), ("sAC", ev_done[bk])]
                if s == 0:
                    w.append(qcv["kcat"])
            else:
                w = [("sAC", kn_evac[p % 2])]
            m = mm(PS[:, 512 * bk:512 * (bk + 1)],
                   wkcat_sb[:, 128 * p:128 * (p + 1)], ckv_sb, waits=w)
            e = act(Ksb[:, S * p + tau:S * p + tau + 512],
                    PS[:, 512 * bk:512 * (bk + 1)],
                    AF.Identity, bias=bkc_sb[:, p:p + 1], waits=[("sPE", m["sPE"])])
            kn_evac[p % 2] = e["sAC"]
        last_knope_evac = max(kn_evac)
        p0_step(3)

        v_evac = [0, 0]
        for i in range(4):
            for half in range(2):
                k = 2 * i + half
                bk = 3 + k % 2
                if k == 0:
                    w = [("sAC", ev_done[3]), ("sAC", ckv_ready), ("qBB", buv["qBB"])]
                    if s == 0:
                        w.append(("sGP", g0))
                        w.append(qcv["uv"])
                elif k == 1:
                    w = [("sAC", e1["sAC"])]
                else:
                    w = [("sDV", v_evac[k % 2])]
                m = mm(PS[:, 512 * bk:512 * (bk + 1)],
                       ckv_sb[:, 128 * i:128 * (i + 1)],
                       wuv_sb[:, 512 * half:512 * (half + 1)], waits=w)
                vdst = Vsb[:, 0:NKC * 1040].rearrange(
                    "p (k h c) -> p k h c", h=16, c=65)[
                    :, 4 * s + i, 8 * half:8 * (half + 1), 0:64]
                e = tt(vdst,
                       PS[:, 512 * bk:512 * (bk + 1)].rearrange(
                           "p (h c) -> p h c", c=64),
                       bbuf[:, 512 * half:512 * (half + 1)].rearrange(
                           "p (h c) -> p h c", c=64),
                       ALU.add, waits=[("sPE", m["sPE"])])
                v_evac[k % 2] = e["sDV"]
        ckv_free = max(v_evac)
        hT_prev_consumed = V("sDV")
        hT_prev_ac = V("sAC")
        last_pa_pe = V("sPE")
        p0_step(3)

    while p0q:
        p0_step(4)
    wsc_cnt = V("qWsc")
    phase0_dv = V("sDV")

    for p in range(NP):
        for r0, hh in ((48, 2 * p), (112, 2 * p + 1)):
            src = rotk_scr.rearrange("(b j s) c -> b j s c", b=2, j=8)[
                :, :, hh:hh + 1, :].rearrange("b j s c -> (b j s) c")
            dma("sync", Ksb[r0:r0 + 16, S * p:S * (p + 1)], src, sem="qR",
                waits=[("qKb", kb_cnt), ("sAC", last_knope_evac)])
    krl_total = V("qR")

    qb_state = dict(cq_free=[last_pa_pe, last_pa_pe], rot_done=rope_done,
                    qb_bank_free=max(rope_done, phase0_dv), nope_evacs=[0] * NP,
                    rot_kb=kb_cnt, rq_w1=0)
    qb_ready = {}

    def qbuild_gen(qt, pair_range, first_half):
        tau = 512 * qt
        par = qt % 2
        if first_half:
            ld = dma("sync", cq_buf[:, 512 * par:512 * (par + 1)],
                     cq_scr[:, tau:tau + 512], sem=qC[par],
                     waits=[("qCw", cw_cnt), ("sPE", qb_state["cq_free"][par])])
            qb_state["cq_ld"] = ld[qC[par]]
            tl = dma("sync", cos_sb[:, :], dr["cosT"][:, tau:tau + 512], sem="qT",
                     waits=[("sDV", qb_state["rot_done"])])
            tl2 = dma("sync", sin_sb[:, :], dr["sinT"][:, tau:tau + 512], sem="qT")
            qb_state["tbl"] = tl2["qT"]
            yield
            rot_evacs = {}
            for blk in (2, 0, 1):
                m = mm(PS[:, 3072:3584], wqr_sb[:, 128 * blk:128 * (blk + 1)],
                       cq_buf[:, 512 * par:512 * (par + 1)],
                       waits=([(qC[par], qb_state["cq_ld"])] if blk == 2 else []) +
                             [("sDV", qb_state["qb_bank_free"])])
                dstl = (c0b_sb, c0s_sb, c0s_sb)[blk]
                ew = [("sPE", m["sPE"])]
                if blk == 2:
                    ew.append(("qKb", qb_state["rq_w1"]))
                    if qb_state["rot_done"]:
                        ew.append(("sDV", qb_state["rot_done"]))
                if blk == 0:
                    ew.append(("qKb", qb_state["rot_kb"]))
                if blk == 1:
                    ew.append(("qKb", qb_state["rq_w1"]))
                e = ts(dstl[:, :], PS[:, 3072:3584], bqr_sb[:, blk:blk + 1], ALU.add,
                       waits=ew)
                rot_evacs[blk] = e["sDV"]
                qb_state["qb_bank_free"] = max(qb_state["qb_bank_free"], e["sDV"])
                if blk == 2:
                    w1 = dma("sync", rotq_scr[128:256, 512 * par:512 * (par + 1)],
                             c0s_sb[:, :], sem="qKb", waits=[("sDV", e["sDV"])])
                    qb_state["rq_w1"] = w1["qKb"]
                yield
            e6 = tt(c0b_sb[:, :], c0b_sb[:, :], cos_sb[:, :], ALU.mult,
                    waits=[("sDV", rot_evacs[0]), ("qT", qb_state["tbl"])])
            yield
            e7 = tt(c0s_sb[:, :], c0s_sb[:, :], sin_sb[:, :], ALU.mult,
                    waits=[("sDV", e6["sDV"])])
            yield
            e8 = tt(c0b_sb[:, :], c0b_sb[:, :], c0s_sb[:, :], ALU.add,
                    waits=[("sDV", e7["sDV"])])
            qb_state["rot_done"] = e8["sDV"]
            w2 = dma("sync", rotq_scr[0:128, 512 * par:512 * (par + 1)],
                     c0b_sb[:, :], sem="qKb", waits=[("sDV", e8["sDV"])])
            qb_state["rq_w"] = w2["qKb"]
            qb_state["rot_kb"] = w2["qKb"]
            yield
        for p in pair_range:
            m = mm(PS[:, 3072:3584], wq_sb[:, 128 * p:128 * (p + 1)],
                   cq_buf[:, 512 * par:512 * (par + 1)],
                   waits=[(qC[par], qb_state["cq_ld"]),
                          ("sDV", qb_state["qb_bank_free"])])
            ew = [("sPE", m["sPE"])]
            if qt > 0:
                ew.append(("sPE", sbd[((qt - 1) * NP + p) * NKC + NKC - 1]))
            e = ts(Qsb[:, 512 * p:512 * (p + 1)], PS[:, 3072:3584],
                   bqc_sb[:, p:p + 1], ALU.add, waits=ew)
            qb_state["qb_bank_free"] = e["sDV"]
            qb_state["nope_evacs"][p] = e["sDV"]
            for r0, hh in ((48, 2 * p), (112, 2 * p + 1)):
                src2 = rotq_scr[:, 512 * par:512 * (par + 1)].rearrange(
                    "(b j s) c -> b j s c", b=2, j=8)[:, :, hh:hh + 1, :].rearrange(
                    "b j s c -> (b j s) c")
                dma("sync", Qsb[r0:r0 + 16, 512 * p:512 * (p + 1)], src2,
                    sem="qRp%d" % p,
                    waits=[("qKb", qb_state["rq_w"]), ("sDV", e["sDV"])])
            qb_ready[(qt, p)] = (V("qRp%d" % p), e["sDV"])
            yield
        if pair_range[-1] == NP - 1:
            qb_state["cq_free"][par] = V("sPE")
        qb_state["done_qR"] = V("qR")
        qb_state["done_dv"] = V("sDV")

    NGT = NQT * NP * NKC
    sbd = [0] * (NGT + 1)
    expd = [0] * (NGT + 1)
    denb = [0] * (NGT + 1)
    dnev = [0] * (NQT * NP + 1)
    oune = [0] * (NQT * NP + 1)
    norm_ = [0] * (NQT * NP + 1)
    bcf_ = [0] * (NQT * NP + 1)
    rdw = [0] * (NQT * NP + 1)

    for _ in qbuild_gen(0, list(range(NP)), True):
        pass

    tail_state = dict(wbuf_free=[0, 0, 0, 0], bbuf_free=0, h2_free=[0, 0],
                      h2_qh=[0, 0], tA_q=0, bank7_free=0, x1T_free=[0, 0],
                      t1_mm_done=0, x1w=0, bo_ld=0, bf_ld=0, xT_evac=0,
                      ln_out=0, lora_done=0, sq_free=0, wpre=None)
    bg_tail = []
    bg_qb = []
    bg_ep = []
    gctr = [0]

    for qt in range(NQT):
        qR_needed = qb_state["done_qR"]
        qb_dv = qb_state["done_dv"]
        base_pair = qt * NP
        def ep_start(PIDX2):
            lastG = PIDX2 * NKC + NKC - 1
            op_ = 512 * (PIDX2 % 2)
            c1 = cp(bcast_sb[0:1, 0:512], PS[64:65, 2048:2560],
                    waits=[("sPE", denb[lastG])] +
                          ([("sDV", bcf_[PIDX2 - 1])] if PIDX2 > 0 else []))
            c2 = cp(bcast_sb[32:33, 0:512], PS[64:65, 2560:3072])
            dnev[PIDX2] = c2["sDV"]
            cp(oun_sb[0:64, op_:op_ + 512], PS[0:64, 2048:2560],
               waits=[("qOD", norm_[PIDX2 - 2])] if PIDX2 > 1 else [])
            e = cp(oun_sb[64:128, op_:op_ + 512], PS[0:64, 2560:3072])
            oune[PIDX2] = e["sDV"]
            return e["sDV"], c2["sDV"]

        def ep_chain_gen(PIDX2, e_dv, c2_dv):
            op_ = 512 * (PIDX2 % 2)
            oup = oun_sb[:, op_:op_ + 512]
            Wg = (PIDX2 + 1) * NKC + 1
            def ph(x):
                return Wg + max(1, (x * NKC) // 32)
            while gctr[0] < ph(2):
                yield "spin"
            wdma0 = dma("gpsimd", rden[PIDX2:PIDX2 + 1, 0:512], bcast_sb[0:1, 0:512],
                        sem="qBC", waits=[("sDV", c2_dv)] +
                        ([("qOD", norm_[PIDX2 - 1])] if PIDX2 > 0 else []))
            wdma = dma("gpsimd", rden[PIDX2:PIDX2 + 1, 512:1024],
                       bcast_sb[32:33, 0:512], sem="qBC")
            rdw[PIDX2] = wdma["qBC"]
            while gctr[0] < ph(4):
                yield "spin"
            bl = dma("gpsimd", bcast_sb[0:64, :],
                     rden[PIDX2:PIDX2 + 1, 0:512].partition_broadcast(64),
                     sem="qBC", waits=[("qBC", wdma["qBC"])])
            bl2 = dma("gpsimd", bcast_sb[64:128, :],
                      rden[PIDX2:PIDX2 + 1, 512:1024].partition_broadcast(64),
                      sem="qBC")
            while gctr[0] < ph(7):
                yield "spin"
            rr = recip(bcast_sb[:, :], bcast_sb[:, :],
                       waits=[("qBC", bl2["qBC"]), ("sDV", e_dv)])
            nn = tt(oup, oup, bcast_sb[:, :], ALU.mult,
                    waits=[("sDV", rr["sDV"])])
            bcf_[PIDX2] = nn["sDV"]
            while gctr[0] < ph(12):
                yield "spin"
            od = dma("scalar", o_dram[PIDX2 * 128:(PIDX2 + 1) * 128, :],
                     oup, sem="qOD", waits=[("sDV", nn["sDV"])])
            norm_[PIDX2] = od["qOD"]

        def emit_S(p, kc):
            G = (qt * NP + p) * NKC + kc
            pi = G % 2
            w = []
            if G >= 2:
                w.append(("sAC", expd[G - 2]))
            if kc == 0:
                qr_, dv_ = qb_ready[(qt, p)]
                w += [("qRp%d" % p, qr_), ("sDV", dv_)]
                if qt == 0:
                    w.append(("qR", krl_total))
            mm(PS[:, 1024 * pi:1024 * pi + 512],
               Ksb[0:64, S * p + 128 * kc:S * p + 128 * (kc + 1)],
               Qsb[0:64, 512 * p:512 * (p + 1)], waits=w, tp=(0, 0))
            r = mm(PS[:, 1024 * pi + 512:1024 * pi + 1024],
                   Ksb[64:128, S * p + 128 * kc:S * p + 128 * (kc + 1)],
                   Qsb[64:128, 512 * p:512 * (p + 1)], tp=(64, 0))
            sbd[G] = r["sPE"]

        def emit_exp(p, kc):
            G = (qt * NP + p) * NKC + kc
            pi = G % 2
            p3 = G % 4
            w = [("sPE", sbd[G])]
            if G >= 4:
                w.append(("sPE", denb[G - 4]))
            r = act(PSB4[p3], PS[:, 1024 * pi:1024 * (pi + 1)],
                    AF.Exp, waits=w, scale=0.125)
            expd[G] = r["sAC"]

        def emit_pv(p, kc):
            G = (qt * NP + p) * NKC + kc
            PIDX = qt * NP + p
            p3 = G % 4
            first = (kc == 0)
            lastk = (kc == NKC - 1)
            w = [("sAC", expd[G])]
            if first and PIDX > 0:
                w += [("sDV", oune[PIDX - 1]), ("sDV", dnev[PIDX - 1])]
            mm(PS[0:65, 2048:2560],
               Vsb[:, 1040 * kc + 65 * (2 * p):1040 * kc + 65 * (2 * p) + 65],
               PSB4[p3][:, 0:512], waits=w,
               start=first, stop=lastk, tp=(0, 0))
            r = mm(PS[0:65, 2560:3072],
                   Vsb[:, 1040 * kc + 65 * (2 * p + 1):1040 * kc + 65 * (2 * p + 1) + 65],
                   PSB4[p3][:, 512:1024],
                   start=first, stop=lastk, tp=(0, 0))
            denb[G] = r["sPE"]

        exp_total = (qt + 1) * NP * NKC
        tau = 512 * qt

        def ln_chain_gen(prev_dv, par):
            sqd = h2[:, 1024 * par:1024 * (par + 1)]
            s1 = red(st1[:, :], tA[:, :], ALU.add, waits=[("sDV", prev_dv)])
            sq = tt(sqd, tA[:, :], tA[:, :], ALU.mult,
                    waits=[("sDV", prev_dv)])
            yield
            s2 = red(st2[:, :], sqd, ALU.add, waits=[("sDV", sq["sDV"])])
            tail_state["sq_free"] = s2["sDV"]
            m1 = ts(stm[:, :], st1[:, :], float(np.float32(1.0 / 1024.0)), ALU.mult,
                    waits=[("sDV", s1["sDV"])])
            yield
            v0 = ts(stv[:, :], st2[:, :], float(np.float32(1.0 / 1024.0)), ALU.mult,
                    waits=[("sDV", s2["sDV"])])
            ms = tt(st1[:, :], stm[:, :], stm[:, :], ALU.mult,
                    waits=[("sDV", m1["sDV"])])
            yield
            vv = tt(stv[:, :], stv[:, :], st1[:, :], ALU.subtract,
                    waits=[("sDV", ms["sDV"]), ("sDV", v0["sDV"])])
            yield
            yield
            l1 = act(st2[:, :], stv[:, :], AF.Ln, bias=eps_sb[:, 0:1],
                     waits=[("sDV", vv["sDV"])])
            l2 = act(str_[:, :], st2[:, :], AF.Exp, scale=-0.5,
                     waits=[("sAC", l1["sAC"])])
            yield
            n1 = ts(tA[:, :], tA[:, :], stm[:, 0:1], ALU.subtract,
                    waits=[("sDV", vv["sDV"]), ("sAC", l2["sAC"])])
            yield
            n2 = ts(tA[:, :], tA[:, :], str_[:, 0:1], ALU.mult,
                    waits=[("sDV", n1["sDV"])])
            tail_state["ln_out"] = n2["sDV"]

        def tail_gen(qt=qt, tau=tau):
            st = tail_state
            BK7 = PS[:, 3584:4096]
            x1w_s = [0, 0, 0, 0]
            bb = dma("gpsimd", bbuf[:, :], dr["b_o"].partition_broadcast(128),
                     sem="qBB", waits=[("sDV", st["bbuf_free"])]
                     if st["bbuf_free"] else [("sDV", ckv_free)])
            st["bo_ld"] = bb["qBB"]
            oldv = {}
            lds = st["wpre"] if st["wpre"] else {}
            st["wpre"] = None
            for cp_ in range(4):
                if cp_ not in lds:
                    wl = [("sPE", st["wbuf_free"][cp_])] \
                        if st["wbuf_free"][cp_] else [("qWsc", wsc_cnt)]
                    lds[cp_] = dma("gpsimd", wbuf[:, 2048 * cp_:2048 * (cp_ + 1)],
                                   wo_p[:, 2048 * cp_:2048 * (cp_ + 1)],
                                   sem=qB[cp_], waits=wl)[qB[cp_]]
                    yield
            t2lds = {}
            for sub in range(4):
                par = sub % 2
                w0 = [("sDV", st["h2_free"][par])] if st["h2_free"][par] else []
                if st["h2_qh"][par]:
                    w0.append((qTH[par], st["h2_qh"][par]))
                if qt == 0 and sub < 2:
                    w0.append(("sPE", hload_seen[par]))
                hld = dma("sync", h2[:, 1024 * par:1024 * (par + 1)],
                          dr["h"][tau + 128 * sub:tau + 128 * (sub + 1), :],
                          sem=qTH[par], waits=w0)
                st["h2_qh"][par] = hld[qTH[par]]
                e1d = 0
                for half in range(2):
                    for c in range(8):
                        cp_ = c // 2
                        if sub == 0 and half == 0:
                            nidx = min(qt * NP + c + 1, NQT * NP - 1)
                            while not norm_[nidx]:
                                yield "spin"
                            w0 = [("qOD", norm_[qt * NP + c])]
                            if c == 0 and st["t1_mm_done"]:
                                w0.append(("sPE", st["t1_mm_done"]))
                            if c > 0:
                                w0.append(("qOL", oldv[c - 1]))
                            ol = dma("sync", osb[:, 512 * c:512 * (c + 1)],
                                     o_dram[(qt * NP + c) * 128:(qt * NP + c + 1) * 128, :],
                                     sem="qOL", waits=w0)
                            oldv[c] = ol["qOL"]
                        w2 = [(qB[cp_], lds[cp_])]
                        if sub == 0 and half == 0:
                            w2.append(("qOL", oldv[c]))
                        if c == 0:
                            w2 += [("sDV", st["bank7_free"])]
                        m = mm(BK7,
                               osb[:, 512 * c + 128 * sub:512 * c + 128 * (sub + 1)],
                               wbuf[:, 2048 * cp_ + 1024 * (c % 2) + 512 * half:
                                    2048 * cp_ + 1024 * (c % 2) + 512 * (half + 1)],
                               waits=w2, start=(c == 0), stop=(c == 7))
                        if c % 2 == 1:
                            if sub == 3 and half == 1:
                                st["wbuf_free"][cp_] = m["sPE"]
                                t2lds[cp_] = dma(
                                    "gpsimd", wbuf[:, 2048 * cp_:2048 * (cp_ + 1)],
                                    wf_p[:, 2048 * cp_:2048 * (cp_ + 1)],
                                    sem=qB[cp_],
                                    waits=[("sPE", m["sPE"])])[qB[cp_]]
                            yield
                    e1w = [("sPE", m["sPE"]), (qTH[par], st["h2_qh"][par])]
                    if st["tA_q"] and half == 0:
                        e1w.append(st["tA_q"])
                    if qt == 0 and sub == 0 and half == 0:
                        e1w.append(("sDV", phase0_dv))
                    e1 = tt(tA[:, 512 * half:512 * (half + 1)], BK7,
                            h2[:, 1024 * par + 512 * half:1024 * par + 512 * (half + 1)],
                            ALU.add, waits=e1w + ([("sDV", e1d)] if e1d else []))
                    e1d = e1["sDV"]
                    st["bank7_free"] = e1["sDV"]
                    yield
                if sub == 3:
                    st["t1_mm_done"] = m["sPE"]
                e2 = tt(tA[:, :], tA[:, :], bbuf[:, :], ALU.add,
                        waits=[("sDV", e1d), ("qBB", st["bo_ld"])])
                st["bbuf_free"] = e2["sDV"]
                yield
                for _ in ln_chain_gen(e2["sDV"], par):
                    yield
                st["h2_free"][par] = st["sq_free"]
                xw = dma("gpsimd", x1_scr[128 * sub:128 * (sub + 1), :], tA[:, :],
                         sem="qX", waits=[("sDV", st["ln_out"])])
                st["tA_q"] = ("qX", xw["qX"])
                x1w_s[sub] = xw["qX"]
                yield
            bb = dma("gpsimd", bbuf[:, :], dr["b_ffn"].partition_broadcast(128),
                     sem="qBB", waits=[("sDV", st["bbuf_free"])])
            st["bf_ld"] = bb["qBB"]

            def issue_xld(sub2):
                par2 = sub2 % 2
                w = [("qX", x1w_s[sub2])]
                if st["h2_free"][par2]:
                    w.append(("sDV", st["h2_free"][par2]))
                if st["h2_qh"][par2]:
                    w.append((qTH[par2], st["h2_qh"][par2]))
                xl = dma("sync", h2[:, 1024 * par2:1024 * (par2 + 1)],
                         x1_scr[128 * sub2:128 * (sub2 + 1), :], sem=qTH[par2],
                         waits=w)
                st["h2_qh"][par2] = xl[qTH[par2]]
                return xl[qTH[par2]]

            xlds = {0: issue_xld(0), 1: issue_xld(1)}
            for sub in range(4):
                par = sub % 2
                xldv = xlds[sub]
                for rnd in range(2):
                    for c4 in range(4):
                        c = 4 * rnd + c4
                        w2 = []
                        if c4 == 0:
                            w2 = [(qTH[par], xldv), ("sDV", st["bank7_free"])]
                            if st["x1T_free"][par]:
                                w2.append(("sPE", st["x1T_free"][par]))
                        last = tr(PS[:, 3584 + 128 * c4:3584 + 128 * (c4 + 1)],
                                  h2[:, 1024 * par + 128 * c:1024 * par + 128 * (c + 1)],
                                  waits=w2)
                    dst = x1T2[:, 0:2048].rearrange("p (g c t) -> p g c t", g=2, t=128)[
                        :, par, 4 * rnd:4 * rnd + 4, :]
                    e = cp(dst, BK7.rearrange("p (c t) -> p c t", t=128),
                           waits=[("sPE", last["sPE"])])
                    st["bank7_free"] = e["sDV"]
                    yield
                st["xT_evac"] = e["sDV"]
                for c in range(8):
                    w2 = [("sDV", st["xT_evac"])] if c == 0 else []
                    m = mm(PS[0:16, 3584:3712], a_sb[:, 16 * c:16 * (c + 1)],
                           x1T2[:, 1024 * par + 128 * c:1024 * par + 128 * (c + 1)],
                           waits=w2, start=(c == 0), stop=(c == 7))
                zw = [("sPE", m["sPE"])]
                if st.get("lora_done"):
                    zw.append(("sPE", st["lora_done"]))
                ze = cp(zb_sb[0:16, 0:128], PS[0:16, 3584:3712], waits=zw)
                st["bank7_free"] = ze["sDV"]
                yield
                e1d = 0
                for half in range(2):
                    for c in range(8):
                        cp_ = c // 2
                        w2 = [(qB[cp_], t2lds[cp_])]
                        if c == 0:
                            w2.append(("sDV", st["bank7_free"]))
                        m = mm(BK7,
                               x1T2[:, 1024 * par + 128 * c:1024 * par + 128 * (c + 1)],
                               wbuf[:, 2048 * cp_ + 1024 * (c % 2) + 512 * half:
                                    2048 * cp_ + 1024 * (c % 2) + 512 * (half + 1)],
                               waits=w2, start=(c == 0), stop=False)
                        if c % 2 == 1:
                            if sub == 3 and half == 1 and qt + 1 < NQT:
                                st["wbuf_free"][cp_] = m["sPE"]
                                if st["wpre"] is None:
                                    st["wpre"] = {}
                                st["wpre"][cp_] = dma(
                                    "gpsimd", wbuf[:, 2048 * cp_:2048 * (cp_ + 1)],
                                    wo_p[:, 2048 * cp_:2048 * (cp_ + 1)],
                                    sem=qB[cp_],
                                    waits=[("sPE", m["sPE"])])[qB[cp_]]
                            yield
                    m = mm(BK7, zb_sb[0:32, 0:128],
                           bpad_sb[0:32, 512 * half:512 * (half + 1)],
                           start=False, stop=True)
                    st["lora_done"] = m["sPE"]
                    e1w = [("sPE", m["sPE"]), ("qBB", st["bf_ld"]),
                           (qTH[par], st["h2_qh"][par])]
                    if st["tA_q"] and half == 0:
                        e1w.append(st["tA_q"])
                    e1 = tt(tA[:, 512 * half:512 * (half + 1)], BK7,
                            h2[:, 1024 * par + 512 * half:1024 * par + 512 * (half + 1)],
                            ALU.add, waits=e1w + ([("sDV", e1d)] if e1d else []))
                    e1d = e1["sDV"]
                    st["bank7_free"] = e1["sDV"]
                    yield
                st["x1T_free"][par] = m["sPE"]
                e2 = tt(tA[:, :], tA[:, :], bbuf[:, :], ALU.add,
                        waits=[("sDV", e1d), ("qBB", st["bf_ld"])])
                st["bbuf_free"] = e2["sDV"]
                yield
                for _ in ln_chain_gen(e2["sDV"], par):
                    yield
                st["h2_free"][par] = st["sq_free"]
                if sub + 2 <= 3:
                    xlds[sub + 2] = issue_xld(sub + 2)
                od = dma("scalar", out_d[tau + 128 * sub:tau + 128 * (sub + 1), :],
                         tA[:, :], sem="qO", waits=[("sDV", st["ln_out"])])
                st["tA_q"] = ("qO", od["qO"])
                yield

        bg_tail.append(tail_gen())
        NGQ = NP * NKC
        for g in range(NGQ):
            p, kc = g // NKC, g % NKC
            gctr[0] = qt * NGQ + g
            S_.gpos = (qt, p, kc)
            emit_S(p, kc)
            emit_exp(p, kc)
            if g >= 2:
                emit_pv((g - 2) // NKC, (g - 2) % NKC)
            if kc == 1 and qt * NP + p > 0:
                P2 = qt * NP + p - 1
                e_dv, c2_dv = ep_start(P2)
                bg_ep.append(ep_chain_gen(P2, e_dv, c2_dv))
            if bg_ep:
                try:
                    next(bg_ep[0])
                except StopIteration:
                    bg_ep.pop(0)
            if kc == NKC // 2 and qt + 1 < NQT and p in (3, 7):
                if p == 3:
                    bg_qb.append(qbuild_gen(qt + 1, [0, 1, 2], True))
                else:
                    bg_qb.append(qbuild_gen(qt + 1, [3, 4, 5, 6], False))
            for q_ in (() if kc in (0, 1, 2) else (bg_qb, bg_tail)):
                if not q_:
                    continue
                for _spin in range(64):
                    try:
                        v = next(q_[0])
                    except StopIteration:
                        q_.pop(0)
                        break
                    if v != "spin":
                        break
        emit_pv(NP - 1, NKC - 2)
        emit_pv(NP - 1, NKC - 1)
        if qt == NQT - 1:
            P2 = qt * NP + NP - 1
            e_dv, c2_dv = ep_start(P2)
            bg_ep.append(ep_chain_gen(P2, e_dv, c2_dv))
        if qt + 1 < NQT:
            bg_qb.append(qbuild_gen(qt + 1, [7], False))
        while bg_qb:
            try:
                next(bg_qb[0])
            except StopIteration:
                bg_qb.pop(0)

        if qt == NQT - 1:
            for p_ in range(NP):
                pass
    gctr[0] = 1 << 60
    for q_ in (bg_ep, bg_qb, bg_tail):
        while q_:
            try:
                next(q_[0])
            except StopIteration:
                q_.pop(0)
    wonly("gpsimd", [("qO", 16 * 4 * NQT)])

    from contextlib import ExitStack
    with ExitStack() as ctx:
        sems = {}
        for s in S_.cnt:
            sems[s] = ctx.enter_context(nc.semaphore(s))
        block = ctx.enter_context(nc.Block())

        def mk(eng_name):
            def emit(eng):
                for fn, waits, incs in S_.events[eng_name]:
                    for s, v in waits:
                        if v > 0:
                            eng.wait_ge(sems[s], v)
                    if fn is None:
                        continue
                    ins = fn(eng)
                    for s, a in incs:
                        ins = ins.then_inc(sems[s], a)
            return emit
        block.sync(mk("sync"))
        block.gpsimd(mk("gpsimd"))
        block.tensor(mk("tensor"))
        block.scalar(mk("scalar"))
        block.vector(mk("vector"))
    return nc


def kernel(**inputs):
    h_all = np.asarray(inputs["L0_attn_h"], np.float32)
    B, S, _ = h_all.shape
    const = _host_constants(inputs, S)
    nc = build_program(S)
    shared = {k: np.ascontiguousarray(v) for k, v in const.items()
              if k not in ("W_dkv", "W_dq", "W_uv", "W_o", "W_ffn", "a_l")}
    shared.update({
        "W_dkv": const["W_dkv"], "W_dq": const["W_dq"], "W_uv": const["W_uv"],
        "W_o": const["W_o"], "W_ffn": const["W_ffn"], "a_l": const["a_l"],
    })
    in_maps = []
    for c in range(B):
        m = dict(shared)
        m["h"] = np.ascontiguousarray(h_all[c])
        in_maps.append(m)
    res = run_bass_kernel_spmd(nc, in_maps, core_ids=list(range(B)))
    out = np.stack([np.asarray(r["out"]) for r in res.results], axis=0)
    return out.astype(np.float32)


# version 46
# speedup vs baseline: 1.0141x; 1.0141x over previous
import numpy as np
import ml_dtypes

import concourse.bass as bass
import concourse.mybir as mybir
from concourse.bass_utils import run_bass_kernel_spmd

F32 = mybir.dt.float32
BF16 = mybir.dt.bfloat16
AF = mybir.ActivationFunctionType
ALU = mybir.AluOpType
AX = mybir.AxisListType
bfdt = ml_dtypes.bfloat16

D = 1024
H = 16
NP = H // 2
RANK = 16

ENGS = ("sync", "gpsimd", "tensor", "scalar", "vector")


class Sched:
    def __init__(self):
        self.events = {e: [] for e in ENGS}
        self.cnt = {}

    def sem(self, name):
        self.cnt.setdefault(name, 0)

    def ev(self, eng, fn, waits=(), incs=()):
        out = {}
        for s, a in incs:
            self.cnt[s] += a
            out[s] = self.cnt[s]
        self.events[eng].append((fn, list(waits), list(incs)))
        return out

    def val(self, s):
        return self.cnt[s]


def _host_constants(inp, S):
    f32 = np.float32
    W_uq = np.asarray(inp["L0_attn_W_uq"], f32)
    W_uk = np.asarray(inp["L0_attn_W_uk"], f32)
    b_uq = np.asarray(inp["L0_attn_b_uq"], f32)
    b_uk = np.asarray(inp["L0_attn_b_uk"], f32)
    W_qr = np.asarray(inp["L0_attn_W_qr"], f32)
    W_kr = np.asarray(inp["L0_attn_W_kr"], f32)
    b_qr = np.asarray(inp["L0_attn_b_qr"], f32)
    b_kr = np.asarray(inp["L0_attn_b_kr"], f32)

    def cat_nope(W, b):
        Wc = np.zeros((W.shape[0], 128 * NP), f32)
        bc = np.zeros((128, NP), f32)
        for p in range(NP):
            Wc[:, 128 * p + 0:128 * p + 48] = W[:, 96 * p:96 * p + 48]
            Wc[:, 128 * p + 64:128 * p + 112] = W[:, 96 * p + 48:96 * p + 96]
            bc[0:48, p] = b[96 * p:96 * p + 48]
            bc[64:112, p] = b[96 * p + 48:96 * p + 96]
        return Wc, bc

    Wq_cat, bq_cat = cat_nope(W_uq, b_uq)
    Wk_cat, bk_cat = cat_nope(W_uk, b_uk)

    def jm(Wr, br):
        nin = Wr.shape[0]
        Wj = np.zeros((nin, 384), f32)
        bj = np.zeros((128, 3), f32)
        for h in range(16):
            for j in range(8):
                Wj[:, 0 * 128 + 16 * j + h] = Wr[:, 16 * h + j]
                bj[16 * j + h, 0] = br[16 * h + j]
                js = j + 4 if j < 4 else j - 4
                Wj[:, 1 * 128 + 16 * j + h] = Wr[:, 16 * h + js]
                bj[16 * j + h, 1] = br[16 * h + js]
                Wj[:, 2 * 128 + 16 * j + h] = Wr[:, 16 * h + 8 + j]
                bj[16 * j + h, 2] = br[16 * h + 8 + j]
        return Wj, bj

    Wqr_jm, bqr_jm = jm(W_qr, b_qr)
    Wkr_jm, bkr_jm = jm(W_kr, b_kr)

    inv_freq = (1.0 / (10000.0 ** (np.arange(0, 8, 2, dtype=f32) / np.float32(8.0)))).astype(f32)
    t = (np.arange(S, dtype=f32) / np.float32(40.0)).astype(f32)
    freqs = (t[:, None] * inv_freq[None, :]).astype(f32)
    cos4 = np.cos(freqs).astype(f32)
    sin4 = np.sin(freqs).astype(f32)
    cosT = np.zeros((128, S), f32)
    sinT = np.zeros((128, S), f32)
    for h in range(16):
        for j in range(8):
            r = 16 * j + h
            cosT[r] = cos4[:, j % 4]
            sinT[r] = (-sin4[:, j % 4]) if j < 4 else sin4[:, j % 4]

    bpad = np.zeros((32, 1024), f32)
    bpad[0:16] = np.asarray(inp["L0_ffn_b"], f32)
    ones32 = np.zeros((32, 128), f32)
    ones32[0] = 1.0

    return dict(
        Wq_cat=Wq_cat, bq_cat=bq_cat, Wk_cat=Wk_cat, bk_cat=bk_cat,
        Wqr_jm=Wqr_jm, bqr_jm=bqr_jm, Wkr_jm=Wkr_jm, bkr_jm=bkr_jm,
        cosT=cosT.astype(bfdt), sinT=sinT.astype(bfdt), bpad=bpad, ones32=ones32,
        ident=np.eye(128, dtype=f32),
        bdkv=np.asarray(inp["L0_attn_b_dkv"], f32).reshape(128, 1),
        bdq=np.asarray(inp["L0_attn_b_dq"], f32).reshape(128, 1),
        b_uv=np.asarray(inp["L0_attn_b_uv"], f32).reshape(1, 1024),
        b_o=np.asarray(inp["L0_attn_b_o"], f32).reshape(1, 1024),
        b_ffn=np.asarray(inp["L0_ffn_bias"], f32).reshape(1, 1024),
        W_dkv=np.asarray(inp["L0_attn_W_dkv"], f32),
        W_dq=np.asarray(inp["L0_attn_W_dq"], f32),
        W_uv=np.asarray(inp["L0_attn_W_uv"], f32),
        W_o=np.asarray(inp["L0_attn_W_o"], f32),
        W_ffn=np.asarray(inp["L0_ffn_kernel"], f32),
        a_l=np.asarray(inp["L0_ffn_a"], f32),
    )


def build_program(S):
    NQT = S // 512
    NKC = S // 128
    NSP = S // 512
    nc = bass.Bass("TRN2", target_bir_lowering=False)

    dr = {}
    def din(name, shape, dt=F32):
        dr[name] = nc.dram_tensor(name, list(shape), dt, kind="ExternalInput").ap()
    din("h", (S, D))
    din("W_dkv", (D, 128)); din("W_dq", (D, 128))
    din("Wq_cat", (128, 1024)); din("Wk_cat", (128, 1024))
    din("Wqr_jm", (128, 384)); din("Wkr_jm", (D, 384))
    din("W_uv", (128, 1024)); din("W_o", (D, D)); din("W_ffn", (D, D))
    din("a_l", (D, RANK)); din("bpad", (32, 1024))
    din("cosT", (128, S), BF16); din("sinT", (128, S), BF16)
    din("ident", (128, 128)); din("ones32", (32, 128))
    din("bdkv", (128, 1)); din("bdq", (128, 1))
    din("bq_cat", (128, NP)); din("bk_cat", (128, NP))
    din("bqr_jm", (128, 3)); din("bkr_jm", (128, 3))
    din("b_uv", (1, 1024)); din("b_o", (1, 1024)); din("b_ffn", (1, 1024))
    out_d = nc.dram_tensor("out", [S, D], F32, kind="ExternalOutput").ap()
    rden = nc.dram_tensor("rden", [NQT * NP, 1024], F32, kind="Internal").ap()
    o_dram = nc.dram_tensor("o_dram", [NQT * NP * 128, 512], BF16, kind="Internal").ap()
    rotk_scr = nc.dram_tensor("rotk_scr", [256, S], BF16, kind="Internal").ap()
    rotq_scr = nc.dram_tensor("rotq_scr", [256, 1024], BF16, kind="Internal").ap()
    cq_scr = nc.dram_tensor("cq_scr", [128, S], BF16, kind="Internal").ap()
    x1_scr = nc.dram_tensor("x1_scr", [512, D], F32, kind="Internal").ap()
    wo_p = nc.dram_tensor("wo_p", [128, 8 * D], BF16, kind="Internal").ap()
    wf_p = nc.dram_tensor("wf_p", [128, 8 * D], BF16, kind="Internal").ap()
    p0_scr = nc.dram_tensor("p0_scr", [1, 192], F32, kind="Internal").ap()

    sbt = nc.alloc_sbuf_tensor
    Ksb = sbt("Ksb", [128, NP * S], BF16)
    Vsb = sbt("Vsb", [128, NKC * 1040], BF16)
    wq_sb = sbt("wq_sb", [128, 1024], BF16)
    wqr_sb = sbt("wqr_sb", [128, 384], BF16)
    wkr_sb = sbt("wkr_sb", [128, 3072], BF16)
    a_sb = sbt("a_sb", [128, 128], BF16)
    bpad_sb = sbt("bpad_sb", [32, 1024], BF16)
    ident_sb = sbt("ident_sb", [128, 128], F32)
    ones1_sb = sbt("ones1_sb", [128, 1], BF16)
    bdkv_sb = sbt("bdkv_sb", [128, 1], F32)
    bdq_sb = sbt("bdq_sb", [128, 1], F32)
    bqc_sb = sbt("bqc_sb", [128, NP], F32)
    bkc_sb = sbt("bkc_sb", [128, NP], F32)
    bqr_sb = sbt("bqr_sb", [128, 3], F32)
    bkr_sb = sbt("bkr_sb", [128, 3], F32)
    eps_sb = sbt("eps_sb", [128, 1], F32)
    scl_sb = sbt("scl_sb", [128, 1], F32)
    rscl_sb = sbt("rscl_sb", [128, 1], F32)
    s32_sb = sbt("s32_sb", [32, 2], F32)
    gmax_sb = sbt("gmax_sb", [128, 9], F32)
    Qsb = sbt("Qsb", [128, NP * 512], BF16)
    _ = None
    Psb = sbt("Psb", [128, 3072], BF16)
    osb = sbt("osb", [128, NP * 512], BF16)
    cos_sb = sbt("cos_sb", [128, 512], BF16)
    sin_sb = sbt("sin_sb", [128, 512], BF16)
    cq_buf = sbt("cq_buf", [128, 1024], BF16)
    bcast_sb = sbt("bcast_sb", [128, 512], F32)
    c0b_sb = sbt("c0b_sb", [128, 512], BF16)
    c0s_sb = sbt("c0s_sb", [128, 512], BF16)
    wbuf = sbt("wbuf", [128, 8192], BF16)
    bbuf = sbt("bbuf", [128, 1024], BF16)
    h2 = sbt("h2", [128, 2048], F32)
    tA = sbt("tA", [128, 1024], F32)
    x1T2 = sbt("x1T2", [128, 2048], BF16)
    zb_sb = sbt("zb_sb", [32, 128], BF16)
    st1 = sbt("st1", [128, 1], F32)
    st2 = sbt("st2", [128, 1], F32)
    stm = sbt("stm", [128, 1], F32)
    stv = sbt("stv", [128, 1], F32)
    str_ = sbt("str_", [128, 1], F32)
    oun_sb = sbt("oun_sb", [128, 1024], BF16)
    assert nc.sbuf_bytes_remaining >= 0

    PS = nc.alloc_psum_tensor("PS", [128, 4096], F32)

    hT = Qsb
    wkcat_sb = osb[:, 0:1024]
    wdkv_sb = osb[:, 1024:2048]
    wdq_sb = osb[:, 2048:3072]
    wuv_sb = osb[:, 3072:4096]
    wbufF = (wbuf.ap() if hasattr(wbuf, "ap") else wbuf).bitcast(F32)
    wstage = (tA, wbufF[:, 0:1024])
    wstage_out = x1T2[:, 0:1024]
    qacc = x1T2[:, 1024:2048]
    qcmp = Psb[:, 0:1024]
    qsgn = Psb[:, 1024:2048]
    PSB4 = [Psb[:, 0:1024], Psb[:, 1024:2048], Psb[:, 2048:3072],
            wkr_sb[:, 0:1024]]
    qabs = wbufF[:, 1024:2048]

    S_ = Sched()
    for s in ("qSm", "qW0", "qW1", "qT", "qC0", "qC1", "qR", "qWsc", "qKb",
              "qCw", "qX", "qO", "qH0", "qH1", "qTH0", "qTH1", "qB0", "qB1", "qB2", "qB3", "qBB", "qBC", "qOD", "qOL",
              "qRp0", "qRp1", "qRp2", "qRp3", "qRp4", "qRp5", "qRp6", "qRp7",
              "qCVa", "qCVb", "qCVc", "qCVd", "qCVe",
              "sPE", "sDV", "sAC", "sGP"):
        S_.sem(s)
    qH = ("qH0", "qH1")
    qW = ("qW0", "qW1")
    qB = ("qB0", "qB1", "qB2", "qB3")
    qTH = ("qTH0", "qTH1")
    qC = ("qC0", "qC1")

    def dma(q, out, in_, waits=(), sem="qSm", extra=()):
        return S_.ev(q, (lambda e, o=out, i=in_: e.dma_start(out=o, in_=i)),
                     waits, [(sem, 16)] + list(extra))

    def mm(out, lhsT, rhs, waits=(), start=True, stop=True, tp=(0, 0), skip=False,
           extra=()):
        return S_.ev("tensor",
                     (lambda e, o=out, l=lhsT, r=rhs, s1=start, s2=stop, t=tp, k=skip:
                      e.matmul(o, l, r, start=s1, stop=s2, tile_position=t,
                               skip_group_check=k)),
                     waits, [("sPE", 1)] + list(extra))

    def tr(out, in_, waits=(), extra=()):
        return S_.ev("tensor",
                     (lambda e, o=out, i=in_: e.transpose(o, i, ident_sb[:, :])),
                     waits, [("sPE", 1)] + list(extra))

    def act(out, in_, func, waits=(), scale=1.0, bias=0.0, extra=()):
        return S_.ev("scalar",
                     (lambda e, o=out, i=in_, f=func, sc=scale, b=bias:
                      e.activation(o, i, f, scale=sc, bias=b)),
                     waits, [("sAC", 1)] + list(extra))

    def tt(out, in0, in1, op, waits=(), extra=()):
        return S_.ev("vector",
                     (lambda e, o=out, x=in0, y=in1, p=op: e.tensor_tensor(o, x, y, op=p)),
                     waits, [("sDV", 1)] + list(extra))

    def ts(out, in0, s1, op0, waits=(), extra=()):
        return S_.ev("vector",
                     (lambda e, o=out, i=in0, a=s1, p=op0:
                      e.tensor_scalar(o, i, a, None, op0=p)),
                     waits, [("sDV", 1)] + list(extra))

    def ts2(out, in0, s1, op0, s2, op1, waits=(), extra=()):
        return S_.ev("vector",
                     (lambda e, o=out, i=in0, a=s1, p=op0, b=s2, q=op1:
                      e.tensor_scalar(o, i, a, b, op0=p, op1=q)),
                     waits, [("sDV", 1)] + list(extra))

    def red(out, in_, op, absval=False, waits=(), extra=()):
        return S_.ev("vector",
                     (lambda e, o=out, i=in_, p=op, av=absval:
                      e.tensor_reduce(o, i, axis=AX.X, op=p, apply_absolute_value=av)),
                     waits, [("sDV", 1)] + list(extra))

    def cp(out, in_, waits=(), extra=()):
        return S_.ev("vector", (lambda e, o=out, i=in_: e.tensor_copy(o, i)),
                     waits, [("sDV", 1)] + list(extra))

    def recip(out, in_, waits=(), extra=()):
        return S_.ev("vector", (lambda e, o=out, i=in_: e.reciprocal(o, i)),
                     waits, [("sDV", 1)] + list(extra))

    def mset(ap, v, waits=()):
        return S_.ev("gpsimd", (lambda e, a=ap, x=v: e.memset(a, x)),
                     waits, [("sGP", 1)])

    def wonly(eng, waits):
        S_.ev(eng, None, waits, [])

    V = S_.val

    mset(ones1_sb[:, :], 1.0)
    mset(Vsb[:, :], 1.0)
    mset(eps_sb[:, :], 1e-5)
    mset(s32_sb[:, :], 0.0)
    mset(zb_sb[:, :], 0.0)
    g0 = V("sGP")

    for nm, dst in (("ident", ident_sb), ("bdkv", bdkv_sb),
                    ("bdq", bdq_sb), ("bq_cat", bqc_sb), ("bk_cat", bkc_sb),
                    ("bqr_jm", bqr_sb), ("bkr_jm", bkr_sb)):
        dma("sync", dst[:, :], dr[nm])
    buv = dma("gpsimd", bbuf[:, :], dr["b_uv"].partition_broadcast(128), sem="qBB")
    smalls_done = V("qSm")

    qcv = {}
    qcv["dkv"] = ("qCVa", dma("gpsimd", wdkv_sb,
                     dr["W_dkv"].rearrange("(c p) n -> p c n", p=128),
                     sem="qCVa")["qCVa"])
    qcv["dq"] = ("qCVb", dma("gpsimd", wdq_sb,
                    dr["W_dq"].rearrange("(c p) n -> p c n", p=128),
                    sem="qCVb")["qCVb"])
    qcv["kr"] = ("qCVc", dma("gpsimd",
                    wkr_sb[:, 0:3072].rearrange("p (c k) -> p c k", k=384),
                    dr["Wkr_jm"].rearrange("(c p) k -> p c k", p=128),
                    sem="qCVc")["qCVc"])
    qcv["kcat"] = ("qCVd", dma("gpsimd", wkcat_sb, dr["Wk_cat"], sem="qCVd")["qCVd"])
    qcv["uv"] = ("qCVe", dma("gpsimd", wuv_sb, dr["W_uv"], sem="qCVe")["qCVe"])
    conv_jobs = [
        (dr["Wq_cat"], wq_sb[:, 0:1024], 128, 1024),
        (dr["Wqr_jm"], wqr_sb[:, 0:384], 128, 384),
        (dr["a_l"].rearrange("(c p) n -> p c n", p=128), a_sb[:, 0:128], 128, 128),
        (dr["bpad"], bpad_sb[0:32, 0:1024], 32, 1024),
    ]
    stage_free = [None, None]

    for half in range(2):
        dma("gpsimd",
            wo_p[:, 4096 * half:4096 * (half + 1)].rearrange(
                "p (j k) -> p j k", k=1024),
            dr["W_o"][512 * half:512 * (half + 1), :].rearrange(
                "(j p) k -> p j k", p=128),
            sem="qWsc")

    wsc_l = [0]

    def p0_gen():
        for i, (src_ap, dst, pp, w) in enumerate(conv_jobs):
            par = i % 2
            ld = dma("sync", wstage[par][0:pp, 0:w], src_ap, sem=qW[par],
                     waits=([stage_free[par]] if stage_free[par] else []))
            yield
            r = cp(dst, wstage[par][0:pp, 0:w], waits=[(qW[par], ld[qW[par]])])
            stage_free[par] = ("sDV", r["sDV"])
            yield
        for c in range(8):
            par = c % 2
            ld = dma("sync", wstage[par][:, :], dr["W_ffn"][128 * c:128 * (c + 1), :],
                     sem=qW[par], waits=[stage_free[par]])
            yield
            r = red(gmax_sb[:, c:c + 1], wstage[par][:, :], ALU.max, absval=True,
                    waits=[(qW[par], ld[qW[par]])])
            stage_free[par] = ("sDV", r["sDV"])
            yield
        r = red(gmax_sb[:, 8:9], gmax_sb[:, 0:8], ALU.max,
                waits=[("sDV", r["sDV"])])
        st0 = dma("sync", p0_scr[0:1, 0:128], gmax_sb[:, 8:9],
                  waits=[("sDV", r["sDV"])])
        yield
        ld0 = dma("sync", bcast_sb[0:1, 0:128], p0_scr[0:1, 0:128],
                  waits=[("qSm", st0["qSm"])])
        yield
        r = red(stm[0:1, 0:1], bcast_sb[0:1, 0:128], ALU.max,
                waits=[("qSm", ld0["qSm"])])
        r2 = ts(stv[0:1, 0:1], stm[0:1, 0:1], float(np.float32(1.0 / 7.0)),
                ALU.mult, waits=[("sDV", r["sDV"])])
        r3 = recip(str_[0:1, 0:1], stv[0:1, 0:1], waits=[("sDV", r2["sDV"])])
        r4 = cp(s32_sb[0:1, 0:1], stv[0:1, 0:1],
                waits=[("sDV", r3["sDV"]), ("sGP", g0)])
        r5 = cp(s32_sb[0:1, 1:2], str_[0:1, 0:1], waits=[("sDV", r4["sDV"])])
        st1 = dma("sync", p0_scr[0:1, 128:130], s32_sb[0:1, 0:2],
                  waits=[("sDV", r5["sDV"])])
        yield
        dma("sync", scl_sb[:, :], p0_scr[0:1, 128:129].partition_broadcast(128),
            waits=[("qSm", st1["qSm"])])
        blr = dma("sync", rscl_sb[:, :],
                  p0_scr[0:1, 129:130].partition_broadcast(128))
        qsc_q = blr["qSm"]
        yield
        for c in range(8):
            par = c % 2
            ld = dma("sync", wstage[par][:, :], dr["W_ffn"][128 * c:128 * (c + 1), :],
                     sem=qW[par], waits=[stage_free[par]])
            yield
            t = ts(wstage[par][:, :], wstage[par][:, :], rscl_sb[:, 0:1], ALU.mult,
                   waits=[(qW[par], ld[qW[par]]), ("qSm", qsc_q)])
            u = act(qabs, wstage[par][:, :], AF.Abs, waits=[("sDV", t["sDV"])])
            yield
            r = ts(qacc, qabs, 0.5, ALU.is_ge, waits=[("sAC", u["sAC"])])
            for j in range(1, 7):
                r1 = ts(qcmp, qabs, float(j) + 0.5, ALU.is_ge,
                        waits=[("sDV", r["sDV"])])
                r = tt(qacc, qacc, qcmp, ALU.add, waits=[("sDV", r1["sDV"])])
                if j % 2 == 0:
                    yield
            g1 = ts(qsgn, wstage[par][:, :], 0.0, ALU.is_ge, waits=[("sDV", r["sDV"])])
            stage_free[par] = ("sDV", g1["sDV"])
            g2 = ts2(qsgn, qsgn, 2.0, ALU.mult, -1.0, ALU.add,
                     waits=[("sDV", g1["sDV"])])
            yield
            qq = tt(qacc, qacc, qsgn, ALU.mult, waits=[("sDV", g2["sDV"])])
            wq_ = ts(wstage_out, qacc, scl_sb[:, 0:1], ALU.mult,
                     waits=[("sDV", qq["sDV"]), ("qWsc", wsc_l[0])])
            st = dma("gpsimd", wf_p[:, 1024 * c:1024 * (c + 1)], wstage_out,
                     sem="qWsc", waits=[("sDV", wq_["sDV"])])
            wsc_l[0] = st["qWsc"]
            yield

    p0q = [p0_gen()]

    def p0_step(n=1):
        for _ in range(n):
            if not p0q:
                return
            try:
                next(p0q[0])
            except StopIteration:
                p0q.pop(0)

    hload_gate = [0, 0]
    hload_seen = [0, 0]
    tbl_free = 0
    ckv_free = 0
    rope_done = 0
    kb_cnt = 0
    cw_cnt = 0
    last_knope_evac = 0
    last_pa_pe = 0
    hT_prev_consumed = 0
    for s in range(NSP):
        tau = 512 * s
        tl = dma("sync", cos_sb[:, :], dr["cosT"][:, tau:tau + 512], sem="qT",
                 waits=([("sDV", tbl_free)] if tbl_free else []))
        tl2 = dma("sync", sin_sb[:, :], dr["sinT"][:, tau:tau + 512], sem="qT")
        tbl_val = tl2["qT"]

        ev_done = [0] * 4
        for i in range(4):
            par = i % 2
            w = [("sPE", hload_seen[par])] if hload_seen[par] else []
            if hload_gate[par]:
                w.append(("sDV", hload_gate[par]))
                hload_gate[par] = 0
            ld = dma("sync", h2[:, 1024 * par:1024 * (par + 1)],
                     dr["h"][tau + 128 * i:tau + 128 * (i + 1), :], sem=qH[par],
                     waits=w)
            ldv = ld[qH[par]]
            for c in range(8):
                bk = 2 * par + c // 4
                w2 = []
                if c == 0:
                    w2 = [(qH[par], ldv)]
                    if s == 0 and i == 0:
                        w2.append(("qSm", smalls_done))
                    if i >= 2:
                        w2 += [("sAC", ev_done[2 * par]), ("sAC", ev_done[2 * par + 1])]
                    elif s > 0:
                        w2 += [("sDV", hT_prev_consumed), ("sAC", hT_prev_ac)]
                last = tr(PS[:, 512 * bk + 128 * (c % 4):512 * bk + 128 * (c % 4) + 128],
                          h2[:, 1024 * par + 128 * c:1024 * par + 128 * (c + 1)],
                          waits=w2)
                if c == 3 or c == 7:
                    dst = hT[:, 0:4096].rearrange("p (c t) -> p c t", t=512)[
                        :, 4 * (c // 4):4 * (c // 4) + 4, 128 * i:128 * (i + 1)]
                    e = act(dst, PS[:, 512 * bk:512 * (bk + 1)].rearrange(
                                "p (c t) -> p c t", t=128),
                            AF.Copy, waits=[("sPE", last["sPE"])])
                    ev_done[bk] = e["sAC"]
            hload_seen[par] = last["sPE"]
        hT_ready = V("sAC")
        p0_step(3)

        for c in range(8):
            w = []
            if c == 0:
                w = [("sAC", hT_ready)]
                if s == 0:
                    w.append(qcv["dkv"])
                if ckv_free:
                    w.append(("sDV", ckv_free))
            mm(PS[:, 2048:2560], wdkv_sb[:, 128 * c:128 * (c + 1)],
               hT[:, 512 * c:512 * (c + 1)], waits=w, start=(c == 0), stop=(c == 7))
        pe_ckv = V("sPE")
        for c in range(8):
            mm(PS[:, 2560:3072], wdq_sb[:, 128 * c:128 * (c + 1)],
               hT[:, 512 * c:512 * (c + 1)],
               waits=([qcv["dq"]] if s == 0 and c == 0 else []),
               start=(c == 0), stop=(c == 7))
        pe_cq = V("sPE")
        e1 = act(cq_buf[:, 0:512], PS[:, 2048:2560], AF.Identity, bias=bdkv_sb[:, 0:1],
                 waits=[("sPE", pe_ckv)])
        ckv_sb = cq_buf[:, 0:512]
        e2 = act(cq_buf[:, 512:1024], PS[:, 2560:3072], AF.Identity, bias=bdq_sb[:, 0:1],
                 waits=[("sPE", pe_cq)] + ([("qCw", cw_cnt)] if cw_cnt else []))
        ckv_ready = e2["sAC"]
        cw = dma("sync", cq_scr[:, tau:tau + 512], cq_buf[:, 512:1024],
                 sem="qCw", waits=[("sAC", e2["sAC"])])
        cw_cnt = cw["qCw"]
        p0_step(3)

        for blk, bk in enumerate((6, 7, 0)):
            for c in range(8):
                w = []
                if c == 0:
                    if blk == 0 and rope_done:
                        w = [("sDV", rope_done)]
                    if blk == 0 and s == 0:
                        w.append(qcv["kr"])
                    if blk == 2:
                        w = [("sAC", ev_done[0])]
                mm(PS[:, 512 * bk:512 * (bk + 1)],
                   wkr_sb[:, 384 * c + 128 * blk:384 * c + 128 * (blk + 1)],
                   hT[:, 512 * c:512 * (c + 1)], waits=w,
                   start=(c == 0), stop=(c == 7))
        pe_krot = V("sPE")
        c0c = Psb[:, 2048:2560]
        e4 = act(c0s_sb[:, :], PS[:, 512 * 7:512 * 8], AF.Identity, bias=bkr_sb[:, 1:2],
                 waits=[("sPE", pe_krot - 8)])
        e5 = act(c0c, PS[:, 0:512], AF.Identity, bias=bkr_sb[:, 2:3],
                 waits=[("sPE", pe_krot)] + ([("qKb", kb_cnt)] if kb_cnt else []))
        kb1 = dma("sync", rotk_scr[128:256, tau:tau + 512], c0c,
                  sem="qKb", waits=[("sAC", e5["sAC"])])
        e3b = act(c0b_sb[:, :], PS[:, 512 * 6:512 * 7], AF.Identity, bias=bkr_sb[:, 0:1],
                  waits=[("sAC", e4["sAC"])] + ([("qKb", kb_cnt)] if kb_cnt else []))
        e6 = tt(c0b_sb[:, :], c0b_sb[:, :], cos_sb[:, :], ALU.mult,
                waits=[("sAC", e3b["sAC"]), ("qT", tbl_val)])
        e7 = tt(c0s_sb[:, :], c0s_sb[:, :], sin_sb[:, :], ALU.mult,
                waits=[("sDV", e6["sDV"])])
        e8 = tt(c0b_sb[:, :], c0b_sb[:, :], c0s_sb[:, :], ALU.add,
                waits=[("sDV", e7["sDV"])])
        rope_done = e8["sDV"]
        tbl_free = rope_done
        kb2 = dma("sync", rotk_scr[0:128, tau:tau + 512], c0b_sb[:, :],
                  sem="qKb", waits=[("sDV", e8["sDV"])])
        kb_cnt = kb2["qKb"]
        p0_step(3)

        kn_evac = [0, 0]
        for p in range(NP):
            bk = 1 + p % 2
            w = []
            if p < 2:
                w = [("sAC", ckv_ready), ("sAC", ev_done[bk])]
                if s == 0:
                    w.append(qcv["kcat"])
            else:
                w = [("sAC", kn_evac[p % 2])]
            m = mm(PS[:, 512 * bk:512 * (bk + 1)],
                   wkcat_sb[:, 128 * p:128 * (p + 1)], ckv_sb, waits=w)
            e = act(Ksb[:, S * p + tau:S * p + tau + 512],
                    PS[:, 512 * bk:512 * (bk + 1)],
                    AF.Identity, bias=bkc_sb[:, p:p + 1], waits=[("sPE", m["sPE"])])
            kn_evac[p % 2] = e["sAC"]
        last_knope_evac = max(kn_evac)
        p0_step(3)

        v_evac = [0, 0]
        for i in range(4):
            for half in range(2):
                k = 2 * i + half
                bk = 3 + k % 2
                if k == 0:
                    w = [("sAC", ev_done[3]), ("sAC", ckv_ready), ("qBB", buv["qBB"])]
                    if s == 0:
                        w.append(("sGP", g0))
                        w.append(qcv["uv"])
                elif k == 1:
                    w = [("sAC", e1["sAC"])]
                else:
                    w = [("sDV", v_evac[k % 2])]
                m = mm(PS[:, 512 * bk:512 * (bk + 1)],
                       ckv_sb[:, 128 * i:128 * (i + 1)],
                       wuv_sb[:, 512 * half:512 * (half + 1)], waits=w)
                vdst = Vsb[:, 0:NKC * 1040].rearrange(
                    "p (k h c) -> p k h c", h=16, c=65)[
                    :, 4 * s + i, 8 * half:8 * (half + 1), 0:64]
                e = tt(vdst,
                       PS[:, 512 * bk:512 * (bk + 1)].rearrange(
                           "p (h c) -> p h c", c=64),
                       bbuf[:, 512 * half:512 * (half + 1)].rearrange(
                           "p (h c) -> p h c", c=64),
                       ALU.add, waits=[("sPE", m["sPE"])])
                v_evac[k % 2] = e["sDV"]
        ckv_free = max(v_evac)
        hT_prev_consumed = V("sDV")
        hT_prev_ac = V("sAC")
        last_pa_pe = V("sPE")
        p0_step(3)

    while p0q:
        p0_step(4)
    wsc_cnt = V("qWsc")
    phase0_dv = V("sDV")

    for p in range(NP):
        for r0, hh in ((48, 2 * p), (112, 2 * p + 1)):
            src = rotk_scr.rearrange("(b j s) c -> b j s c", b=2, j=8)[
                :, :, hh:hh + 1, :].rearrange("b j s c -> (b j s) c")
            dma("sync", Ksb[r0:r0 + 16, S * p:S * (p + 1)], src, sem="qR",
                waits=[("qKb", kb_cnt), ("sAC", last_knope_evac)])
    krl_total = V("qR")

    qb_state = dict(cq_free=[last_pa_pe, last_pa_pe], rot_done=rope_done,
                    qb_bank_free=max(rope_done, phase0_dv), nope_evacs=[0] * NP,
                    rot_kb=kb_cnt, rq_w1=0)
    qb_ready = {}

    def qbuild_gen(qt, pair_range, first_half):
        tau = 512 * qt
        par = qt % 2
        if first_half:
            ld = dma("sync", cq_buf[:, 512 * par:512 * (par + 1)],
                     cq_scr[:, tau:tau + 512], sem=qC[par],
                     waits=[("qCw", cw_cnt), ("sPE", qb_state["cq_free"][par])])
            qb_state["cq_ld"] = ld[qC[par]]
            tl = dma("sync", cos_sb[:, :], dr["cosT"][:, tau:tau + 512], sem="qT",
                     waits=[("sDV", qb_state["rot_done"])])
            tl2 = dma("sync", sin_sb[:, :], dr["sinT"][:, tau:tau + 512], sem="qT")
            qb_state["tbl"] = tl2["qT"]
            yield
            rot_evacs = {}
            for blk in (2, 0, 1):
                m = mm(PS[:, 3072:3584], wqr_sb[:, 128 * blk:128 * (blk + 1)],
                       cq_buf[:, 512 * par:512 * (par + 1)],
                       waits=([(qC[par], qb_state["cq_ld"])] if blk == 2 else []) +
                             [("sDV", qb_state["qb_bank_free"])])
                dstl = (c0b_sb, c0s_sb, c0s_sb)[blk]
                ew = [("sPE", m["sPE"])]
                if blk == 2:
                    ew.append(("qKb", qb_state["rq_w1"]))
                    if qb_state["rot_done"]:
                        ew.append(("sDV", qb_state["rot_done"]))
                if blk == 0:
                    ew.append(("qKb", qb_state["rot_kb"]))
                if blk == 1:
                    ew.append(("qKb", qb_state["rq_w1"]))
                e = ts(dstl[:, :], PS[:, 3072:3584], bqr_sb[:, blk:blk + 1], ALU.add,
                       waits=ew)
                rot_evacs[blk] = e["sDV"]
                qb_state["qb_bank_free"] = max(qb_state["qb_bank_free"], e["sDV"])
                if blk == 2:
                    w1 = dma("sync", rotq_scr[128:256, 512 * par:512 * (par + 1)],
                             c0s_sb[:, :], sem="qKb", waits=[("sDV", e["sDV"])])
                    qb_state["rq_w1"] = w1["qKb"]
                yield
            e6 = tt(c0b_sb[:, :], c0b_sb[:, :], cos_sb[:, :], ALU.mult,
                    waits=[("sDV", rot_evacs[0]), ("qT", qb_state["tbl"])])
            yield
            e7 = tt(c0s_sb[:, :], c0s_sb[:, :], sin_sb[:, :], ALU.mult,
                    waits=[("sDV", e6["sDV"])])
            yield
            e8 = tt(c0b_sb[:, :], c0b_sb[:, :], c0s_sb[:, :], ALU.add,
                    waits=[("sDV", e7["sDV"])])
            qb_state["rot_done"] = e8["sDV"]
            w2 = dma("sync", rotq_scr[0:128, 512 * par:512 * (par + 1)],
                     c0b_sb[:, :], sem="qKb", waits=[("sDV", e8["sDV"])])
            qb_state["rq_w"] = w2["qKb"]
            qb_state["rot_kb"] = w2["qKb"]
            yield
        for p in pair_range:
            m = mm(PS[:, 3072:3584], wq_sb[:, 128 * p:128 * (p + 1)],
                   cq_buf[:, 512 * par:512 * (par + 1)],
                   waits=[(qC[par], qb_state["cq_ld"]),
                          ("sDV", qb_state["qb_bank_free"])])
            ew = [("sPE", m["sPE"])]
            if qt > 0:
                ew.append(("sPE", sbd[((qt - 1) * NP + p) * NKC + NKC - 1]))
            e = ts(Qsb[:, 512 * p:512 * (p + 1)], PS[:, 3072:3584],
                   bqc_sb[:, p:p + 1], ALU.add, waits=ew)
            qb_state["qb_bank_free"] = e["sDV"]
            qb_state["nope_evacs"][p] = e["sDV"]
            for r0, hh in ((48, 2 * p), (112, 2 * p + 1)):
                src2 = rotq_scr[:, 512 * par:512 * (par + 1)].rearrange(
                    "(b j s) c -> b j s c", b=2, j=8)[:, :, hh:hh + 1, :].rearrange(
                    "b j s c -> (b j s) c")
                dma("sync", Qsb[r0:r0 + 16, 512 * p:512 * (p + 1)], src2,
                    sem="qRp%d" % p,
                    waits=[("qKb", qb_state["rq_w"]), ("sDV", e["sDV"])])
            qb_ready[(qt, p)] = (V("qRp%d" % p), e["sDV"])
            yield
        if pair_range[-1] == NP - 1:
            qb_state["cq_free"][par] = V("sPE")
        qb_state["done_qR"] = V("qR")
        qb_state["done_dv"] = V("sDV")

    NGT = NQT * NP * NKC
    sbd = [0] * (NGT + 1)
    expd = [0] * (NGT + 1)
    denb = [0] * (NGT + 1)
    dnev = [0] * (NQT * NP + 1)
    oune = [0] * (NQT * NP + 1)
    norm_ = [0] * (NQT * NP + 1)
    bcf_ = [0] * (NQT * NP + 1)
    rdw = [0] * (NQT * NP + 1)

    for _ in qbuild_gen(0, list(range(NP)), True):
        pass

    tail_state = dict(wbuf_free=[0, 0, 0, 0], bbuf_free=0, h2_free=[0, 0],
                      h2_qh=[0, 0], tA_q=0, bank7_free=0, x1T_free=[0, 0],
                      t1_mm_done=0, x1w=0, bo_ld=0, bf_ld=0, xT_evac=0,
                      ln_out=0, lora_done=0, sq_free=0, wpre=None)
    bg_tail = []
    bg_qb = []
    bg_ep = []
    gctr = [0]

    for qt in range(NQT):
        qR_needed = qb_state["done_qR"]
        qb_dv = qb_state["done_dv"]
        base_pair = qt * NP
        def ep_start(PIDX2):
            lastG = PIDX2 * NKC + NKC - 1
            op_ = 512 * (PIDX2 % 2)
            cp(oun_sb[0:64, op_:op_ + 512], PS[0:64, 2048:2560],
               waits=[("sPE", denb[lastG])] +
                     ([("qOD", norm_[PIDX2 - 2])] if PIDX2 > 1 else []))
            e = cp(oun_sb[64:128, op_:op_ + 512], PS[0:64, 2560:3072])
            oune[PIDX2] = e["sDV"]
            c1 = act(bcast_sb[0:1, 0:512], PS[64:65, 2048:2560], AF.Copy,
                     waits=[("sPE", denb[lastG])] +
                           ([("sDV", bcf_[PIDX2 - 1])] if PIDX2 > 0 else []))
            c2 = act(bcast_sb[32:33, 0:512], PS[64:65, 2560:3072], AF.Copy)
            dnev[PIDX2] = c2["sAC"]
            return e["sDV"], c2["sAC"]

        def ep_chain_gen(PIDX2, e_dv, c2_dv):
            op_ = 512 * (PIDX2 % 2)
            oup = oun_sb[:, op_:op_ + 512]
            Wg = (PIDX2 + 1) * NKC + 1
            def ph(x):
                return Wg + max(1, (x * NKC) // 32)
            while gctr[0] < ph(2):
                yield "spin"
            wdma0 = dma("gpsimd", rden[PIDX2:PIDX2 + 1, 0:512], bcast_sb[0:1, 0:512],
                        sem="qBC", waits=[("sAC", c2_dv)] +
                        ([("qOD", norm_[PIDX2 - 1])] if PIDX2 > 0 else []))
            wdma = dma("gpsimd", rden[PIDX2:PIDX2 + 1, 512:1024],
                       bcast_sb[32:33, 0:512], sem="qBC")
            rdw[PIDX2] = wdma["qBC"]
            while gctr[0] < ph(4):
                yield "spin"
            bl = dma("gpsimd", bcast_sb[0:64, :],
                     rden[PIDX2:PIDX2 + 1, 0:512].partition_broadcast(64),
                     sem="qBC", waits=[("qBC", wdma["qBC"])])
            bl2 = dma("gpsimd", bcast_sb[64:128, :],
                      rden[PIDX2:PIDX2 + 1, 512:1024].partition_broadcast(64),
                      sem="qBC")
            while gctr[0] < ph(7):
                yield "spin"
            rr = recip(bcast_sb[:, :], bcast_sb[:, :],
                       waits=[("qBC", bl2["qBC"]), ("sDV", e_dv)])
            nn = tt(oup, oup, bcast_sb[:, :], ALU.mult,
                    waits=[("sDV", rr["sDV"])])
            bcf_[PIDX2] = nn["sDV"]
            while gctr[0] < ph(12):
                yield "spin"
            od = dma("scalar", o_dram[PIDX2 * 128:(PIDX2 + 1) * 128, :],
                     oup, sem="qOD", waits=[("sDV", nn["sDV"])])
            norm_[PIDX2] = od["qOD"]

        def emit_S(p, kc):
            G = (qt * NP + p) * NKC + kc
            pi = G % 2
            w = []
            if G >= 2:
                w.append(("sAC", expd[G - 2]))
            if kc == 0:
                qr_, dv_ = qb_ready[(qt, p)]
                w += [("qRp%d" % p, qr_), ("sDV", dv_)]
                if qt == 0:
                    w.append(("qR", krl_total))
            mm(PS[:, 1024 * pi:1024 * pi + 512],
               Ksb[0:64, S * p + 128 * kc:S * p + 128 * (kc + 1)],
               Qsb[0:64, 512 * p:512 * (p + 1)], waits=w, tp=(0, 0))
            r = mm(PS[:, 1024 * pi + 512:1024 * pi + 1024],
                   Ksb[64:128, S * p + 128 * kc:S * p + 128 * (kc + 1)],
                   Qsb[64:128, 512 * p:512 * (p + 1)], tp=(64, 0))
            sbd[G] = r["sPE"]

        def emit_exp(p, kc):
            G = (qt * NP + p) * NKC + kc
            pi = G % 2
            p3 = G % 4
            w = [("sPE", sbd[G])]
            if G >= 4:
                w.append(("sPE", denb[G - 4]))
            r = act(PSB4[p3], PS[:, 1024 * pi:1024 * (pi + 1)],
                    AF.Exp, waits=w, scale=0.125)
            expd[G] = r["sAC"]

        def emit_pv(p, kc):
            G = (qt * NP + p) * NKC + kc
            PIDX = qt * NP + p
            p3 = G % 4
            first = (kc == 0)
            lastk = (kc == NKC - 1)
            w = [("sAC", expd[G])]
            if first and PIDX > 0:
                w += [("sDV", oune[PIDX - 1]), ("sAC", dnev[PIDX - 1])]
            mm(PS[0:65, 2048:2560],
               Vsb[:, 1040 * kc + 65 * (2 * p):1040 * kc + 65 * (2 * p) + 65],
               PSB4[p3][:, 0:512], waits=w,
               start=first, stop=lastk, tp=(0, 0))
            r = mm(PS[0:65, 2560:3072],
                   Vsb[:, 1040 * kc + 65 * (2 * p + 1):1040 * kc + 65 * (2 * p + 1) + 65],
                   PSB4[p3][:, 512:1024],
                   start=first, stop=lastk, tp=(0, 0))
            denb[G] = r["sPE"]

        exp_total = (qt + 1) * NP * NKC
        tau = 512 * qt

        def ln_chain_gen(prev_dv, par):
            sqd = h2[:, 1024 * par:1024 * (par + 1)]
            s1 = red(st1[:, :], tA[:, :], ALU.add, waits=[("sDV", prev_dv)])
            sq = tt(sqd, tA[:, :], tA[:, :], ALU.mult,
                    waits=[("sDV", prev_dv)])
            yield
            s2 = red(st2[:, :], sqd, ALU.add, waits=[("sDV", sq["sDV"])])
            tail_state["sq_free"] = s2["sDV"]
            m1 = ts(stm[:, :], st1[:, :], float(np.float32(1.0 / 1024.0)), ALU.mult,
                    waits=[("sDV", s1["sDV"])])
            yield
            v0 = ts(stv[:, :], st2[:, :], float(np.float32(1.0 / 1024.0)), ALU.mult,
                    waits=[("sDV", s2["sDV"])])
            ms = tt(st1[:, :], stm[:, :], stm[:, :], ALU.mult,
                    waits=[("sDV", m1["sDV"])])
            yield
            vv = tt(stv[:, :], stv[:, :], st1[:, :], ALU.subtract,
                    waits=[("sDV", ms["sDV"]), ("sDV", v0["sDV"])])
            yield
            yield
            l1 = act(st2[:, :], stv[:, :], AF.Ln, bias=eps_sb[:, 0:1],
                     waits=[("sDV", vv["sDV"])])
            l2 = act(str_[:, :], st2[:, :], AF.Exp, scale=-0.5,
                     waits=[("sAC", l1["sAC"])])
            yield
            n1 = ts(tA[:, :], tA[:, :], stm[:, 0:1], ALU.subtract,
                    waits=[("sDV", vv["sDV"]), ("sAC", l2["sAC"])])
            yield
            n2 = ts(tA[:, :], tA[:, :], str_[:, 0:1], ALU.mult,
                    waits=[("sDV", n1["sDV"])])
            tail_state["ln_out"] = n2["sDV"]

        def tail_gen(qt=qt, tau=tau):
            st = tail_state
            BK7 = PS[:, 3584:4096]
            x1w_s = [0, 0, 0, 0]
            bb = dma("gpsimd", bbuf[:, :], dr["b_o"].partition_broadcast(128),
                     sem="qBB", waits=[("sDV", st["bbuf_free"])]
                     if st["bbuf_free"] else [("sDV", ckv_free)])
            st["bo_ld"] = bb["qBB"]
            oldv = {}
            lds = st["wpre"] if st["wpre"] else {}
            st["wpre"] = None
            for cp_ in range(4):
                if cp_ not in lds:
                    wl = [("sPE", st["wbuf_free"][cp_])] \
                        if st["wbuf_free"][cp_] else [("qWsc", wsc_cnt)]
                    lds[cp_] = dma("gpsimd", wbuf[:, 2048 * cp_:2048 * (cp_ + 1)],
                                   wo_p[:, 2048 * cp_:2048 * (cp_ + 1)],
                                   sem=qB[cp_], waits=wl)[qB[cp_]]
                    yield
            t2lds = {}
            for sub in range(4):
                par = sub % 2
                w0 = [("sDV", st["h2_free"][par])] if st["h2_free"][par] else []
                if st["h2_qh"][par]:
                    w0.append((qTH[par], st["h2_qh"][par]))
                if qt == 0 and sub < 2:
                    w0.append(("sPE", hload_seen[par]))
                hld = dma("sync", h2[:, 1024 * par:1024 * (par + 1)],
                          dr["h"][tau + 128 * sub:tau + 128 * (sub + 1), :],
                          sem=qTH[par], waits=w0)
                st["h2_qh"][par] = hld[qTH[par]]
                e1d = 0
                for half in range(2):
                    for c in range(8):
                        cp_ = c // 2
                        if sub == 0 and half == 0:
                            nidx = min(qt * NP + c + 1, NQT * NP - 1)
                            while not norm_[nidx]:
                                yield "spin"
                            w0 = [("qOD", norm_[qt * NP + c])]
                            if c == 0 and st["t1_mm_done"]:
                                w0.append(("sPE", st["t1_mm_done"]))
                            if c > 0:
                                w0.append(("qOL", oldv[c - 1]))
                            ol = dma("sync", osb[:, 512 * c:512 * (c + 1)],
                                     o_dram[(qt * NP + c) * 128:(qt * NP + c + 1) * 128, :],
                                     sem="qOL", waits=w0)
                            oldv[c] = ol["qOL"]
                        w2 = [(qB[cp_], lds[cp_])]
                        if sub == 0 and half == 0:
                            w2.append(("qOL", oldv[c]))
                        if c == 0:
                            w2 += [("sDV", st["bank7_free"])]
                        m = mm(BK7,
                               osb[:, 512 * c + 128 * sub:512 * c + 128 * (sub + 1)],
                               wbuf[:, 2048 * cp_ + 1024 * (c % 2) + 512 * half:
                                    2048 * cp_ + 1024 * (c % 2) + 512 * (half + 1)],
                               waits=w2, start=(c == 0), stop=(c == 7))
                        if c % 2 == 1:
                            if sub == 3 and half == 1:
                                st["wbuf_free"][cp_] = m["sPE"]
                                t2lds[cp_] = dma(
                                    "gpsimd", wbuf[:, 2048 * cp_:2048 * (cp_ + 1)],
                                    wf_p[:, 2048 * cp_:2048 * (cp_ + 1)],
                                    sem=qB[cp_],
                                    waits=[("sPE", m["sPE"])])[qB[cp_]]
                            yield
                    e1w = [("sPE", m["sPE"]), (qTH[par], st["h2_qh"][par])]
                    if st["tA_q"] and half == 0:
                        e1w.append(st["tA_q"])
                    if qt == 0 and sub == 0 and half == 0:
                        e1w.append(("sDV", phase0_dv))
                    e1 = tt(tA[:, 512 * half:512 * (half + 1)], BK7,
                            h2[:, 1024 * par + 512 * half:1024 * par + 512 * (half + 1)],
                            ALU.add, waits=e1w + ([("sDV", e1d)] if e1d else []))
                    e1d = e1["sDV"]
                    st["bank7_free"] = e1["sDV"]
                    yield
                if sub == 3:
                    st["t1_mm_done"] = m["sPE"]
                e2 = tt(tA[:, :], tA[:, :], bbuf[:, :], ALU.add,
                        waits=[("sDV", e1d), ("qBB", st["bo_ld"])])
                st["bbuf_free"] = e2["sDV"]
                yield
                for _ in ln_chain_gen(e2["sDV"], par):
                    yield
                st["h2_free"][par] = st["sq_free"]
                xw = dma("gpsimd", x1_scr[128 * sub:128 * (sub + 1), :], tA[:, :],
                         sem="qX", waits=[("sDV", st["ln_out"])])
                st["tA_q"] = ("qX", xw["qX"])
                x1w_s[sub] = xw["qX"]
                yield
            bb = dma("gpsimd", bbuf[:, :], dr["b_ffn"].partition_broadcast(128),
                     sem="qBB", waits=[("sDV", st["bbuf_free"])])
            st["bf_ld"] = bb["qBB"]

            def issue_xld(sub2):
                par2 = sub2 % 2
                w = [("qX", x1w_s[sub2])]
                if st["h2_free"][par2]:
                    w.append(("sDV", st["h2_free"][par2]))
                if st["h2_qh"][par2]:
                    w.append((qTH[par2], st["h2_qh"][par2]))
                xl = dma("sync", h2[:, 1024 * par2:1024 * (par2 + 1)],
                         x1_scr[128 * sub2:128 * (sub2 + 1), :], sem=qTH[par2],
                         waits=w)
                st["h2_qh"][par2] = xl[qTH[par2]]
                return xl[qTH[par2]]

            xlds = {0: issue_xld(0), 1: issue_xld(1)}
            for sub in range(4):
                par = sub % 2
                xldv = xlds[sub]
                for rnd in range(2):
                    for c4 in range(4):
                        c = 4 * rnd + c4
                        w2 = []
                        if c4 == 0:
                            w2 = [(qTH[par], xldv), ("sDV", st["bank7_free"])]
                            if st["x1T_free"][par]:
                                w2.append(("sPE", st["x1T_free"][par]))
                        last = tr(PS[:, 3584 + 128 * c4:3584 + 128 * (c4 + 1)],
                                  h2[:, 1024 * par + 128 * c:1024 * par + 128 * (c + 1)],
                                  waits=w2)
                    dst = x1T2[:, 0:2048].rearrange("p (g c t) -> p g c t", g=2, t=128)[
                        :, par, 4 * rnd:4 * rnd + 4, :]
                    e = cp(dst, BK7.rearrange("p (c t) -> p c t", t=128),
                           waits=[("sPE", last["sPE"])])
                    st["bank7_free"] = e["sDV"]
                    yield
                st["xT_evac"] = e["sDV"]
                for c in range(8):
                    w2 = [("sDV", st["xT_evac"])] if c == 0 else []
                    m = mm(PS[0:16, 3584:3712], a_sb[:, 16 * c:16 * (c + 1)],
                           x1T2[:, 1024 * par + 128 * c:1024 * par + 128 * (c + 1)],
                           waits=w2, start=(c == 0), stop=(c == 7))
                zw = [("sPE", m["sPE"])]
                if st.get("lora_done"):
                    zw.append(("sPE", st["lora_done"]))
                ze = cp(zb_sb[0:16, 0:128], PS[0:16, 3584:3712], waits=zw)
                st["bank7_free"] = ze["sDV"]
                yield
                e1d = 0
                for half in range(2):
                    for c in range(8):
                        cp_ = c // 2
                        w2 = [(qB[cp_], t2lds[cp_])]
                        if c == 0:
                            w2.append(("sDV", st["bank7_free"]))
                        m = mm(BK7,
                               x1T2[:, 1024 * par + 128 * c:1024 * par + 128 * (c + 1)],
                               wbuf[:, 2048 * cp_ + 1024 * (c % 2) + 512 * half:
                                    2048 * cp_ + 1024 * (c % 2) + 512 * (half + 1)],
                               waits=w2, start=(c == 0), stop=False)
                        if c % 2 == 1:
                            if sub == 3 and half == 1 and qt + 1 < NQT:
                                st["wbuf_free"][cp_] = m["sPE"]
                                if st["wpre"] is None:
                                    st["wpre"] = {}
                                st["wpre"][cp_] = dma(
                                    "gpsimd", wbuf[:, 2048 * cp_:2048 * (cp_ + 1)],
                                    wo_p[:, 2048 * cp_:2048 * (cp_ + 1)],
                                    sem=qB[cp_],
                                    waits=[("sPE", m["sPE"])])[qB[cp_]]
                            yield
                    m = mm(BK7, zb_sb[0:32, 0:128],
                           bpad_sb[0:32, 512 * half:512 * (half + 1)],
                           start=False, stop=True)
                    st["lora_done"] = m["sPE"]
                    e1w = [("sPE", m["sPE"]), ("qBB", st["bf_ld"]),
                           (qTH[par], st["h2_qh"][par])]
                    if st["tA_q"] and half == 0:
                        e1w.append(st["tA_q"])
                    e1 = tt(tA[:, 512 * half:512 * (half + 1)], BK7,
                            h2[:, 1024 * par + 512 * half:1024 * par + 512 * (half + 1)],
                            ALU.add, waits=e1w + ([("sDV", e1d)] if e1d else []))
                    e1d = e1["sDV"]
                    st["bank7_free"] = e1["sDV"]
                    yield
                st["x1T_free"][par] = m["sPE"]
                e2 = tt(tA[:, :], tA[:, :], bbuf[:, :], ALU.add,
                        waits=[("sDV", e1d), ("qBB", st["bf_ld"])])
                st["bbuf_free"] = e2["sDV"]
                yield
                for _ in ln_chain_gen(e2["sDV"], par):
                    yield
                st["h2_free"][par] = st["sq_free"]
                if sub + 2 <= 3:
                    xlds[sub + 2] = issue_xld(sub + 2)
                od = dma("scalar", out_d[tau + 128 * sub:tau + 128 * (sub + 1), :],
                         tA[:, :], sem="qO", waits=[("sDV", st["ln_out"])])
                st["tA_q"] = ("qO", od["qO"])
                yield

        bg_tail.append(tail_gen())
        NGQ = NP * NKC
        for g in range(NGQ):
            p, kc = g // NKC, g % NKC
            gctr[0] = qt * NGQ + g
            S_.gpos = (qt, p, kc)
            emit_S(p, kc)
            emit_exp(p, kc)
            if g >= 2:
                emit_pv((g - 2) // NKC, (g - 2) % NKC)
            if kc == 1 and qt * NP + p > 0:
                P2 = qt * NP + p - 1
                e_dv, c2_dv = ep_start(P2)
                bg_ep.append(ep_chain_gen(P2, e_dv, c2_dv))
            if bg_ep:
                try:
                    next(bg_ep[0])
                except StopIteration:
                    bg_ep.pop(0)
            if kc == NKC // 2 and qt + 1 < NQT and p in (3, 7):
                if p == 3:
                    bg_qb.append(qbuild_gen(qt + 1, [0, 1, 2], True))
                else:
                    bg_qb.append(qbuild_gen(qt + 1, [3, 4, 5, 6], False))
            for q_ in (() if kc in (0, 1, 2) else (bg_qb, bg_tail)):
                if not q_:
                    continue
                for _spin in range(64):
                    try:
                        v = next(q_[0])
                    except StopIteration:
                        q_.pop(0)
                        break
                    if v != "spin":
                        break
        emit_pv(NP - 1, NKC - 2)
        emit_pv(NP - 1, NKC - 1)
        if qt == NQT - 1:
            P2 = qt * NP + NP - 1
            e_dv, c2_dv = ep_start(P2)
            bg_ep.append(ep_chain_gen(P2, e_dv, c2_dv))
        if qt + 1 < NQT:
            bg_qb.append(qbuild_gen(qt + 1, [7], False))
        while bg_qb:
            try:
                next(bg_qb[0])
            except StopIteration:
                bg_qb.pop(0)

        if qt == NQT - 1:
            for p_ in range(NP):
                pass
    gctr[0] = 1 << 60
    for q_ in (bg_ep, bg_qb, bg_tail):
        while q_:
            try:
                next(q_[0])
            except StopIteration:
                q_.pop(0)
    wonly("gpsimd", [("qO", 16 * 4 * NQT)])

    from contextlib import ExitStack
    with ExitStack() as ctx:
        sems = {}
        for s in S_.cnt:
            sems[s] = ctx.enter_context(nc.semaphore(s))
        block = ctx.enter_context(nc.Block())

        def mk(eng_name):
            def emit(eng):
                for fn, waits, incs in S_.events[eng_name]:
                    for s, v in waits:
                        if v > 0:
                            eng.wait_ge(sems[s], v)
                    if fn is None:
                        continue
                    ins = fn(eng)
                    for s, a in incs:
                        ins = ins.then_inc(sems[s], a)
            return emit
        block.sync(mk("sync"))
        block.gpsimd(mk("gpsimd"))
        block.tensor(mk("tensor"))
        block.scalar(mk("scalar"))
        block.vector(mk("vector"))
    return nc


def kernel(**inputs):
    h_all = np.asarray(inputs["L0_attn_h"], np.float32)
    B, S, _ = h_all.shape
    const = _host_constants(inputs, S)
    nc = build_program(S)
    shared = {k: np.ascontiguousarray(v) for k, v in const.items()
              if k not in ("W_dkv", "W_dq", "W_uv", "W_o", "W_ffn", "a_l")}
    shared.update({
        "W_dkv": const["W_dkv"], "W_dq": const["W_dq"], "W_uv": const["W_uv"],
        "W_o": const["W_o"], "W_ffn": const["W_ffn"], "a_l": const["a_l"],
    })
    in_maps = []
    for c in range(B):
        m = dict(shared)
        m["h"] = np.ascontiguousarray(h_all[c])
        in_maps.append(m)
    res = run_bass_kernel_spmd(nc, in_maps, core_ids=list(range(B)))
    out = np.stack([np.asarray(r["out"]) for r in res.results], axis=0)
    return out.astype(np.float32)


# version 47
# speedup vs baseline: 1.0162x; 1.0020x over previous
import numpy as np
import ml_dtypes

import concourse.bass as bass
import concourse.mybir as mybir
from concourse.bass_utils import run_bass_kernel_spmd

F32 = mybir.dt.float32
BF16 = mybir.dt.bfloat16
AF = mybir.ActivationFunctionType
ALU = mybir.AluOpType
AX = mybir.AxisListType
bfdt = ml_dtypes.bfloat16

D = 1024
H = 16
NP = H // 2
RANK = 16

ENGS = ("sync", "gpsimd", "tensor", "scalar", "vector")


class Sched:
    def __init__(self):
        self.events = {e: [] for e in ENGS}
        self.cnt = {}

    def sem(self, name):
        self.cnt.setdefault(name, 0)

    def ev(self, eng, fn, waits=(), incs=()):
        out = {}
        for s, a in incs:
            self.cnt[s] += a
            out[s] = self.cnt[s]
        self.events[eng].append((fn, list(waits), list(incs)))
        return out

    def val(self, s):
        return self.cnt[s]


def _host_constants(inp, S):
    f32 = np.float32
    W_uq = np.asarray(inp["L0_attn_W_uq"], f32)
    W_uk = np.asarray(inp["L0_attn_W_uk"], f32)
    b_uq = np.asarray(inp["L0_attn_b_uq"], f32)
    b_uk = np.asarray(inp["L0_attn_b_uk"], f32)
    W_qr = np.asarray(inp["L0_attn_W_qr"], f32)
    W_kr = np.asarray(inp["L0_attn_W_kr"], f32)
    b_qr = np.asarray(inp["L0_attn_b_qr"], f32)
    b_kr = np.asarray(inp["L0_attn_b_kr"], f32)

    def cat_nope(W, b):
        Wc = np.zeros((W.shape[0], 128 * NP), f32)
        bc = np.zeros((128, NP), f32)
        for p in range(NP):
            Wc[:, 128 * p + 0:128 * p + 48] = W[:, 96 * p:96 * p + 48]
            Wc[:, 128 * p + 64:128 * p + 112] = W[:, 96 * p + 48:96 * p + 96]
            bc[0:48, p] = b[96 * p:96 * p + 48]
            bc[64:112, p] = b[96 * p + 48:96 * p + 96]
        return Wc, bc

    Wq_cat, bq_cat = cat_nope(W_uq, b_uq)
    Wk_cat, bk_cat = cat_nope(W_uk, b_uk)

    def jm(Wr, br):
        nin = Wr.shape[0]
        Wj = np.zeros((nin, 384), f32)
        bj = np.zeros((128, 3), f32)
        for h in range(16):
            for j in range(8):
                Wj[:, 0 * 128 + 16 * j + h] = Wr[:, 16 * h + j]
                bj[16 * j + h, 0] = br[16 * h + j]
                js = j + 4 if j < 4 else j - 4
                Wj[:, 1 * 128 + 16 * j + h] = Wr[:, 16 * h + js]
                bj[16 * j + h, 1] = br[16 * h + js]
                Wj[:, 2 * 128 + 16 * j + h] = Wr[:, 16 * h + 8 + j]
                bj[16 * j + h, 2] = br[16 * h + 8 + j]
        return Wj, bj

    Wqr_jm, bqr_jm = jm(W_qr, b_qr)
    Wkr_jm, bkr_jm = jm(W_kr, b_kr)

    inv_freq = (1.0 / (10000.0 ** (np.arange(0, 8, 2, dtype=f32) / np.float32(8.0)))).astype(f32)
    t = (np.arange(S, dtype=f32) / np.float32(40.0)).astype(f32)
    freqs = (t[:, None] * inv_freq[None, :]).astype(f32)
    cos4 = np.cos(freqs).astype(f32)
    sin4 = np.sin(freqs).astype(f32)
    cosT = np.zeros((128, S), f32)
    sinT = np.zeros((128, S), f32)
    for h in range(16):
        for j in range(8):
            r = 16 * j + h
            cosT[r] = cos4[:, j % 4]
            sinT[r] = (-sin4[:, j % 4]) if j < 4 else sin4[:, j % 4]

    bpad = np.zeros((32, 1024), f32)
    bpad[0:16] = np.asarray(inp["L0_ffn_b"], f32)
    ones32 = np.zeros((32, 128), f32)
    ones32[0] = 1.0

    return dict(
        Wq_cat=Wq_cat, bq_cat=bq_cat, Wk_cat=Wk_cat, bk_cat=bk_cat,
        Wqr_jm=Wqr_jm, bqr_jm=bqr_jm, Wkr_jm=Wkr_jm, bkr_jm=bkr_jm,
        cosT=cosT.astype(bfdt), sinT=sinT.astype(bfdt), bpad=bpad, ones32=ones32,
        ident=np.eye(128, dtype=f32),
        bdkv=np.asarray(inp["L0_attn_b_dkv"], f32).reshape(128, 1),
        bdq=np.asarray(inp["L0_attn_b_dq"], f32).reshape(128, 1),
        b_uv=np.asarray(inp["L0_attn_b_uv"], f32).reshape(1, 1024),
        b_o=np.asarray(inp["L0_attn_b_o"], f32).reshape(1, 1024),
        b_ffn=np.asarray(inp["L0_ffn_bias"], f32).reshape(1, 1024),
        W_dkv=np.asarray(inp["L0_attn_W_dkv"], f32),
        W_dq=np.asarray(inp["L0_attn_W_dq"], f32),
        W_uv=np.asarray(inp["L0_attn_W_uv"], f32),
        W_o=np.asarray(inp["L0_attn_W_o"], f32),
        W_ffn=np.asarray(inp["L0_ffn_kernel"], f32),
        a_l=np.asarray(inp["L0_ffn_a"], f32),
    )


def build_program(S):
    NQT = S // 512
    NKC = S // 128
    NSP = S // 512
    nc = bass.Bass("TRN2", target_bir_lowering=False)

    dr = {}
    def din(name, shape, dt=F32):
        dr[name] = nc.dram_tensor(name, list(shape), dt, kind="ExternalInput").ap()
    din("h", (S, D))
    din("W_dkv", (D, 128)); din("W_dq", (D, 128))
    din("Wq_cat", (128, 1024)); din("Wk_cat", (128, 1024))
    din("Wqr_jm", (128, 384)); din("Wkr_jm", (D, 384))
    din("W_uv", (128, 1024)); din("W_o", (D, D)); din("W_ffn", (D, D))
    din("a_l", (D, RANK)); din("bpad", (32, 1024))
    din("cosT", (128, S), BF16); din("sinT", (128, S), BF16)
    din("ident", (128, 128)); din("ones32", (32, 128))
    din("bdkv", (128, 1)); din("bdq", (128, 1))
    din("bq_cat", (128, NP)); din("bk_cat", (128, NP))
    din("bqr_jm", (128, 3)); din("bkr_jm", (128, 3))
    din("b_uv", (1, 1024)); din("b_o", (1, 1024)); din("b_ffn", (1, 1024))
    out_d = nc.dram_tensor("out", [S, D], F32, kind="ExternalOutput").ap()
    rden = nc.dram_tensor("rden", [NQT * NP, 1024], F32, kind="Internal").ap()
    o_dram = nc.dram_tensor("o_dram", [NQT * NP * 128, 512], BF16, kind="Internal").ap()
    rotk_scr = nc.dram_tensor("rotk_scr", [256, S], BF16, kind="Internal").ap()
    rotq_scr = nc.dram_tensor("rotq_scr", [256, 1024], BF16, kind="Internal").ap()
    cq_scr = nc.dram_tensor("cq_scr", [128, S], BF16, kind="Internal").ap()
    x1_scr = nc.dram_tensor("x1_scr", [512, D], F32, kind="Internal").ap()
    wo_p = nc.dram_tensor("wo_p", [128, 8 * D], BF16, kind="Internal").ap()
    wf_p = nc.dram_tensor("wf_p", [128, 8 * D], BF16, kind="Internal").ap()
    p0_scr = nc.dram_tensor("p0_scr", [1, 192], F32, kind="Internal").ap()

    sbt = nc.alloc_sbuf_tensor
    Ksb = sbt("Ksb", [128, NP * S], BF16)
    Vsb = sbt("Vsb", [128, NKC * 1040], BF16)
    wq_sb = sbt("wq_sb", [128, 1024], BF16)
    wqr_sb = sbt("wqr_sb", [128, 384], BF16)
    wkr_sb = sbt("wkr_sb", [128, 3072], BF16)
    a_sb = sbt("a_sb", [128, 128], BF16)
    bpad_sb = sbt("bpad_sb", [32, 1024], BF16)
    ident_sb = sbt("ident_sb", [128, 128], F32)
    ones1_sb = sbt("ones1_sb", [128, 1], BF16)
    bdkv_sb = sbt("bdkv_sb", [128, 1], F32)
    bdq_sb = sbt("bdq_sb", [128, 1], F32)
    bqc_sb = sbt("bqc_sb", [128, NP], F32)
    bkc_sb = sbt("bkc_sb", [128, NP], F32)
    bqr_sb = sbt("bqr_sb", [128, 3], F32)
    bkr_sb = sbt("bkr_sb", [128, 3], F32)
    eps_sb = sbt("eps_sb", [128, 1], F32)
    scl_sb = sbt("scl_sb", [128, 1], F32)
    rscl_sb = sbt("rscl_sb", [128, 1], F32)
    s32_sb = sbt("s32_sb", [32, 2], F32)
    gmax_sb = sbt("gmax_sb", [128, 9], F32)
    Qsb = sbt("Qsb", [128, NP * 512], BF16)
    _ = None
    Psb = sbt("Psb", [128, 3072], BF16)
    osb = sbt("osb", [128, NP * 512], BF16)
    cos_sb = sbt("cos_sb", [128, 512], BF16)
    sin_sb = sbt("sin_sb", [128, 512], BF16)
    cq_buf = sbt("cq_buf", [128, 1024], BF16)
    bcast_sb = sbt("bcast_sb", [128, 512], F32)
    c0b_sb = sbt("c0b_sb", [128, 512], BF16)
    c0s_sb = sbt("c0s_sb", [128, 512], BF16)
    wbuf = sbt("wbuf", [128, 8192], BF16)
    bbuf = sbt("bbuf", [128, 1024], BF16)
    h2 = sbt("h2", [128, 2048], F32)
    tA = sbt("tA", [128, 1024], F32)
    x1T2 = sbt("x1T2", [128, 2048], BF16)
    zb_sb = sbt("zb_sb", [32, 128], BF16)
    st1 = sbt("st1", [128, 1], F32)
    st2 = sbt("st2", [128, 1], F32)
    stm = sbt("stm", [128, 1], F32)
    stv = sbt("stv", [128, 1], F32)
    str_ = sbt("str_", [128, 1], F32)
    oun_sb = sbt("oun_sb", [128, 1024], BF16)
    assert nc.sbuf_bytes_remaining >= 0

    PS = nc.alloc_psum_tensor("PS", [128, 4096], F32)

    hT = Qsb
    wkcat_sb = osb[:, 0:1024]
    wdkv_sb = osb[:, 1024:2048]
    wdq_sb = osb[:, 2048:3072]
    wuv_sb = osb[:, 3072:4096]
    wbufF = (wbuf.ap() if hasattr(wbuf, "ap") else wbuf).bitcast(F32)
    wstage = (tA, wbufF[:, 0:1024])
    wstage_out = x1T2[:, 0:1024]
    qacc = x1T2[:, 1024:2048]
    qcmp = Psb[:, 0:1024]
    qsgn = Psb[:, 1024:2048]
    PSB4 = [Psb[:, 0:1024], Psb[:, 1024:2048], Psb[:, 2048:3072],
            wkr_sb[:, 0:1024]]
    qabs = wbufF[:, 1024:2048]

    S_ = Sched()
    for s in ("qSm", "qW0", "qW1", "qT", "qC0", "qC1", "qR", "qWsc", "qKb",
              "qCw", "qX", "qO", "qH0", "qH1", "qTH0", "qTH1", "qB0", "qB1", "qB2", "qB3", "qBB", "qBC", "qOD", "qOL",
              "qRp0", "qRp1", "qRp2", "qRp3", "qRp4", "qRp5", "qRp6", "qRp7",
              "qCVa", "qCVb", "qCVc", "qCVd", "qCVe",
              "sPE", "sDV", "sAC", "sGP"):
        S_.sem(s)
    qH = ("qH0", "qH1")
    qW = ("qW0", "qW1")
    qB = ("qB0", "qB1", "qB2", "qB3")
    qTH = ("qTH0", "qTH1")
    qC = ("qC0", "qC1")

    def dma(q, out, in_, waits=(), sem="qSm", extra=()):
        return S_.ev(q, (lambda e, o=out, i=in_: e.dma_start(out=o, in_=i)),
                     waits, [(sem, 16)] + list(extra))

    def mm(out, lhsT, rhs, waits=(), start=True, stop=True, tp=(0, 0), skip=False,
           extra=()):
        return S_.ev("tensor",
                     (lambda e, o=out, l=lhsT, r=rhs, s1=start, s2=stop, t=tp, k=skip:
                      e.matmul(o, l, r, start=s1, stop=s2, tile_position=t,
                               skip_group_check=k)),
                     waits, [("sPE", 1)] + list(extra))

    def tr(out, in_, waits=(), extra=()):
        return S_.ev("tensor",
                     (lambda e, o=out, i=in_: e.transpose(o, i, ident_sb[:, :])),
                     waits, [("sPE", 1)] + list(extra))

    def act(out, in_, func, waits=(), scale=1.0, bias=0.0, extra=()):
        return S_.ev("scalar",
                     (lambda e, o=out, i=in_, f=func, sc=scale, b=bias:
                      e.activation(o, i, f, scale=sc, bias=b)),
                     waits, [("sAC", 1)] + list(extra))

    def tt(out, in0, in1, op, waits=(), extra=()):
        return S_.ev("vector",
                     (lambda e, o=out, x=in0, y=in1, p=op: e.tensor_tensor(o, x, y, op=p)),
                     waits, [("sDV", 1)] + list(extra))

    def ts(out, in0, s1, op0, waits=(), extra=()):
        return S_.ev("vector",
                     (lambda e, o=out, i=in0, a=s1, p=op0:
                      e.tensor_scalar(o, i, a, None, op0=p)),
                     waits, [("sDV", 1)] + list(extra))

    def ts2(out, in0, s1, op0, s2, op1, waits=(), extra=()):
        return S_.ev("vector",
                     (lambda e, o=out, i=in0, a=s1, p=op0, b=s2, q=op1:
                      e.tensor_scalar(o, i, a, b, op0=p, op1=q)),
                     waits, [("sDV", 1)] + list(extra))

    def red(out, in_, op, absval=False, waits=(), extra=()):
        return S_.ev("vector",
                     (lambda e, o=out, i=in_, p=op, av=absval:
                      e.tensor_reduce(o, i, axis=AX.X, op=p, apply_absolute_value=av)),
                     waits, [("sDV", 1)] + list(extra))

    def cp(out, in_, waits=(), extra=()):
        return S_.ev("vector", (lambda e, o=out, i=in_: e.tensor_copy(o, i)),
                     waits, [("sDV", 1)] + list(extra))

    def recip(out, in_, waits=(), extra=()):
        return S_.ev("vector", (lambda e, o=out, i=in_: e.reciprocal(o, i)),
                     waits, [("sDV", 1)] + list(extra))

    def mset(ap, v, waits=()):
        return S_.ev("gpsimd", (lambda e, a=ap, x=v: e.memset(a, x)),
                     waits, [("sGP", 1)])

    def wonly(eng, waits):
        S_.ev(eng, None, waits, [])

    V = S_.val

    mset(ones1_sb[:, :], 1.0)
    mset(Vsb[:, :], 1.0)
    mset(eps_sb[:, :], 1e-5)
    mset(s32_sb[:, :], 0.0)
    mset(zb_sb[:, :], 0.0)
    g0 = V("sGP")

    for nm, dst in (("ident", ident_sb), ("bdkv", bdkv_sb),
                    ("bdq", bdq_sb), ("bq_cat", bqc_sb), ("bk_cat", bkc_sb),
                    ("bqr_jm", bqr_sb), ("bkr_jm", bkr_sb)):
        dma("sync", dst[:, :], dr[nm])
    buv = dma("gpsimd", bbuf[:, :], dr["b_uv"].partition_broadcast(128), sem="qBB")
    smalls_done = V("qSm")

    pre_h = []
    for i in range(2):
        ld = dma("sync", h2[:, 1024 * i:1024 * (i + 1)],
                 dr["h"][128 * i:128 * (i + 1), :], sem=qH[i])
        pre_h.append(ld[qH[i]])

    cvst = (wbufF[:, 2048:3072], wbufF[:, 3072:4096])
    cv_jobs = [
        ("dkv", dr["W_dkv"].rearrange("(c p) n -> p c n", p=128), wdkv_sb, 1024),
        ("dq", dr["W_dq"].rearrange("(c p) n -> p c n", p=128), wdq_sb, 1024),
    ] + [
        (f"kr{c}", dr["Wkr_jm"][128 * c:128 * (c + 1), :],
         wkr_sb[:, 384 * c:384 * (c + 1)], 384) for c in range(8)
    ] + [
        ("kcat", dr["Wk_cat"], wkcat_sb, 1024),
        ("uv", dr["W_uv"], wuv_sb, 1024),
    ]
    qcv = {}
    cv_free = [None, None]
    cv_lastld = [None, None]
    for i, (nm, src_ap, dst, w) in enumerate(cv_jobs):
        par = i % 2
        ld = dma("sync", cvst[par][:, 0:w], src_ap, sem=qW[par],
                 waits=([cv_free[par]] if cv_free[par] else []))
        cv_lastld[par] = (qW[par], ld[qW[par]])
        r = act(dst, cvst[par][:, 0:w], AF.Copy, waits=[(qW[par], ld[qW[par]])])
        cv_free[par] = ("sAC", r["sAC"])
        qcv[nm] = ("sAC", r["sAC"])
    qcv["kr"] = qcv["kr7"]
    conv_jobs = [
        (dr["Wq_cat"], wq_sb[:, 0:1024], 128, 1024),
        (dr["Wqr_jm"], wqr_sb[:, 0:384], 128, 384),
        (dr["a_l"].rearrange("(c p) n -> p c n", p=128), a_sb[:, 0:128], 128, 128),
        (dr["bpad"], bpad_sb[0:32, 0:1024], 32, 1024),
    ]
    stage_free = [cv_lastld[0], cv_lastld[1]]

    for half in range(2):
        dma("gpsimd",
            wo_p[:, 4096 * half:4096 * (half + 1)].rearrange(
                "p (j k) -> p j k", k=1024),
            dr["W_o"][512 * half:512 * (half + 1), :].rearrange(
                "(j p) k -> p j k", p=128),
            sem="qWsc")

    wsc_l = [0]

    def p0_gen():
        for i, (src_ap, dst, pp, w) in enumerate(conv_jobs):
            par = i % 2
            ld = dma("sync", wstage[par][0:pp, 0:w], src_ap, sem=qW[par],
                     waits=([stage_free[par]] if stage_free[par] else []))
            yield
            r = cp(dst, wstage[par][0:pp, 0:w], waits=[(qW[par], ld[qW[par]])])
            stage_free[par] = ("sDV", r["sDV"])
            yield
        for c in range(8):
            par = c % 2
            ld = dma("sync", wstage[par][:, :], dr["W_ffn"][128 * c:128 * (c + 1), :],
                     sem=qW[par], waits=[stage_free[par]])
            yield
            r = red(gmax_sb[:, c:c + 1], wstage[par][:, :], ALU.max, absval=True,
                    waits=[(qW[par], ld[qW[par]])])
            stage_free[par] = ("sDV", r["sDV"])
            yield
        r = red(gmax_sb[:, 8:9], gmax_sb[:, 0:8], ALU.max,
                waits=[("sDV", r["sDV"])])
        st0 = dma("sync", p0_scr[0:1, 0:128], gmax_sb[:, 8:9],
                  waits=[("sDV", r["sDV"])])
        yield
        ld0 = dma("sync", bcast_sb[0:1, 0:128], p0_scr[0:1, 0:128],
                  waits=[("qSm", st0["qSm"])])
        yield
        r = red(stm[0:1, 0:1], bcast_sb[0:1, 0:128], ALU.max,
                waits=[("qSm", ld0["qSm"])])
        r2 = ts(stv[0:1, 0:1], stm[0:1, 0:1], float(np.float32(1.0 / 7.0)),
                ALU.mult, waits=[("sDV", r["sDV"])])
        r3 = recip(str_[0:1, 0:1], stv[0:1, 0:1], waits=[("sDV", r2["sDV"])])
        r4 = cp(s32_sb[0:1, 0:1], stv[0:1, 0:1],
                waits=[("sDV", r3["sDV"]), ("sGP", g0)])
        r5 = cp(s32_sb[0:1, 1:2], str_[0:1, 0:1], waits=[("sDV", r4["sDV"])])
        st1 = dma("sync", p0_scr[0:1, 128:130], s32_sb[0:1, 0:2],
                  waits=[("sDV", r5["sDV"])])
        yield
        dma("sync", scl_sb[:, :], p0_scr[0:1, 128:129].partition_broadcast(128),
            waits=[("qSm", st1["qSm"])])
        blr = dma("sync", rscl_sb[:, :],
                  p0_scr[0:1, 129:130].partition_broadcast(128))
        qsc_q = blr["qSm"]
        yield
        for c in range(8):
            par = c % 2
            ld = dma("sync", wstage[par][:, :], dr["W_ffn"][128 * c:128 * (c + 1), :],
                     sem=qW[par], waits=[stage_free[par]])
            yield
            t = ts(wstage[par][:, :], wstage[par][:, :], rscl_sb[:, 0:1], ALU.mult,
                   waits=[(qW[par], ld[qW[par]]), ("qSm", qsc_q)])
            u = act(qabs, wstage[par][:, :], AF.Abs, waits=[("sDV", t["sDV"])])
            yield
            r = ts(qacc, qabs, 0.5, ALU.is_ge, waits=[("sAC", u["sAC"])])
            for j in range(1, 7):
                r1 = ts(qcmp, qabs, float(j) + 0.5, ALU.is_ge,
                        waits=[("sDV", r["sDV"])])
                r = tt(qacc, qacc, qcmp, ALU.add, waits=[("sDV", r1["sDV"])])
                if j % 2 == 0:
                    yield
            g1 = ts(qsgn, wstage[par][:, :], 0.0, ALU.is_ge, waits=[("sDV", r["sDV"])])
            stage_free[par] = ("sDV", g1["sDV"])
            g2 = ts2(qsgn, qsgn, 2.0, ALU.mult, -1.0, ALU.add,
                     waits=[("sDV", g1["sDV"])])
            yield
            qq = tt(qacc, qacc, qsgn, ALU.mult, waits=[("sDV", g2["sDV"])])
            wq_ = ts(wstage_out, qacc, scl_sb[:, 0:1], ALU.mult,
                     waits=[("sDV", qq["sDV"]), ("qWsc", wsc_l[0])])
            st = dma("gpsimd", wf_p[:, 1024 * c:1024 * (c + 1)], wstage_out,
                     sem="qWsc", waits=[("sDV", wq_["sDV"])])
            wsc_l[0] = st["qWsc"]
            yield

    p0q = [p0_gen()]

    def p0_step(n=1):
        for _ in range(n):
            if not p0q:
                return
            try:
                next(p0q[0])
            except StopIteration:
                p0q.pop(0)

    hload_gate = [0, 0]
    hload_seen = [0, 0]
    tbl_free = 0
    ckv_free = 0
    rope_done = 0
    kb_cnt = 0
    cw_cnt = 0
    last_knope_evac = 0
    last_pa_pe = 0
    hT_prev_consumed = 0
    for s in range(NSP):
        tau = 512 * s
        tl = dma("sync", cos_sb[:, :], dr["cosT"][:, tau:tau + 512], sem="qT",
                 waits=([("sDV", tbl_free)] if tbl_free else []))
        tl2 = dma("sync", sin_sb[:, :], dr["sinT"][:, tau:tau + 512], sem="qT")
        tbl_val = tl2["qT"]

        ev_done = [0] * 4
        for i in range(4):
            par = i % 2
            if s == 0 and i < 2:
                ldv = pre_h[i]
            else:
                w = [("sPE", hload_seen[par])] if hload_seen[par] else []
                if hload_gate[par]:
                    w.append(("sDV", hload_gate[par]))
                    hload_gate[par] = 0
                ld = dma("sync", h2[:, 1024 * par:1024 * (par + 1)],
                         dr["h"][tau + 128 * i:tau + 128 * (i + 1), :], sem=qH[par],
                         waits=w)
                ldv = ld[qH[par]]
            for c in range(8):
                bk = 2 * par + c // 4
                w2 = []
                if c == 0:
                    w2 = [(qH[par], ldv)]
                    if s == 0 and i == 0:
                        w2.append(("qSm", smalls_done))
                    if i >= 2:
                        w2 += [("sAC", ev_done[2 * par]), ("sAC", ev_done[2 * par + 1])]
                    elif s > 0:
                        w2 += [("sDV", hT_prev_consumed), ("sAC", hT_prev_ac)]
                last = tr(PS[:, 512 * bk + 128 * (c % 4):512 * bk + 128 * (c % 4) + 128],
                          h2[:, 1024 * par + 128 * c:1024 * par + 128 * (c + 1)],
                          waits=w2)
                if c == 3 or c == 7:
                    dst = hT[:, 0:4096].rearrange("p (c t) -> p c t", t=512)[
                        :, 4 * (c // 4):4 * (c // 4) + 4, 128 * i:128 * (i + 1)]
                    e = act(dst, PS[:, 512 * bk:512 * (bk + 1)].rearrange(
                                "p (c t) -> p c t", t=128),
                            AF.Copy, waits=[("sPE", last["sPE"])])
                    ev_done[bk] = e["sAC"]
            hload_seen[par] = last["sPE"]
        hT_ready = V("sAC")
        p0_step(3)

        for c in range(8):
            w = []
            if c == 0:
                w = [("sAC", hT_ready)]
                if s == 0:
                    w.append(qcv["dkv"])
                if ckv_free:
                    w.append(("sDV", ckv_free))
            mm(PS[:, 2048:2560], wdkv_sb[:, 128 * c:128 * (c + 1)],
               hT[:, 512 * c:512 * (c + 1)], waits=w, start=(c == 0), stop=(c == 7))
        pe_ckv = V("sPE")
        for c in range(8):
            mm(PS[:, 2560:3072], wdq_sb[:, 128 * c:128 * (c + 1)],
               hT[:, 512 * c:512 * (c + 1)],
               waits=([qcv["dq"]] if s == 0 and c == 0 else []),
               start=(c == 0), stop=(c == 7))
        pe_cq = V("sPE")
        e1 = act(cq_buf[:, 0:512], PS[:, 2048:2560], AF.Identity, bias=bdkv_sb[:, 0:1],
                 waits=[("sPE", pe_ckv)])
        ckv_sb = cq_buf[:, 0:512]
        e2 = act(cq_buf[:, 512:1024], PS[:, 2560:3072], AF.Identity, bias=bdq_sb[:, 0:1],
                 waits=[("sPE", pe_cq)] + ([("qCw", cw_cnt)] if cw_cnt else []))
        ckv_ready = e2["sAC"]
        cw = dma("sync", cq_scr[:, tau:tau + 512], cq_buf[:, 512:1024],
                 sem="qCw", waits=[("sAC", e2["sAC"])])
        cw_cnt = cw["qCw"]
        p0_step(3)

        for blk, bk in enumerate((6, 7, 0)):
            for c in range(8):
                w = []
                if c == 0:
                    if blk == 0 and rope_done:
                        w = [("sDV", rope_done)]
                    if blk == 0 and s == 0:
                        w.append(qcv["kr"])
                    if blk == 2:
                        w = [("sAC", ev_done[0])]
                mm(PS[:, 512 * bk:512 * (bk + 1)],
                   wkr_sb[:, 384 * c + 128 * blk:384 * c + 128 * (blk + 1)],
                   hT[:, 512 * c:512 * (c + 1)], waits=w,
                   start=(c == 0), stop=(c == 7))
        pe_krot = V("sPE")
        c0c = Psb[:, 2048:2560]
        e4 = act(c0s_sb[:, :], PS[:, 512 * 7:512 * 8], AF.Identity, bias=bkr_sb[:, 1:2],
                 waits=[("sPE", pe_krot - 8)])
        e5 = act(c0c, PS[:, 0:512], AF.Identity, bias=bkr_sb[:, 2:3],
                 waits=[("sPE", pe_krot)] + ([("qKb", kb_cnt)] if kb_cnt else []))
        kb1 = dma("sync", rotk_scr[128:256, tau:tau + 512], c0c,
                  sem="qKb", waits=[("sAC", e5["sAC"])])
        e3b = act(c0b_sb[:, :], PS[:, 512 * 6:512 * 7], AF.Identity, bias=bkr_sb[:, 0:1],
                  waits=[("sAC", e4["sAC"])] + ([("qKb", kb_cnt)] if kb_cnt else []))
        e6 = tt(c0b_sb[:, :], c0b_sb[:, :], cos_sb[:, :], ALU.mult,
                waits=[("sAC", e3b["sAC"]), ("qT", tbl_val)])
        e7 = tt(c0s_sb[:, :], c0s_sb[:, :], sin_sb[:, :], ALU.mult,
                waits=[("sDV", e6["sDV"])])
        e8 = tt(c0b_sb[:, :], c0b_sb[:, :], c0s_sb[:, :], ALU.add,
                waits=[("sDV", e7["sDV"])])
        rope_done = e8["sDV"]
        tbl_free = rope_done
        kb2 = dma("sync", rotk_scr[0:128, tau:tau + 512], c0b_sb[:, :],
                  sem="qKb", waits=[("sDV", e8["sDV"])])
        kb_cnt = kb2["qKb"]
        p0_step(3)

        kn_evac = [0, 0]
        for p in range(NP):
            bk = 1 + p % 2
            w = []
            if p < 2:
                w = [("sAC", ckv_ready), ("sAC", ev_done[bk])]
                if s == 0:
                    w.append(qcv["kcat"])
            else:
                w = [("sAC", kn_evac[p % 2])]
            m = mm(PS[:, 512 * bk:512 * (bk + 1)],
                   wkcat_sb[:, 128 * p:128 * (p + 1)], ckv_sb, waits=w)
            e = act(Ksb[:, S * p + tau:S * p + tau + 512],
                    PS[:, 512 * bk:512 * (bk + 1)],
                    AF.Identity, bias=bkc_sb[:, p:p + 1], waits=[("sPE", m["sPE"])])
            kn_evac[p % 2] = e["sAC"]
        last_knope_evac = max(kn_evac)
        p0_step(3)

        v_evac = [0, 0]
        for i in range(4):
            for half in range(2):
                k = 2 * i + half
                bk = 3 + k % 2
                if k == 0:
                    w = [("sAC", ev_done[3]), ("sAC", ckv_ready), ("qBB", buv["qBB"])]
                    if s == 0:
                        w.append(("sGP", g0))
                        w.append(qcv["uv"])
                elif k == 1:
                    w = [("sAC", e1["sAC"])]
                else:
                    w = [("sDV", v_evac[k % 2])]
                m = mm(PS[:, 512 * bk:512 * (bk + 1)],
                       ckv_sb[:, 128 * i:128 * (i + 1)],
                       wuv_sb[:, 512 * half:512 * (half + 1)], waits=w)
                vdst = Vsb[:, 0:NKC * 1040].rearrange(
                    "p (k h c) -> p k h c", h=16, c=65)[
                    :, 4 * s + i, 8 * half:8 * (half + 1), 0:64]
                e = tt(vdst,
                       PS[:, 512 * bk:512 * (bk + 1)].rearrange(
                           "p (h c) -> p h c", c=64),
                       bbuf[:, 512 * half:512 * (half + 1)].rearrange(
                           "p (h c) -> p h c", c=64),
                       ALU.add, waits=[("sPE", m["sPE"])])
                v_evac[k % 2] = e["sDV"]
        ckv_free = max(v_evac)
        hT_prev_consumed = V("sDV")
        hT_prev_ac = V("sAC")
        last_pa_pe = V("sPE")
        p0_step(3)

    while p0q:
        p0_step(4)
    wsc_cnt = V("qWsc")
    phase0_dv = V("sDV")

    for p in range(NP):
        for r0, hh in ((48, 2 * p), (112, 2 * p + 1)):
            src = rotk_scr.rearrange("(b j s) c -> b j s c", b=2, j=8)[
                :, :, hh:hh + 1, :].rearrange("b j s c -> (b j s) c")
            dma("sync", Ksb[r0:r0 + 16, S * p:S * (p + 1)], src, sem="qR",
                waits=[("qKb", kb_cnt), ("sAC", last_knope_evac)])
    krl_total = V("qR")

    qb_state = dict(cq_free=[last_pa_pe, last_pa_pe], rot_done=rope_done,
                    qb_bank_free=max(rope_done, phase0_dv), nope_evacs=[0] * NP,
                    rot_kb=kb_cnt, rq_w1=0)
    qb_ready = {}

    def qbuild_gen(qt, pair_range, first_half):
        tau = 512 * qt
        par = qt % 2
        if first_half:
            ld = dma("sync", cq_buf[:, 512 * par:512 * (par + 1)],
                     cq_scr[:, tau:tau + 512], sem=qC[par],
                     waits=[("qCw", cw_cnt), ("sPE", qb_state["cq_free"][par])])
            qb_state["cq_ld"] = ld[qC[par]]
            tl = dma("sync", cos_sb[:, :], dr["cosT"][:, tau:tau + 512], sem="qT",
                     waits=[("sDV", qb_state["rot_done"])])
            tl2 = dma("sync", sin_sb[:, :], dr["sinT"][:, tau:tau + 512], sem="qT")
            qb_state["tbl"] = tl2["qT"]
            yield
            rot_evacs = {}
            for blk in (2, 0, 1):
                m = mm(PS[:, 3072:3584], wqr_sb[:, 128 * blk:128 * (blk + 1)],
                       cq_buf[:, 512 * par:512 * (par + 1)],
                       waits=([(qC[par], qb_state["cq_ld"])] if blk == 2 else []) +
                             [("sDV", qb_state["qb_bank_free"])])
                dstl = (c0b_sb, c0s_sb, c0s_sb)[blk]
                ew = [("sPE", m["sPE"])]
                if blk == 2:
                    ew.append(("qKb", qb_state["rq_w1"]))
                    if qb_state["rot_done"]:
                        ew.append(("sDV", qb_state["rot_done"]))
                if blk == 0:
                    ew.append(("qKb", qb_state["rot_kb"]))
                if blk == 1:
                    ew.append(("qKb", qb_state["rq_w1"]))
                e = ts(dstl[:, :], PS[:, 3072:3584], bqr_sb[:, blk:blk + 1], ALU.add,
                       waits=ew)
                rot_evacs[blk] = e["sDV"]
                qb_state["qb_bank_free"] = max(qb_state["qb_bank_free"], e["sDV"])
                if blk == 2:
                    w1 = dma("sync", rotq_scr[128:256, 512 * par:512 * (par + 1)],
                             c0s_sb[:, :], sem="qKb", waits=[("sDV", e["sDV"])])
                    qb_state["rq_w1"] = w1["qKb"]
                yield
            e6 = tt(c0b_sb[:, :], c0b_sb[:, :], cos_sb[:, :], ALU.mult,
                    waits=[("sDV", rot_evacs[0]), ("qT", qb_state["tbl"])])
            yield
            e7 = tt(c0s_sb[:, :], c0s_sb[:, :], sin_sb[:, :], ALU.mult,
                    waits=[("sDV", e6["sDV"])])
            yield
            e8 = tt(c0b_sb[:, :], c0b_sb[:, :], c0s_sb[:, :], ALU.add,
                    waits=[("sDV", e7["sDV"])])
            qb_state["rot_done"] = e8["sDV"]
            w2 = dma("sync", rotq_scr[0:128, 512 * par:512 * (par + 1)],
                     c0b_sb[:, :], sem="qKb", waits=[("sDV", e8["sDV"])])
            qb_state["rq_w"] = w2["qKb"]
            qb_state["rot_kb"] = w2["qKb"]
            yield
        for p in pair_range:
            m = mm(PS[:, 3072:3584], wq_sb[:, 128 * p:128 * (p + 1)],
                   cq_buf[:, 512 * par:512 * (par + 1)],
                   waits=[(qC[par], qb_state["cq_ld"]),
                          ("sDV", qb_state["qb_bank_free"])])
            ew = [("sPE", m["sPE"])]
            if qt > 0:
                ew.append(("sPE", sbd[((qt - 1) * NP + p) * NKC + NKC - 1]))
            e = ts(Qsb[:, 512 * p:512 * (p + 1)], PS[:, 3072:3584],
                   bqc_sb[:, p:p + 1], ALU.add, waits=ew)
            qb_state["qb_bank_free"] = e["sDV"]
            qb_state["nope_evacs"][p] = e["sDV"]
            for r0, hh in ((48, 2 * p), (112, 2 * p + 1)):
                src2 = rotq_scr[:, 512 * par:512 * (par + 1)].rearrange(
                    "(b j s) c -> b j s c", b=2, j=8)[:, :, hh:hh + 1, :].rearrange(
                    "b j s c -> (b j s) c")
                dma("sync", Qsb[r0:r0 + 16, 512 * p:512 * (p + 1)], src2,
                    sem="qRp%d" % p,
                    waits=[("qKb", qb_state["rq_w"]), ("sDV", e["sDV"])])
            qb_ready[(qt, p)] = (V("qRp%d" % p), e["sDV"])
            yield
        if pair_range[-1] == NP - 1:
            qb_state["cq_free"][par] = V("sPE")
        qb_state["done_qR"] = V("qR")
        qb_state["done_dv"] = V("sDV")

    NGT = NQT * NP * NKC
    sbd = [0] * (NGT + 1)
    expd = [0] * (NGT + 1)
    denb = [0] * (NGT + 1)
    dnev = [0] * (NQT * NP + 1)
    oune = [0] * (NQT * NP + 1)
    norm_ = [0] * (NQT * NP + 1)
    bcf_ = [0] * (NQT * NP + 1)
    rdw = [0] * (NQT * NP + 1)

    for _ in qbuild_gen(0, list(range(NP)), True):
        pass

    tail_state = dict(wbuf_free=[0, 0, 0, 0], bbuf_free=0, h2_free=[0, 0],
                      h2_qh=[0, 0], tA_q=0, bank7_free=0, x1T_free=[0, 0],
                      t1_mm_done=0, x1w=0, bo_ld=0, bf_ld=0, xT_evac=0,
                      ln_out=0, lora_done=0, sq_free=0, wpre=None)
    bg_tail = []
    bg_qb = []
    bg_ep = []
    gctr = [0]

    for qt in range(NQT):
        qR_needed = qb_state["done_qR"]
        qb_dv = qb_state["done_dv"]
        base_pair = qt * NP
        def ep_start(PIDX2):
            lastG = PIDX2 * NKC + NKC - 1
            op_ = 512 * (PIDX2 % 2)
            cp(oun_sb[0:64, op_:op_ + 512], PS[0:64, 2048:2560],
               waits=[("sPE", denb[lastG])] +
                     ([("qOD", norm_[PIDX2 - 2])] if PIDX2 > 1 else []))
            e = cp(oun_sb[64:128, op_:op_ + 512], PS[0:64, 2560:3072])
            oune[PIDX2] = e["sDV"]
            c1 = act(bcast_sb[0:1, 0:512], PS[64:65, 2048:2560], AF.Copy,
                     waits=[("sPE", denb[lastG])] +
                           ([("sDV", bcf_[PIDX2 - 1])] if PIDX2 > 0 else []))
            c2 = act(bcast_sb[32:33, 0:512], PS[64:65, 2560:3072], AF.Copy)
            dnev[PIDX2] = c2["sAC"]
            return e["sDV"], c2["sAC"]

        def ep_chain_gen(PIDX2, e_dv, c2_dv):
            op_ = 512 * (PIDX2 % 2)
            oup = oun_sb[:, op_:op_ + 512]
            Wg = (PIDX2 + 1) * NKC + 1
            def ph(x):
                return Wg + max(1, (x * NKC) // 32)
            while gctr[0] < ph(2):
                yield "spin"
            wdma0 = dma("gpsimd", rden[PIDX2:PIDX2 + 1, 0:512], bcast_sb[0:1, 0:512],
                        sem="qBC", waits=[("sAC", c2_dv)] +
                        ([("qOD", norm_[PIDX2 - 1])] if PIDX2 > 0 else []))
            wdma = dma("gpsimd", rden[PIDX2:PIDX2 + 1, 512:1024],
                       bcast_sb[32:33, 0:512], sem="qBC")
            rdw[PIDX2] = wdma["qBC"]
            while gctr[0] < ph(4):
                yield "spin"
            bl = dma("gpsimd", bcast_sb[0:64, :],
                     rden[PIDX2:PIDX2 + 1, 0:512].partition_broadcast(64),
                     sem="qBC", waits=[("qBC", wdma["qBC"])])
            bl2 = dma("gpsimd", bcast_sb[64:128, :],
                      rden[PIDX2:PIDX2 + 1, 512:1024].partition_broadcast(64),
                      sem="qBC")
            while gctr[0] < ph(7):
                yield "spin"
            rr = recip(bcast_sb[:, :], bcast_sb[:, :],
                       waits=[("qBC", bl2["qBC"]), ("sDV", e_dv)])
            nn = tt(oup, oup, bcast_sb[:, :], ALU.mult,
                    waits=[("sDV", rr["sDV"])])
            bcf_[PIDX2] = nn["sDV"]
            while gctr[0] < ph(12):
                yield "spin"
            od = dma("scalar", o_dram[PIDX2 * 128:(PIDX2 + 1) * 128, :],
                     oup, sem="qOD", waits=[("sDV", nn["sDV"])])
            norm_[PIDX2] = od["qOD"]

        def emit_S(p, kc):
            G = (qt * NP + p) * NKC + kc
            pi = G % 2
            w = []
            if G >= 2:
                w.append(("sAC", expd[G - 2]))
            if kc == 0:
                qr_, dv_ = qb_ready[(qt, p)]
                w += [("qRp%d" % p, qr_), ("sDV", dv_)]
                if qt == 0:
                    w.append(("qR", krl_total))
            mm(PS[:, 1024 * pi:1024 * pi + 512],
               Ksb[0:64, S * p + 128 * kc:S * p + 128 * (kc + 1)],
               Qsb[0:64, 512 * p:512 * (p + 1)], waits=w, tp=(0, 0))
            r = mm(PS[:, 1024 * pi + 512:1024 * pi + 1024],
                   Ksb[64:128, S * p + 128 * kc:S * p + 128 * (kc + 1)],
                   Qsb[64:128, 512 * p:512 * (p + 1)], tp=(64, 0))
            sbd[G] = r["sPE"]

        def emit_exp(p, kc):
            G = (qt * NP + p) * NKC + kc
            pi = G % 2
            p3 = G % 4
            w = [("sPE", sbd[G])]
            if G >= 4:
                w.append(("sPE", denb[G - 4]))
            r = act(PSB4[p3], PS[:, 1024 * pi:1024 * (pi + 1)],
                    AF.Exp, waits=w, scale=0.125)
            expd[G] = r["sAC"]

        def emit_pv(p, kc):
            G = (qt * NP + p) * NKC + kc
            PIDX = qt * NP + p
            p3 = G % 4
            first = (kc == 0)
            lastk = (kc == NKC - 1)
            w = [("sAC", expd[G])]
            if first and PIDX > 0:
                w += [("sDV", oune[PIDX - 1]), ("sAC", dnev[PIDX - 1])]
            mm(PS[0:65, 2048:2560],
               Vsb[:, 1040 * kc + 65 * (2 * p):1040 * kc + 65 * (2 * p) + 65],
               PSB4[p3][:, 0:512], waits=w,
               start=first, stop=lastk, tp=(0, 0))
            r = mm(PS[0:65, 2560:3072],
                   Vsb[:, 1040 * kc + 65 * (2 * p + 1):1040 * kc + 65 * (2 * p + 1) + 65],
                   PSB4[p3][:, 512:1024],
                   start=first, stop=lastk, tp=(0, 0))
            denb[G] = r["sPE"]

        exp_total = (qt + 1) * NP * NKC
        tau = 512 * qt

        def ln_chain_gen(prev_dv, par):
            sqd = h2[:, 1024 * par:1024 * (par + 1)]
            s1 = red(st1[:, :], tA[:, :], ALU.add, waits=[("sDV", prev_dv)])
            sq = tt(sqd, tA[:, :], tA[:, :], ALU.mult,
                    waits=[("sDV", prev_dv)])
            yield
            s2 = red(st2[:, :], sqd, ALU.add, waits=[("sDV", sq["sDV"])])
            tail_state["sq_free"] = s2["sDV"]
            m1 = ts(stm[:, :], st1[:, :], float(np.float32(1.0 / 1024.0)), ALU.mult,
                    waits=[("sDV", s1["sDV"])])
            yield
            v0 = ts(stv[:, :], st2[:, :], float(np.float32(1.0 / 1024.0)), ALU.mult,
                    waits=[("sDV", s2["sDV"])])
            ms = tt(st1[:, :], stm[:, :], stm[:, :], ALU.mult,
                    waits=[("sDV", m1["sDV"])])
            yield
            vv = tt(stv[:, :], stv[:, :], st1[:, :], ALU.subtract,
                    waits=[("sDV", ms["sDV"]), ("sDV", v0["sDV"])])
            yield
            yield
            l1 = act(st2[:, :], stv[:, :], AF.Ln, bias=eps_sb[:, 0:1],
                     waits=[("sDV", vv["sDV"])])
            l2 = act(str_[:, :], st2[:, :], AF.Exp, scale=-0.5,
                     waits=[("sAC", l1["sAC"])])
            yield
            n1 = ts(tA[:, :], tA[:, :], stm[:, 0:1], ALU.subtract,
                    waits=[("sDV", vv["sDV"]), ("sAC", l2["sAC"])])
            yield
            n2 = ts(tA[:, :], tA[:, :], str_[:, 0:1], ALU.mult,
                    waits=[("sDV", n1["sDV"])])
            tail_state["ln_out"] = n2["sDV"]

        def tail_gen(qt=qt, tau=tau):
            st = tail_state
            BK7 = PS[:, 3584:4096]
            x1w_s = [0, 0, 0, 0]
            bb = dma("gpsimd", bbuf[:, :], dr["b_o"].partition_broadcast(128),
                     sem="qBB", waits=[("sDV", st["bbuf_free"])]
                     if st["bbuf_free"] else [("sDV", ckv_free)])
            st["bo_ld"] = bb["qBB"]
            oldv = {}
            lds = st["wpre"] if st["wpre"] else {}
            st["wpre"] = None
            for cp_ in range(4):
                if cp_ not in lds:
                    wl = [("sPE", st["wbuf_free"][cp_])] \
                        if st["wbuf_free"][cp_] else [("qWsc", wsc_cnt)]
                    lds[cp_] = dma("gpsimd", wbuf[:, 2048 * cp_:2048 * (cp_ + 1)],
                                   wo_p[:, 2048 * cp_:2048 * (cp_ + 1)],
                                   sem=qB[cp_], waits=wl)[qB[cp_]]
                    yield
            t2lds = {}
            for sub in range(4):
                par = sub % 2
                w0 = [("sDV", st["h2_free"][par])] if st["h2_free"][par] else []
                if st["h2_qh"][par]:
                    w0.append((qTH[par], st["h2_qh"][par]))
                if qt == 0 and sub < 2:
                    w0.append(("sPE", hload_seen[par]))
                hld = dma("sync", h2[:, 1024 * par:1024 * (par + 1)],
                          dr["h"][tau + 128 * sub:tau + 128 * (sub + 1), :],
                          sem=qTH[par], waits=w0)
                st["h2_qh"][par] = hld[qTH[par]]
                e1d = 0
                for half in range(2):
                    for c in range(8):
                        cp_ = c // 2
                        if sub == 0 and half == 0:
                            nidx = min(qt * NP + c + 1, NQT * NP - 1)
                            while not norm_[nidx]:
                                yield "spin"
                            w0 = [("qOD", norm_[qt * NP + c])]
                            if c == 0 and st["t1_mm_done"]:
                                w0.append(("sPE", st["t1_mm_done"]))
                            if c > 0:
                                w0.append(("qOL", oldv[c - 1]))
                            ol = dma("sync", osb[:, 512 * c:512 * (c + 1)],
                                     o_dram[(qt * NP + c) * 128:(qt * NP + c + 1) * 128, :],
                                     sem="qOL", waits=w0)
                            oldv[c] = ol["qOL"]
                        w2 = [(qB[cp_], lds[cp_])]
                        if sub == 0 and half == 0:
                            w2.append(("qOL", oldv[c]))
                        if c == 0:
                            w2 += [("sDV", st["bank7_free"])]
                        m = mm(BK7,
                               osb[:, 512 * c + 128 * sub:512 * c + 128 * (sub + 1)],
                               wbuf[:, 2048 * cp_ + 1024 * (c % 2) + 512 * half:
                                    2048 * cp_ + 1024 * (c % 2) + 512 * (half + 1)],
                               waits=w2, start=(c == 0), stop=(c == 7))
                        if c % 2 == 1:
                            if sub == 3 and half == 1:
                                st["wbuf_free"][cp_] = m["sPE"]
                                t2lds[cp_] = dma(
                                    "gpsimd", wbuf[:, 2048 * cp_:2048 * (cp_ + 1)],
                                    wf_p[:, 2048 * cp_:2048 * (cp_ + 1)],
                                    sem=qB[cp_],
                                    waits=[("sPE", m["sPE"])])[qB[cp_]]
                            yield
                    e1w = [("sPE", m["sPE"]), (qTH[par], st["h2_qh"][par])]
                    if st["tA_q"] and half == 0:
                        e1w.append(st["tA_q"])
                    if qt == 0 and sub == 0 and half == 0:
                        e1w.append(("sDV", phase0_dv))
                    e1 = tt(tA[:, 512 * half:512 * (half + 1)], BK7,
                            h2[:, 1024 * par + 512 * half:1024 * par + 512 * (half + 1)],
                            ALU.add, waits=e1w + ([("sDV", e1d)] if e1d else []))
                    e1d = e1["sDV"]
                    st["bank7_free"] = e1["sDV"]
                    yield
                if sub == 3:
                    st["t1_mm_done"] = m["sPE"]
                e2 = tt(tA[:, :], tA[:, :], bbuf[:, :], ALU.add,
                        waits=[("sDV", e1d), ("qBB", st["bo_ld"])])
                st["bbuf_free"] = e2["sDV"]
                yield
                for _ in ln_chain_gen(e2["sDV"], par):
                    yield
                st["h2_free"][par] = st["sq_free"]
                xw = dma("gpsimd", x1_scr[128 * sub:128 * (sub + 1), :], tA[:, :],
                         sem="qX", waits=[("sDV", st["ln_out"])])
                st["tA_q"] = ("qX", xw["qX"])
                x1w_s[sub] = xw["qX"]
                yield
            bb = dma("gpsimd", bbuf[:, :], dr["b_ffn"].partition_broadcast(128),
                     sem="qBB", waits=[("sDV", st["bbuf_free"])])
            st["bf_ld"] = bb["qBB"]

            def issue_xld(sub2):
                par2 = sub2 % 2
                w = [("qX", x1w_s[sub2])]
                if st["h2_free"][par2]:
                    w.append(("sDV", st["h2_free"][par2]))
                if st["h2_qh"][par2]:
                    w.append((qTH[par2], st["h2_qh"][par2]))
                xl = dma("sync", h2[:, 1024 * par2:1024 * (par2 + 1)],
                         x1_scr[128 * sub2:128 * (sub2 + 1), :], sem=qTH[par2],
                         waits=w)
                st["h2_qh"][par2] = xl[qTH[par2]]
                return xl[qTH[par2]]

            xlds = {0: issue_xld(0), 1: issue_xld(1)}
            for sub in range(4):
                par = sub % 2
                xldv = xlds[sub]
                for rnd in range(2):
                    for c4 in range(4):
                        c = 4 * rnd + c4
                        w2 = []
                        if c4 == 0:
                            w2 = [(qTH[par], xldv), ("sDV", st["bank7_free"])]
                            if st["x1T_free"][par]:
                                w2.append(("sPE", st["x1T_free"][par]))
                        last = tr(PS[:, 3584 + 128 * c4:3584 + 128 * (c4 + 1)],
                                  h2[:, 1024 * par + 128 * c:1024 * par + 128 * (c + 1)],
                                  waits=w2)
                    dst = x1T2[:, 0:2048].rearrange("p (g c t) -> p g c t", g=2, t=128)[
                        :, par, 4 * rnd:4 * rnd + 4, :]
                    e = cp(dst, BK7.rearrange("p (c t) -> p c t", t=128),
                           waits=[("sPE", last["sPE"])])
                    st["bank7_free"] = e["sDV"]
                    yield
                st["xT_evac"] = e["sDV"]
                for c in range(8):
                    w2 = [("sDV", st["xT_evac"])] if c == 0 else []
                    m = mm(PS[0:16, 3584:3712], a_sb[:, 16 * c:16 * (c + 1)],
                           x1T2[:, 1024 * par + 128 * c:1024 * par + 128 * (c + 1)],
                           waits=w2, start=(c == 0), stop=(c == 7))
                zw = [("sPE", m["sPE"])]
                if st.get("lora_done"):
                    zw.append(("sPE", st["lora_done"]))
                ze = cp(zb_sb[0:16, 0:128], PS[0:16, 3584:3712], waits=zw)
                st["bank7_free"] = ze["sDV"]
                yield
                e1d = 0
                for half in range(2):
                    for c in range(8):
                        cp_ = c // 2
                        w2 = [(qB[cp_], t2lds[cp_])]
                        if c == 0:
                            w2.append(("sDV", st["bank7_free"]))
                        m = mm(BK7,
                               x1T2[:, 1024 * par + 128 * c:1024 * par + 128 * (c + 1)],
                               wbuf[:, 2048 * cp_ + 1024 * (c % 2) + 512 * half:
                                    2048 * cp_ + 1024 * (c % 2) + 512 * (half + 1)],
                               waits=w2, start=(c == 0), stop=False)
                        if c % 2 == 1:
                            if sub == 3 and half == 1 and qt + 1 < NQT:
                                st["wbuf_free"][cp_] = m["sPE"]
                                if st["wpre"] is None:
                                    st["wpre"] = {}
                                st["wpre"][cp_] = dma(
                                    "gpsimd", wbuf[:, 2048 * cp_:2048 * (cp_ + 1)],
                                    wo_p[:, 2048 * cp_:2048 * (cp_ + 1)],
                                    sem=qB[cp_],
                                    waits=[("sPE", m["sPE"])])[qB[cp_]]
                            yield
                    m = mm(BK7, zb_sb[0:32, 0:128],
                           bpad_sb[0:32, 512 * half:512 * (half + 1)],
                           start=False, stop=True)
                    st["lora_done"] = m["sPE"]
                    e1w = [("sPE", m["sPE"]), ("qBB", st["bf_ld"]),
                           (qTH[par], st["h2_qh"][par])]
                    if st["tA_q"] and half == 0:
                        e1w.append(st["tA_q"])
                    e1 = tt(tA[:, 512 * half:512 * (half + 1)], BK7,
                            h2[:, 1024 * par + 512 * half:1024 * par + 512 * (half + 1)],
                            ALU.add, waits=e1w + ([("sDV", e1d)] if e1d else []))
                    e1d = e1["sDV"]
                    st["bank7_free"] = e1["sDV"]
                    yield
                st["x1T_free"][par] = m["sPE"]
                e2 = tt(tA[:, :], tA[:, :], bbuf[:, :], ALU.add,
                        waits=[("sDV", e1d), ("qBB", st["bf_ld"])])
                st["bbuf_free"] = e2["sDV"]
                yield
                for _ in ln_chain_gen(e2["sDV"], par):
                    yield
                st["h2_free"][par] = st["sq_free"]
                if sub + 2 <= 3:
                    xlds[sub + 2] = issue_xld(sub + 2)
                od = dma("scalar", out_d[tau + 128 * sub:tau + 128 * (sub + 1), :],
                         tA[:, :], sem="qO", waits=[("sDV", st["ln_out"])])
                st["tA_q"] = ("qO", od["qO"])
                yield

        bg_tail.append(tail_gen())
        NGQ = NP * NKC
        for g in range(NGQ):
            p, kc = g // NKC, g % NKC
            gctr[0] = qt * NGQ + g
            S_.gpos = (qt, p, kc)
            emit_S(p, kc)
            emit_exp(p, kc)
            if g >= 2:
                emit_pv((g - 2) // NKC, (g - 2) % NKC)
            if kc == 1 and qt * NP + p > 0:
                P2 = qt * NP + p - 1
                e_dv, c2_dv = ep_start(P2)
                bg_ep.append(ep_chain_gen(P2, e_dv, c2_dv))
            if bg_ep:
                try:
                    next(bg_ep[0])
                except StopIteration:
                    bg_ep.pop(0)
            if kc == NKC // 2 and qt + 1 < NQT and p in (3, 7):
                if p == 3:
                    bg_qb.append(qbuild_gen(qt + 1, [0, 1, 2], True))
                else:
                    bg_qb.append(qbuild_gen(qt + 1, [3, 4, 5, 6], False))
            for q_ in (() if kc in (0, 1, 2) else (bg_qb, bg_tail)):
                if not q_:
                    continue
                for _spin in range(64):
                    try:
                        v = next(q_[0])
                    except StopIteration:
                        q_.pop(0)
                        break
                    if v != "spin":
                        break
        emit_pv(NP - 1, NKC - 2)
        emit_pv(NP - 1, NKC - 1)
        if qt == NQT - 1:
            P2 = qt * NP + NP - 1
            e_dv, c2_dv = ep_start(P2)
            bg_ep.append(ep_chain_gen(P2, e_dv, c2_dv))
        if qt + 1 < NQT:
            bg_qb.append(qbuild_gen(qt + 1, [7], False))
        while bg_qb:
            try:
                next(bg_qb[0])
            except StopIteration:
                bg_qb.pop(0)

        if qt == NQT - 1:
            for p_ in range(NP):
                pass
    gctr[0] = 1 << 60
    for q_ in (bg_ep, bg_qb, bg_tail):
        while q_:
            try:
                next(q_[0])
            except StopIteration:
                q_.pop(0)
    wonly("gpsimd", [("qO", 16 * 4 * NQT)])

    from contextlib import ExitStack
    with ExitStack() as ctx:
        sems = {}
        for s in S_.cnt:
            sems[s] = ctx.enter_context(nc.semaphore(s))
        block = ctx.enter_context(nc.Block())

        def mk(eng_name):
            def emit(eng):
                for fn, waits, incs in S_.events[eng_name]:
                    for s, v in waits:
                        if v > 0:
                            eng.wait_ge(sems[s], v)
                    if fn is None:
                        continue
                    ins = fn(eng)
                    for s, a in incs:
                        ins = ins.then_inc(sems[s], a)
            return emit
        block.sync(mk("sync"))
        block.gpsimd(mk("gpsimd"))
        block.tensor(mk("tensor"))
        block.scalar(mk("scalar"))
        block.vector(mk("vector"))
    return nc


def kernel(**inputs):
    h_all = np.asarray(inputs["L0_attn_h"], np.float32)
    B, S, _ = h_all.shape
    const = _host_constants(inputs, S)
    nc = build_program(S)
    shared = {k: np.ascontiguousarray(v) for k, v in const.items()
              if k not in ("W_dkv", "W_dq", "W_uv", "W_o", "W_ffn", "a_l")}
    shared.update({
        "W_dkv": const["W_dkv"], "W_dq": const["W_dq"], "W_uv": const["W_uv"],
        "W_o": const["W_o"], "W_ffn": const["W_ffn"], "a_l": const["a_l"],
    })
    in_maps = []
    for c in range(B):
        m = dict(shared)
        m["h"] = np.ascontiguousarray(h_all[c])
        in_maps.append(m)
    res = run_bass_kernel_spmd(nc, in_maps, core_ids=list(range(B)))
    out = np.stack([np.asarray(r["out"]) for r in res.results], axis=0)
    return out.astype(np.float32)


# version 49
# speedup vs baseline: 1.0220x; 1.0057x over previous
import numpy as np
import ml_dtypes

import concourse.bass as bass
import concourse.mybir as mybir
from concourse.bass_utils import run_bass_kernel_spmd

F32 = mybir.dt.float32
BF16 = mybir.dt.bfloat16
AF = mybir.ActivationFunctionType
ALU = mybir.AluOpType
AX = mybir.AxisListType
bfdt = ml_dtypes.bfloat16

D = 1024
H = 16
NP = H // 2
RANK = 16

ENGS = ("sync", "gpsimd", "tensor", "scalar", "vector")


class Sched:
    def __init__(self):
        self.events = {e: [] for e in ENGS}
        self.cnt = {}

    def sem(self, name):
        self.cnt.setdefault(name, 0)

    def ev(self, eng, fn, waits=(), incs=()):
        out = {}
        for s, a in incs:
            self.cnt[s] += a
            out[s] = self.cnt[s]
        self.events[eng].append((fn, list(waits), list(incs)))
        return out

    def val(self, s):
        return self.cnt[s]


def _host_constants(inp, S):
    f32 = np.float32
    W_uq = np.asarray(inp["L0_attn_W_uq"], f32)
    W_uk = np.asarray(inp["L0_attn_W_uk"], f32)
    b_uq = np.asarray(inp["L0_attn_b_uq"], f32)
    b_uk = np.asarray(inp["L0_attn_b_uk"], f32)
    W_qr = np.asarray(inp["L0_attn_W_qr"], f32)
    W_kr = np.asarray(inp["L0_attn_W_kr"], f32)
    b_qr = np.asarray(inp["L0_attn_b_qr"], f32)
    b_kr = np.asarray(inp["L0_attn_b_kr"], f32)

    def cat_nope(W, b):
        Wc = np.zeros((W.shape[0], 128 * NP), f32)
        bc = np.zeros((128, NP), f32)
        for p in range(NP):
            Wc[:, 128 * p + 0:128 * p + 48] = W[:, 96 * p:96 * p + 48]
            Wc[:, 128 * p + 64:128 * p + 112] = W[:, 96 * p + 48:96 * p + 96]
            bc[0:48, p] = b[96 * p:96 * p + 48]
            bc[64:112, p] = b[96 * p + 48:96 * p + 96]
        return Wc, bc

    Wq_cat, bq_cat = cat_nope(W_uq, b_uq)
    Wk_cat, bk_cat = cat_nope(W_uk, b_uk)

    def jm(Wr, br):
        nin = Wr.shape[0]
        Wj = np.zeros((nin, 384), f32)
        bj = np.zeros((128, 3), f32)
        for h in range(16):
            for j in range(8):
                Wj[:, 0 * 128 + 16 * j + h] = Wr[:, 16 * h + j]
                bj[16 * j + h, 0] = br[16 * h + j]
                js = j + 4 if j < 4 else j - 4
                Wj[:, 1 * 128 + 16 * j + h] = Wr[:, 16 * h + js]
                bj[16 * j + h, 1] = br[16 * h + js]
                Wj[:, 2 * 128 + 16 * j + h] = Wr[:, 16 * h + 8 + j]
                bj[16 * j + h, 2] = br[16 * h + 8 + j]
        return Wj, bj

    Wqr_jm, bqr_jm = jm(W_qr, b_qr)
    Wkr_jm, bkr_jm = jm(W_kr, b_kr)

    inv_freq = (1.0 / (10000.0 ** (np.arange(0, 8, 2, dtype=f32) / np.float32(8.0)))).astype(f32)
    t = (np.arange(S, dtype=f32) / np.float32(40.0)).astype(f32)
    freqs = (t[:, None] * inv_freq[None, :]).astype(f32)
    cos4 = np.cos(freqs).astype(f32)
    sin4 = np.sin(freqs).astype(f32)
    cosT = np.zeros((128, S), f32)
    sinT = np.zeros((128, S), f32)
    for h in range(16):
        for j in range(8):
            r = 16 * j + h
            cosT[r] = cos4[:, j % 4]
            sinT[r] = (-sin4[:, j % 4]) if j < 4 else sin4[:, j % 4]

    bpad = np.zeros((32, 1024), f32)
    bpad[0:16] = np.asarray(inp["L0_ffn_b"], f32)
    ones32 = np.zeros((32, 128), f32)
    ones32[0] = 1.0

    return dict(
        Wq_cat=Wq_cat, bq_cat=bq_cat, Wk_cat=Wk_cat, bk_cat=bk_cat,
        Wqr_jm=Wqr_jm, bqr_jm=bqr_jm, Wkr_jm=Wkr_jm, bkr_jm=bkr_jm,
        cosT=cosT.astype(bfdt), sinT=sinT.astype(bfdt), bpad=bpad, ones32=ones32,
        ident=np.eye(128, dtype=f32),
        bdkv=np.asarray(inp["L0_attn_b_dkv"], f32).reshape(128, 1),
        bdq=np.asarray(inp["L0_attn_b_dq"], f32).reshape(128, 1),
        b_uv=np.asarray(inp["L0_attn_b_uv"], f32).reshape(1, 1024),
        b_o=np.asarray(inp["L0_attn_b_o"], f32).reshape(1, 1024),
        b_ffn=np.asarray(inp["L0_ffn_bias"], f32).reshape(1, 1024),
        W_dkv=np.asarray(inp["L0_attn_W_dkv"], f32),
        W_dq=np.asarray(inp["L0_attn_W_dq"], f32),
        W_uv=np.asarray(inp["L0_attn_W_uv"], f32),
        W_o=np.asarray(inp["L0_attn_W_o"], f32),
        W_ffn=np.asarray(inp["L0_ffn_kernel"], f32),
        a_l=np.asarray(inp["L0_ffn_a"], f32),
    )


def build_program(S):
    NQT = S // 512
    NKC = S // 128
    NSP = S // 512
    nc = bass.Bass("TRN2", target_bir_lowering=False)

    dr = {}
    def din(name, shape, dt=F32):
        dr[name] = nc.dram_tensor(name, list(shape), dt, kind="ExternalInput").ap()
    din("h", (S, D))
    din("W_dkv", (D, 128)); din("W_dq", (D, 128))
    din("Wq_cat", (128, 1024)); din("Wk_cat", (128, 1024))
    din("Wqr_jm", (128, 384)); din("Wkr_jm", (D, 384))
    din("W_uv", (128, 1024)); din("W_o", (D, D)); din("W_ffn", (D, D))
    din("a_l", (D, RANK)); din("bpad", (32, 1024))
    din("cosT", (128, S), BF16); din("sinT", (128, S), BF16)
    din("ident", (128, 128)); din("ones32", (32, 128))
    din("bdkv", (128, 1)); din("bdq", (128, 1))
    din("bq_cat", (128, NP)); din("bk_cat", (128, NP))
    din("bqr_jm", (128, 3)); din("bkr_jm", (128, 3))
    din("b_uv", (1, 1024)); din("b_o", (1, 1024)); din("b_ffn", (1, 1024))
    out_d = nc.dram_tensor("out", [S, D], F32, kind="ExternalOutput").ap()
    rden = nc.dram_tensor("rden", [NQT * NP, 1024], F32, kind="Internal").ap()
    o_dram = nc.dram_tensor("o_dram", [NQT * NP * 128, 512], BF16, kind="Internal").ap()
    rotk_scr = nc.dram_tensor("rotk_scr", [256, S], BF16, kind="Internal").ap()
    rotq_scr = nc.dram_tensor("rotq_scr", [256, 1024], BF16, kind="Internal").ap()
    cq_scr = nc.dram_tensor("cq_scr", [128, S], BF16, kind="Internal").ap()
    x1_scr = nc.dram_tensor("x1_scr", [512, D], F32, kind="Internal").ap()
    wo_p = nc.dram_tensor("wo_p", [128, 8 * D], BF16, kind="Internal").ap()
    wf_p = nc.dram_tensor("wf_p", [128, 8 * D], BF16, kind="Internal").ap()
    p0_scr = nc.dram_tensor("p0_scr", [1, 192], F32, kind="Internal").ap()

    sbt = nc.alloc_sbuf_tensor
    Ksb = sbt("Ksb", [128, NP * S], BF16)
    Vsb = sbt("Vsb", [128, NKC * 1040], BF16)
    wq_sb = sbt("wq_sb", [128, 1024], BF16)
    wqr_sb = sbt("wqr_sb", [128, 384], BF16)
    wkr_sb = sbt("wkr_sb", [128, 3072], BF16)
    a_sb = sbt("a_sb", [128, 128], BF16)
    bpad_sb = sbt("bpad_sb", [32, 1024], BF16)
    ident_sb = sbt("ident_sb", [128, 128], F32)
    ones1_sb = sbt("ones1_sb", [128, 1], BF16)
    bdkv_sb = sbt("bdkv_sb", [128, 1], F32)
    bdq_sb = sbt("bdq_sb", [128, 1], F32)
    bqc_sb = sbt("bqc_sb", [128, NP], F32)
    bkc_sb = sbt("bkc_sb", [128, NP], F32)
    bqr_sb = sbt("bqr_sb", [128, 3], F32)
    bkr_sb = sbt("bkr_sb", [128, 3], F32)
    eps_sb = sbt("eps_sb", [128, 1], F32)
    scl_sb = sbt("scl_sb", [128, 1], F32)
    rscl_sb = sbt("rscl_sb", [128, 1], F32)
    s32_sb = sbt("s32_sb", [32, 2], F32)
    gmax_sb = sbt("gmax_sb", [128, 9], F32)
    Qsb = sbt("Qsb", [128, NP * 512], BF16)
    _ = None
    Psb = sbt("Psb", [128, 3072], BF16)
    osb = sbt("osb", [128, NP * 512], BF16)
    cos_sb = sbt("cos_sb", [128, 512], BF16)
    sin_sb = sbt("sin_sb", [128, 512], BF16)
    cq_buf = sbt("cq_buf", [128, 1024], BF16)
    bcast_sb = sbt("bcast_sb", [128, 512], F32)
    c0b_sb = sbt("c0b_sb", [128, 512], BF16)
    c0s_sb = sbt("c0s_sb", [128, 512], BF16)
    wbuf = sbt("wbuf", [128, 8192], BF16)
    bbuf = sbt("bbuf", [128, 1024], BF16)
    h2 = sbt("h2", [128, 2048], F32)
    tA = sbt("tA", [128, 1024], F32)
    x1T2 = sbt("x1T2", [128, 2048], BF16)
    zb_sb = sbt("zb_sb", [32, 128], BF16)
    st1 = sbt("st1", [128, 1], F32)
    st2 = sbt("st2", [128, 1], F32)
    stm = sbt("stm", [128, 1], F32)
    stv = sbt("stv", [128, 1], F32)
    str_ = sbt("str_", [128, 1], F32)
    oun_sb = sbt("oun_sb", [128, 1024], BF16)
    assert nc.sbuf_bytes_remaining >= 0

    PS = nc.alloc_psum_tensor("PS", [128, 4096], F32)

    hT = Qsb
    wkcat_sb = osb[:, 0:1024]
    wdkv_sb = osb[:, 1024:2048]
    wdq_sb = osb[:, 2048:3072]
    wuv_sb = osb[:, 3072:4096]
    wbufF = (wbuf.ap() if hasattr(wbuf, "ap") else wbuf).bitcast(F32)
    wstage = (tA, wbufF[:, 0:1024])
    wstage_out = x1T2[:, 0:1024]
    qacc = x1T2[:, 1024:2048]
    qcmp = Psb[:, 0:1024]
    qsgn = Psb[:, 1024:2048]
    PSB4 = [Psb[:, 0:1024], Psb[:, 1024:2048], Psb[:, 2048:3072],
            wkr_sb[:, 0:1024]]
    qabs = wbufF[:, 1024:2048]

    S_ = Sched()
    for s in ("qSm", "qW0", "qW1", "qT", "qC0", "qC1", "qR", "qWsc", "qKb",
              "qCw", "qX", "qO", "qH0", "qH1", "qTH0", "qTH1", "qB0", "qB1", "qB2", "qB3", "qBB", "qBC", "qOD", "qOL",
              "qRp0", "qRp1", "qRp2", "qRp3", "qRp4", "qRp5", "qRp6", "qRp7", "qWF",
              "qCVa", "qCVb", "qCVc", "qCVd", "qCVe",
              "sPE", "sDV", "sAC", "sGP"):
        S_.sem(s)
    qH = ("qH0", "qH1")
    qW = ("qW0", "qW1")
    qB = ("qB0", "qB1", "qB2", "qB3")
    qTH = ("qTH0", "qTH1")
    qC = ("qC0", "qC1")

    def dma(q, out, in_, waits=(), sem="qSm", extra=()):
        return S_.ev(q, (lambda e, o=out, i=in_: e.dma_start(out=o, in_=i)),
                     waits, [(sem, 16)] + list(extra))

    def mm(out, lhsT, rhs, waits=(), start=True, stop=True, tp=(0, 0), skip=False,
           extra=()):
        return S_.ev("tensor",
                     (lambda e, o=out, l=lhsT, r=rhs, s1=start, s2=stop, t=tp, k=skip:
                      e.matmul(o, l, r, start=s1, stop=s2, tile_position=t,
                               skip_group_check=k)),
                     waits, [("sPE", 1)] + list(extra))

    def tr(out, in_, waits=(), extra=()):
        return S_.ev("tensor",
                     (lambda e, o=out, i=in_: e.transpose(o, i, ident_sb[:, :])),
                     waits, [("sPE", 1)] + list(extra))

    def act(out, in_, func, waits=(), scale=1.0, bias=0.0, extra=()):
        return S_.ev("scalar",
                     (lambda e, o=out, i=in_, f=func, sc=scale, b=bias:
                      e.activation(o, i, f, scale=sc, bias=b)),
                     waits, [("sAC", 1)] + list(extra))

    def tt(out, in0, in1, op, waits=(), extra=()):
        return S_.ev("vector",
                     (lambda e, o=out, x=in0, y=in1, p=op: e.tensor_tensor(o, x, y, op=p)),
                     waits, [("sDV", 1)] + list(extra))

    def ts(out, in0, s1, op0, waits=(), extra=()):
        return S_.ev("vector",
                     (lambda e, o=out, i=in0, a=s1, p=op0:
                      e.tensor_scalar(o, i, a, None, op0=p)),
                     waits, [("sDV", 1)] + list(extra))

    def ts2(out, in0, s1, op0, s2, op1, waits=(), extra=()):
        return S_.ev("vector",
                     (lambda e, o=out, i=in0, a=s1, p=op0, b=s2, q=op1:
                      e.tensor_scalar(o, i, a, b, op0=p, op1=q)),
                     waits, [("sDV", 1)] + list(extra))

    def red(out, in_, op, absval=False, waits=(), extra=()):
        return S_.ev("vector",
                     (lambda e, o=out, i=in_, p=op, av=absval:
                      e.tensor_reduce(o, i, axis=AX.X, op=p, apply_absolute_value=av)),
                     waits, [("sDV", 1)] + list(extra))

    def cp(out, in_, waits=(), extra=()):
        return S_.ev("vector", (lambda e, o=out, i=in_: e.tensor_copy(o, i)),
                     waits, [("sDV", 1)] + list(extra))

    def recip(out, in_, waits=(), extra=()):
        return S_.ev("vector", (lambda e, o=out, i=in_: e.reciprocal(o, i)),
                     waits, [("sDV", 1)] + list(extra))

    def mset(ap, v, waits=()):
        return S_.ev("gpsimd", (lambda e, a=ap, x=v: e.memset(a, x)),
                     waits, [("sGP", 1)])

    def wonly(eng, waits):
        S_.ev(eng, None, waits, [])

    V = S_.val

    mset(ones1_sb[:, :], 1.0)
    mset(Vsb[:, :], 1.0)
    mset(eps_sb[:, :], 1e-5)
    mset(s32_sb[:, :], 0.0)
    mset(zb_sb[:, :], 0.0)
    g0 = V("sGP")

    for nm, dst in (("ident", ident_sb), ("bdkv", bdkv_sb),
                    ("bdq", bdq_sb), ("bq_cat", bqc_sb), ("bk_cat", bkc_sb),
                    ("bqr_jm", bqr_sb), ("bkr_jm", bkr_sb)):
        dma("sync", dst[:, :], dr[nm])
    buv = dma("gpsimd", bbuf[:, :], dr["b_uv"].partition_broadcast(128), sem="qBB")
    smalls_done = V("qSm")

    qcv = {}
    qcv["dkv"] = ("qCVa", dma("gpsimd", wdkv_sb,
                     dr["W_dkv"].rearrange("(c p) n -> p c n", p=128),
                     sem="qCVa")["qCVa"])
    qcv["dq"] = ("qCVb", dma("gpsimd", wdq_sb,
                    dr["W_dq"].rearrange("(c p) n -> p c n", p=128),
                    sem="qCVb")["qCVb"])
    qcv["kr"] = ("qCVc", dma("gpsimd",
                    wkr_sb[:, 0:3072].rearrange("p (c k) -> p c k", k=384),
                    dr["Wkr_jm"].rearrange("(c p) k -> p c k", p=128),
                    sem="qCVc")["qCVc"])
    qcv["kcat"] = ("qCVd", dma("gpsimd", wkcat_sb, dr["Wk_cat"], sem="qCVd")["qCVd"])
    qcv["uv"] = ("qCVe", dma("gpsimd", wuv_sb, dr["W_uv"], sem="qCVe")["qCVe"])
    conv_jobs = [
        (dr["Wq_cat"], wq_sb[:, 0:1024], 128, 1024),
        (dr["Wqr_jm"], wqr_sb[:, 0:384], 128, 384),
        (dr["a_l"].rearrange("(c p) n -> p c n", p=128), a_sb[:, 0:128], 128, 128),
        (dr["bpad"], bpad_sb[0:32, 0:1024], 32, 1024),
    ]
    stage_free = [None, None]

    for half in range(2):
        dma("gpsimd",
            wo_p[:, 4096 * half:4096 * (half + 1)].rearrange(
                "p (j k) -> p j k", k=1024),
            dr["W_o"][512 * half:512 * (half + 1), :].rearrange(
                "(j p) k -> p j k", p=128),
            sem="qWsc")

    wsc_l = [0]

    def p0_gen():
        for i, (src_ap, dst, pp, w) in enumerate(conv_jobs):
            par = i % 2
            ld = dma("sync", wstage[par][0:pp, 0:w], src_ap, sem=qW[par],
                     waits=([stage_free[par]] if stage_free[par] else []))
            yield
            r = cp(dst, wstage[par][0:pp, 0:w], waits=[(qW[par], ld[qW[par]])])
            stage_free[par] = ("sDV", r["sDV"])
            yield
        for c in range(8):
            par = c % 2
            ld = dma("sync", wstage[par][:, :], dr["W_ffn"][128 * c:128 * (c + 1), :],
                     sem=qW[par], waits=[stage_free[par]])
            yield
            r = red(gmax_sb[:, c:c + 1], wstage[par][:, :], ALU.max, absval=True,
                    waits=[(qW[par], ld[qW[par]])])
            stage_free[par] = ("sDV", r["sDV"])
            yield
        r = red(gmax_sb[:, 8:9], gmax_sb[:, 0:8], ALU.max,
                waits=[("sDV", r["sDV"])])
        st0 = dma("sync", p0_scr[0:1, 0:128], gmax_sb[:, 8:9],
                  waits=[("sDV", r["sDV"])])
        yield
        ld0 = dma("sync", bcast_sb[0:1, 0:128], p0_scr[0:1, 0:128],
                  waits=[("qSm", st0["qSm"])])
        yield
        r = red(stm[0:1, 0:1], bcast_sb[0:1, 0:128], ALU.max,
                waits=[("qSm", ld0["qSm"])])
        r2 = ts(stv[0:1, 0:1], stm[0:1, 0:1], float(np.float32(1.0 / 7.0)),
                ALU.mult, waits=[("sDV", r["sDV"])])
        r3 = recip(str_[0:1, 0:1], stv[0:1, 0:1], waits=[("sDV", r2["sDV"])])
        r4 = cp(s32_sb[0:1, 0:1], stv[0:1, 0:1],
                waits=[("sDV", r3["sDV"]), ("sGP", g0)])
        r5 = cp(s32_sb[0:1, 1:2], str_[0:1, 0:1], waits=[("sDV", r4["sDV"])])
        st1 = dma("sync", p0_scr[0:1, 128:130], s32_sb[0:1, 0:2],
                  waits=[("sDV", r5["sDV"])])
        yield
        dma("sync", scl_sb[:, :], p0_scr[0:1, 128:129].partition_broadcast(128),
            waits=[("qSm", st1["qSm"])])
        blr = dma("sync", rscl_sb[:, :],
                  p0_scr[0:1, 129:130].partition_broadcast(128))
        qsc_q = blr["qSm"]
        yield
        for c in range(8):
            par = c % 2
            ld = dma("sync", wstage[par][:, :], dr["W_ffn"][128 * c:128 * (c + 1), :],
                     sem=qW[par], waits=[stage_free[par]])
            yield
            t = ts(wstage[par][:, :], wstage[par][:, :], rscl_sb[:, 0:1], ALU.mult,
                   waits=[(qW[par], ld[qW[par]]), ("qSm", qsc_q)])
            u = act(qabs, wstage[par][:, :], AF.Abs, waits=[("sDV", t["sDV"])])
            yield
            r = ts(qacc, qabs, 0.5, ALU.is_ge, waits=[("sAC", u["sAC"])])
            for j in range(1, 7):
                r1 = ts(qcmp, qabs, float(j) + 0.5, ALU.is_ge,
                        waits=[("sDV", r["sDV"])])
                r = tt(qacc, qacc, qcmp, ALU.add, waits=[("sDV", r1["sDV"])])
                if j % 2 == 0:
                    yield
            g1 = ts(qsgn, wstage[par][:, :], 0.0, ALU.is_ge, waits=[("sDV", r["sDV"])])
            stage_free[par] = ("sDV", g1["sDV"])
            g2 = ts2(qsgn, qsgn, 2.0, ALU.mult, -1.0, ALU.add,
                     waits=[("sDV", g1["sDV"])])
            yield
            qq = tt(qacc, qacc, qsgn, ALU.mult, waits=[("sDV", g2["sDV"])])
            wq_ = ts(wstage_out, qacc, scl_sb[:, 0:1], ALU.mult,
                     waits=[("sDV", qq["sDV"]), ("qWsc", wsc_l[0])])
            st = dma("gpsimd", wf_p[:, 1024 * c:1024 * (c + 1)], wstage_out,
                     sem="qWsc", waits=[("sDV", wq_["sDV"])])
            wsc_l[0] = st["qWsc"]
            yield

    p0q = [p0_gen()]

    def p0_step(n=1):
        for _ in range(n):
            if not p0q:
                return
            try:
                next(p0q[0])
            except StopIteration:
                p0q.pop(0)

    hload_gate = [0, 0]
    hload_seen = [0, 0]
    tbl_free = 0
    ckv_free = 0
    rope_done = 0
    kb_cnt = 0
    cw_cnt = 0
    last_knope_evac = 0
    last_pa_pe = 0
    hT_prev_consumed = 0
    for s in range(NSP):
        tau = 512 * s
        tl = dma("sync", cos_sb[:, :], dr["cosT"][:, tau:tau + 512], sem="qT",
                 waits=([("sDV", tbl_free)] if tbl_free else []))
        tl2 = dma("sync", sin_sb[:, :], dr["sinT"][:, tau:tau + 512], sem="qT")
        tbl_val = tl2["qT"]

        ev_done = [0] * 4
        for i in range(4):
            par = i % 2
            w = [("sPE", hload_seen[par])] if hload_seen[par] else []
            if hload_gate[par]:
                w.append(("sDV", hload_gate[par]))
                hload_gate[par] = 0
            ld = dma("sync", h2[:, 1024 * par:1024 * (par + 1)],
                     dr["h"][tau + 128 * i:tau + 128 * (i + 1), :], sem=qH[par],
                     waits=w)
            ldv = ld[qH[par]]
            for c in range(8):
                bk = 2 * par + c // 4
                w2 = []
                if c == 0:
                    w2 = [(qH[par], ldv)]
                    if s == 0 and i == 0:
                        w2.append(("qSm", smalls_done))
                    if i >= 2:
                        w2 += [("sAC", ev_done[2 * par]), ("sAC", ev_done[2 * par + 1])]
                    elif s > 0:
                        w2 += [("sDV", hT_prev_consumed), ("sAC", hT_prev_ac)]
                last = tr(PS[:, 512 * bk + 128 * (c % 4):512 * bk + 128 * (c % 4) + 128],
                          h2[:, 1024 * par + 128 * c:1024 * par + 128 * (c + 1)],
                          waits=w2)
                if c == 3 or c == 7:
                    dst = hT[:, 0:4096].rearrange("p (c t) -> p c t", t=512)[
                        :, 4 * (c // 4):4 * (c // 4) + 4, 128 * i:128 * (i + 1)]
                    e = act(dst, PS[:, 512 * bk:512 * (bk + 1)].rearrange(
                                "p (c t) -> p c t", t=128),
                            AF.Copy, waits=[("sPE", last["sPE"])])
                    ev_done[bk] = e["sAC"]
            hload_seen[par] = last["sPE"]
        hT_ready = V("sAC")
        p0_step(3)

        for c in range(8):
            w = []
            if c == 0:
                w = [("sAC", hT_ready)]
                if s == 0:
                    w.append(qcv["dkv"])
                if ckv_free:
                    w.append(("sDV", ckv_free))
            mm(PS[:, 2048:2560], wdkv_sb[:, 128 * c:128 * (c + 1)],
               hT[:, 512 * c:512 * (c + 1)], waits=w, start=(c == 0), stop=(c == 7))
        pe_ckv = V("sPE")
        for c in range(8):
            mm(PS[:, 2560:3072], wdq_sb[:, 128 * c:128 * (c + 1)],
               hT[:, 512 * c:512 * (c + 1)],
               waits=([qcv["dq"]] if s == 0 and c == 0 else []),
               start=(c == 0), stop=(c == 7))
        pe_cq = V("sPE")
        e1 = act(cq_buf[:, 0:512], PS[:, 2048:2560], AF.Identity, bias=bdkv_sb[:, 0:1],
                 waits=[("sPE", pe_ckv)])
        ckv_sb = cq_buf[:, 0:512]
        e2 = act(cq_buf[:, 512:1024], PS[:, 2560:3072], AF.Identity, bias=bdq_sb[:, 0:1],
                 waits=[("sPE", pe_cq)] + ([("qCw", cw_cnt)] if cw_cnt else []))
        ckv_ready = e2["sAC"]
        cw = dma("sync", cq_scr[:, tau:tau + 512], cq_buf[:, 512:1024],
                 sem="qCw", waits=[("sAC", e2["sAC"])])
        cw_cnt = cw["qCw"]
        p0_step(3)

        for blk, bk in enumerate((6, 7, 0)):
            for c in range(8):
                w = []
                if c == 0:
                    if blk == 0 and rope_done:
                        w = [("sDV", rope_done)]
                    if blk == 0 and s == 0:
                        w.append(qcv["kr"])
                    if blk == 2:
                        w = [("sAC", ev_done[0])]
                mm(PS[:, 512 * bk:512 * (bk + 1)],
                   wkr_sb[:, 384 * c + 128 * blk:384 * c + 128 * (blk + 1)],
                   hT[:, 512 * c:512 * (c + 1)], waits=w,
                   start=(c == 0), stop=(c == 7))
        pe_krot = V("sPE")
        c0c = Psb[:, 2048:2560]
        e4 = act(c0s_sb[:, :], PS[:, 512 * 7:512 * 8], AF.Identity, bias=bkr_sb[:, 1:2],
                 waits=[("sPE", pe_krot - 8)])
        e5 = act(c0c, PS[:, 0:512], AF.Identity, bias=bkr_sb[:, 2:3],
                 waits=[("sPE", pe_krot)] + ([("qKb", kb_cnt)] if kb_cnt else []))
        kb1 = dma("sync", rotk_scr[128:256, tau:tau + 512], c0c,
                  sem="qKb", waits=[("sAC", e5["sAC"])])
        e3b = act(c0b_sb[:, :], PS[:, 512 * 6:512 * 7], AF.Identity, bias=bkr_sb[:, 0:1],
                  waits=[("sAC", e4["sAC"])] + ([("qKb", kb_cnt)] if kb_cnt else []))
        e6 = tt(c0b_sb[:, :], c0b_sb[:, :], cos_sb[:, :], ALU.mult,
                waits=[("sAC", e3b["sAC"]), ("qT", tbl_val)])
        e7 = tt(c0s_sb[:, :], c0s_sb[:, :], sin_sb[:, :], ALU.mult,
                waits=[("sDV", e6["sDV"])])
        e8 = tt(c0b_sb[:, :], c0b_sb[:, :], c0s_sb[:, :], ALU.add,
                waits=[("sDV", e7["sDV"])])
        rope_done = e8["sDV"]
        tbl_free = rope_done
        kb2 = dma("sync", rotk_scr[0:128, tau:tau + 512], c0b_sb[:, :],
                  sem="qKb", waits=[("sDV", e8["sDV"])])
        kb_cnt = kb2["qKb"]
        p0_step(3)

        kn_evac = [0, 0]
        for p in range(NP):
            bk = 1 + p % 2
            w = []
            if p < 2:
                w = [("sAC", ckv_ready), ("sAC", ev_done[bk])]
                if s == 0:
                    w.append(qcv["kcat"])
            else:
                w = [("sAC", kn_evac[p % 2])]
            m = mm(PS[:, 512 * bk:512 * (bk + 1)],
                   wkcat_sb[:, 128 * p:128 * (p + 1)], ckv_sb, waits=w)
            e = act(Ksb[:, S * p + tau:S * p + tau + 512],
                    PS[:, 512 * bk:512 * (bk + 1)],
                    AF.Identity, bias=bkc_sb[:, p:p + 1], waits=[("sPE", m["sPE"])])
            kn_evac[p % 2] = e["sAC"]
        last_knope_evac = max(kn_evac)
        p0_step(3)

        v_evac = [0, 0]
        for i in range(4):
            for half in range(2):
                k = 2 * i + half
                bk = 3 + k % 2
                if k == 0:
                    w = [("sAC", ev_done[3]), ("sAC", ckv_ready), ("qBB", buv["qBB"])]
                    if s == 0:
                        w.append(("sGP", g0))
                        w.append(qcv["uv"])
                elif k == 1:
                    w = [("sAC", e1["sAC"])]
                else:
                    w = [("sDV", v_evac[k % 2])]
                m = mm(PS[:, 512 * bk:512 * (bk + 1)],
                       ckv_sb[:, 128 * i:128 * (i + 1)],
                       wuv_sb[:, 512 * half:512 * (half + 1)], waits=w)
                vdst = Vsb[:, 0:NKC * 1040].rearrange(
                    "p (k h c) -> p k h c", h=16, c=65)[
                    :, 4 * s + i, 8 * half:8 * (half + 1), 0:64]
                e = tt(vdst,
                       PS[:, 512 * bk:512 * (bk + 1)].rearrange(
                           "p (h c) -> p h c", c=64),
                       bbuf[:, 512 * half:512 * (half + 1)].rearrange(
                           "p (h c) -> p h c", c=64),
                       ALU.add, waits=[("sPE", m["sPE"])])
                v_evac[k % 2] = e["sDV"]
        ckv_free = max(v_evac)
        hT_prev_consumed = V("sDV")
        hT_prev_ac = V("sAC")
        last_pa_pe = V("sPE")
        p0_step(3)

    while p0q:
        p0_step(4)
    wsc_cnt = V("qWsc")
    phase0_dv = V("sDV")

    for p in range(NP):
        for r0, hh in ((48, 2 * p), (112, 2 * p + 1)):
            src = rotk_scr.rearrange("(b j s) c -> b j s c", b=2, j=8)[
                :, :, hh:hh + 1, :].rearrange("b j s c -> (b j s) c")
            dma("sync", Ksb[r0:r0 + 16, S * p:S * (p + 1)], src, sem="qR",
                waits=[("qKb", kb_cnt), ("sAC", last_knope_evac)])
    krl_total = V("qR")

    qb_state = dict(cq_free=[last_pa_pe, last_pa_pe], rot_done=rope_done,
                    qb_bank_free=max(rope_done, phase0_dv), nope_evacs=[0] * NP,
                    rot_kb=kb_cnt, rq_w1=0)
    qb_ready = {}

    def qbuild_gen(qt, pair_range, first_half):
        tau = 512 * qt
        par = qt % 2
        if first_half:
            ld = dma("sync", cq_buf[:, 512 * par:512 * (par + 1)],
                     cq_scr[:, tau:tau + 512], sem=qC[par],
                     waits=[("qCw", cw_cnt), ("sPE", qb_state["cq_free"][par])])
            qb_state["cq_ld"] = ld[qC[par]]
            tl = dma("sync", cos_sb[:, :], dr["cosT"][:, tau:tau + 512], sem="qT",
                     waits=[("sDV", qb_state["rot_done"])])
            tl2 = dma("sync", sin_sb[:, :], dr["sinT"][:, tau:tau + 512], sem="qT")
            qb_state["tbl"] = tl2["qT"]
            yield
            rot_evacs = {}
            for blk in (2, 0, 1):
                m = mm(PS[:, 3072:3584], wqr_sb[:, 128 * blk:128 * (blk + 1)],
                       cq_buf[:, 512 * par:512 * (par + 1)],
                       waits=([(qC[par], qb_state["cq_ld"])] if blk == 2 else []) +
                             [("sDV", qb_state["qb_bank_free"])])
                dstl = (c0b_sb, c0s_sb, c0s_sb)[blk]
                ew = [("sPE", m["sPE"])]
                if blk == 2:
                    ew.append(("qKb", qb_state["rq_w1"]))
                    if qb_state["rot_done"]:
                        ew.append(("sDV", qb_state["rot_done"]))
                if blk == 0:
                    ew.append(("qKb", qb_state["rot_kb"]))
                if blk == 1:
                    ew.append(("qKb", qb_state["rq_w1"]))
                e = ts(dstl[:, :], PS[:, 3072:3584], bqr_sb[:, blk:blk + 1], ALU.add,
                       waits=ew)
                rot_evacs[blk] = e["sDV"]
                qb_state["qb_bank_free"] = max(qb_state["qb_bank_free"], e["sDV"])
                if blk == 2:
                    w1 = dma("sync", rotq_scr[128:256, 512 * par:512 * (par + 1)],
                             c0s_sb[:, :], sem="qKb", waits=[("sDV", e["sDV"])])
                    qb_state["rq_w1"] = w1["qKb"]
                yield
            e6 = tt(c0b_sb[:, :], c0b_sb[:, :], cos_sb[:, :], ALU.mult,
                    waits=[("sDV", rot_evacs[0]), ("qT", qb_state["tbl"])])
            yield
            e7 = tt(c0s_sb[:, :], c0s_sb[:, :], sin_sb[:, :], ALU.mult,
                    waits=[("sDV", e6["sDV"])])
            yield
            e8 = tt(c0b_sb[:, :], c0b_sb[:, :], c0s_sb[:, :], ALU.add,
                    waits=[("sDV", e7["sDV"])])
            qb_state["rot_done"] = e8["sDV"]
            w2 = dma("sync", rotq_scr[0:128, 512 * par:512 * (par + 1)],
                     c0b_sb[:, :], sem="qKb", waits=[("sDV", e8["sDV"])])
            qb_state["rq_w"] = w2["qKb"]
            qb_state["rot_kb"] = w2["qKb"]
            yield
        for p in pair_range:
            m = mm(PS[:, 3072:3584], wq_sb[:, 128 * p:128 * (p + 1)],
                   cq_buf[:, 512 * par:512 * (par + 1)],
                   waits=[(qC[par], qb_state["cq_ld"]),
                          ("sDV", qb_state["qb_bank_free"])])
            ew = [("sPE", m["sPE"])]
            if qt > 0:
                ew.append(("sPE", sbd[((qt - 1) * NP + p) * NKC + NKC - 1]))
            e = ts(Qsb[:, 512 * p:512 * (p + 1)], PS[:, 3072:3584],
                   bqc_sb[:, p:p + 1], ALU.add, waits=ew)
            qb_state["qb_bank_free"] = e["sDV"]
            qb_state["nope_evacs"][p] = e["sDV"]
            for r0, hh in ((48, 2 * p), (112, 2 * p + 1)):
                src2 = rotq_scr[:, 512 * par:512 * (par + 1)].rearrange(
                    "(b j s) c -> b j s c", b=2, j=8)[:, :, hh:hh + 1, :].rearrange(
                    "b j s c -> (b j s) c")
                dma("sync", Qsb[r0:r0 + 16, 512 * p:512 * (p + 1)], src2,
                    sem="qRp%d" % p,
                    waits=[("qKb", qb_state["rq_w"]), ("sDV", e["sDV"])])
            qb_ready[(qt, p)] = (V("qRp%d" % p), e["sDV"])
            yield
        if pair_range[-1] == NP - 1:
            qb_state["cq_free"][par] = V("sPE")
        qb_state["done_qR"] = V("qR")
        qb_state["done_dv"] = V("sDV")

    NGT = NQT * NP * NKC
    sbd = [0] * (NGT + 1)
    expd = [0] * (NGT + 1)
    denb = [0] * (NGT + 1)
    dnev = [0] * (NQT * NP + 1)
    oune = [0] * (NQT * NP + 1)
    norm_ = [0] * (NQT * NP + 1)
    bcf_ = [0] * (NQT * NP + 1)
    rdw = [0] * (NQT * NP + 1)

    for _ in qbuild_gen(0, list(range(NP)), True):
        pass

    tail_state = dict(wbuf_free=[0, 0, 0, 0], bbuf_free=0, h2_free=[0, 0],
                      h2_qh=[0, 0], tA_q=0, bank7_free=0, x1T_free=[0, 0],
                      t1_mm_done=0, x1w=0, bo_ld=0, bf_ld=0, xT_evac=0,
                      ln_out=0, lora_done=0, sq_free=0, wpre=None)
    bg_tail = []
    bg_qb = []
    bg_ep = []
    gctr = [0]

    for qt in range(NQT):
        qR_needed = qb_state["done_qR"]
        qb_dv = qb_state["done_dv"]
        base_pair = qt * NP
        def ep_start(PIDX2):
            lastG = PIDX2 * NKC + NKC - 1
            op_ = 512 * (PIDX2 % 2)
            cp(oun_sb[0:64, op_:op_ + 512], PS[0:64, 2048:2560],
               waits=[("sPE", denb[lastG])] +
                     ([("qOD", norm_[PIDX2 - 2])] if PIDX2 > 1 else []))
            e = cp(oun_sb[64:128, op_:op_ + 512], PS[0:64, 2560:3072])
            oune[PIDX2] = e["sDV"]
            c1 = act(bcast_sb[0:1, 0:512], PS[64:65, 2048:2560], AF.Copy,
                     waits=[("sPE", denb[lastG])] +
                           ([("sDV", bcf_[PIDX2 - 1])] if PIDX2 > 0 else []))
            c2 = act(bcast_sb[32:33, 0:512], PS[64:65, 2560:3072], AF.Copy)
            dnev[PIDX2] = c2["sAC"]
            return e["sDV"], c2["sAC"]

        def ep_chain_gen(PIDX2, e_dv, c2_dv):
            op_ = 512 * (PIDX2 % 2)
            oup = oun_sb[:, op_:op_ + 512]
            Wg = (PIDX2 + 1) * NKC + 1
            def ph(x):
                return Wg + max(1, (x * NKC) // 32)
            while gctr[0] < ph(2):
                yield "spin"
            wdma0 = dma("gpsimd", rden[PIDX2:PIDX2 + 1, 0:512], bcast_sb[0:1, 0:512],
                        sem="qBC", waits=[("sAC", c2_dv)] +
                        ([("qOD", norm_[PIDX2 - 1])] if PIDX2 > 0 else []))
            wdma = dma("gpsimd", rden[PIDX2:PIDX2 + 1, 512:1024],
                       bcast_sb[32:33, 0:512], sem="qBC")
            rdw[PIDX2] = wdma["qBC"]
            while gctr[0] < ph(4):
                yield "spin"
            bl = dma("gpsimd", bcast_sb[0:64, :],
                     rden[PIDX2:PIDX2 + 1, 0:512].partition_broadcast(64),
                     sem="qBC", waits=[("qBC", wdma["qBC"])])
            bl2 = dma("gpsimd", bcast_sb[64:128, :],
                      rden[PIDX2:PIDX2 + 1, 512:1024].partition_broadcast(64),
                      sem="qBC")
            while gctr[0] < ph(7):
                yield "spin"
            rr = recip(bcast_sb[:, :], bcast_sb[:, :],
                       waits=[("qBC", bl2["qBC"]), ("sDV", e_dv)])
            nn = tt(oup, oup, bcast_sb[:, :], ALU.mult,
                    waits=[("sDV", rr["sDV"])])
            bcf_[PIDX2] = nn["sDV"]
            while gctr[0] < ph(12):
                yield "spin"
            od = dma("scalar", o_dram[PIDX2 * 128:(PIDX2 + 1) * 128, :],
                     oup, sem="qOD", waits=[("sDV", nn["sDV"])])
            norm_[PIDX2] = od["qOD"]

        def emit_S(p, kc):
            G = (qt * NP + p) * NKC + kc
            pi = G % 2
            w = []
            if G >= 2:
                w.append(("sAC", expd[G - 2]))
            if kc == 0:
                qr_, dv_ = qb_ready[(qt, p)]
                w += [("qRp%d" % p, qr_), ("sDV", dv_)]
                if qt == 0:
                    w.append(("qR", krl_total))
            mm(PS[:, 1024 * pi:1024 * pi + 512],
               Ksb[0:64, S * p + 128 * kc:S * p + 128 * (kc + 1)],
               Qsb[0:64, 512 * p:512 * (p + 1)], waits=w, tp=(0, 0))
            r = mm(PS[:, 1024 * pi + 512:1024 * pi + 1024],
                   Ksb[64:128, S * p + 128 * kc:S * p + 128 * (kc + 1)],
                   Qsb[64:128, 512 * p:512 * (p + 1)], tp=(64, 0))
            sbd[G] = r["sPE"]

        def emit_exp(p, kc):
            G = (qt * NP + p) * NKC + kc
            pi = G % 2
            p3 = G % 4
            w = [("sPE", sbd[G])]
            if G >= 4:
                w.append(("sPE", denb[G - 4]))
            r = act(PSB4[p3], PS[:, 1024 * pi:1024 * (pi + 1)],
                    AF.Exp, waits=w, scale=0.125)
            expd[G] = r["sAC"]

        def emit_pv(p, kc):
            G = (qt * NP + p) * NKC + kc
            PIDX = qt * NP + p
            p3 = G % 4
            first = (kc == 0)
            lastk = (kc == NKC - 1)
            w = [("sAC", expd[G])]
            if first and PIDX > 0:
                w += [("sDV", oune[PIDX - 1]), ("sAC", dnev[PIDX - 1])]
            mm(PS[0:65, 2048:2560],
               Vsb[:, 1040 * kc + 65 * (2 * p):1040 * kc + 65 * (2 * p) + 65],
               PSB4[p3][:, 0:512], waits=w,
               start=first, stop=lastk, tp=(0, 0))
            r = mm(PS[0:65, 2560:3072],
                   Vsb[:, 1040 * kc + 65 * (2 * p + 1):1040 * kc + 65 * (2 * p + 1) + 65],
                   PSB4[p3][:, 512:1024],
                   start=first, stop=lastk, tp=(0, 0))
            denb[G] = r["sPE"]

        exp_total = (qt + 1) * NP * NKC
        tau = 512 * qt

        def ln_chain_gen(prev_dv, par, buf=None, cells=None):
            bA = tA if buf is None else buf
            c1_, c2_, cm_, cv_, cr_ = cells if cells else (st1, st2, stm, stv, str_)
            sqd = h2[:, 1024 * par:1024 * (par + 1)]
            s1 = red(c1_[:, :], bA[:, :], ALU.add, waits=[("sDV", prev_dv)])
            sq = tt(sqd, bA[:, :], bA[:, :], ALU.mult,
                    waits=[("sDV", prev_dv)])
            yield
            s2 = red(c2_[:, :], sqd, ALU.add, waits=[("sDV", sq["sDV"])])
            tail_state["sq_free"] = s2["sDV"]
            m1 = ts(cm_[:, :], c1_[:, :], float(np.float32(1.0 / 1024.0)), ALU.mult,
                    waits=[("sDV", s1["sDV"])])
            yield
            v0 = ts(cv_[:, :], c2_[:, :], float(np.float32(1.0 / 1024.0)), ALU.mult,
                    waits=[("sDV", s2["sDV"])])
            ms = tt(c1_[:, :], cm_[:, :], cm_[:, :], ALU.mult,
                    waits=[("sDV", m1["sDV"])])
            yield
            vv = tt(cv_[:, :], cv_[:, :], c1_[:, :], ALU.subtract,
                    waits=[("sDV", ms["sDV"]), ("sDV", v0["sDV"])])
            yield
            yield
            l1 = act(c2_[:, :], cv_[:, :], AF.Ln, bias=eps_sb[:, 0:1],
                     waits=[("sDV", vv["sDV"])])
            l2 = act(cr_[:, :], c2_[:, :], AF.Exp, scale=-0.5,
                     waits=[("sAC", l1["sAC"])])
            yield
            n1 = ts(bA[:, :], bA[:, :], cm_[:, 0:1], ALU.subtract,
                    waits=[("sDV", vv["sDV"]), ("sAC", l2["sAC"])])
            yield
            n2 = ts(bA[:, :], bA[:, :], cr_[:, 0:1], ALU.mult,
                    waits=[("sDV", n1["sDV"])])
            tail_state["ln_out"] = n2["sDV"]

        def tail_gen(qt=qt, tau=tau):
            st = tail_state
            BK7 = PS[:, 3584:4096]
            x1w_s = [0, 0, 0, 0]
            last_qt = False
            bb = dma("gpsimd", bbuf[:, :], dr["b_o"].partition_broadcast(128),
                     sem="qBB", waits=[("sDV", st["bbuf_free"])]
                     if st["bbuf_free"] else [("sDV", ckv_free)])
            st["bo_ld"] = bb["qBB"]
            oldv = {}
            lds = st["wpre"] if st["wpre"] else {}
            st["wpre"] = None
            for cp_ in range(4):
                if cp_ not in lds:
                    wl = [("sPE", st["wbuf_free"][cp_])] \
                        if st["wbuf_free"][cp_] else [("qWsc", wsc_cnt)]
                    lds[cp_] = dma("gpsimd", wbuf[:, 2048 * cp_:2048 * (cp_ + 1)],
                                   wo_p[:, 2048 * cp_:2048 * (cp_ + 1)],
                                   sem=qB[cp_], waits=wl)[qB[cp_]]
                    yield
            t2lds = {}
            if last_qt:
                VsbF = (Vsb.ap() if hasattr(Vsb, "ap") else Vsb).bitcast(F32)
                tA2 = VsbF[:, 512:1536]
                cells2 = (VsbF[:, 1536:1537], VsbF[:, 1537:1538],
                          VsbF[:, 1538:1539], VsbF[:, 1539:1540],
                          VsbF[:, 1540:1541])
                bbuf2 = Vsb[:, 0:1024]
                wfk = {}
                wtot = 0
                for cp_ in range(4):
                    pown = min(NP - 1, (2048 * (cp_ + 1) - 1) // S)
                    gidx = (qt * NP + pown) * NKC + NKC - 1
                    while not sbd[gidx]:
                        yield "spin"
                    d = dma("gpsimd", Ksb[:, 2048 * cp_:2048 * (cp_ + 1)],
                            wf_p[:, 2048 * cp_:2048 * (cp_ + 1)], sem="qWF",
                            waits=[("sPE", sbd[gidx])] +
                                  ([("qWF", wtot)] if wtot else []))
                    wtot = d["qWF"]
                    wfk[cp_] = wtot
                    yield
                while not denb[NGT - 1]:
                    yield "spin"
                bb2 = dma("gpsimd", bbuf2, dr["b_ffn"].partition_broadcast(128),
                          sem="qBB", waits=[("sPE", denb[NGT - 1]),
                                            ("qBB", st["bo_ld"])])
                st["bank6_free"] = qb_state["qb_bank_free"]
                ctxL = dict(base=3072, tA=tA2, cells=cells2, bbuf=bbuf2,
                            bfld=bb2["qBB"], tAq="tA_q2", bkfree="bank6_free",
                            wf=lambda cp_, c, half: (
                                Ksb[:, 2048 * cp_ + 1024 * (c % 2) + 512 * half:
                                    2048 * cp_ + 1024 * (c % 2) + 512 * (half + 1)],
                                ("qWF", wfk[cp_])))
                st["tA_q2"] = 0

            def issue_xld(sub2):
                par2 = sub2 % 2
                w = [("qX", x1w_s[sub2])]
                if st["h2_free"][par2]:
                    w.append(("sDV", st["h2_free"][par2]))
                if st["h2_qh"][par2]:
                    w.append((qTH[par2], st["h2_qh"][par2]))
                xl = dma("sync", h2[:, 1024 * par2:1024 * (par2 + 1)],
                         x1_scr[128 * sub2:128 * (sub2 + 1), :], sem=qTH[par2],
                         waits=w)
                st["h2_qh"][par2] = xl[qTH[par2]]
                return xl[qTH[par2]]

            xlds = {}

            def t2_sub_gen(sub, ctx):
                par = sub % 2
                base = ctx["base"]
                BKx = PS[:, base:base + 512]
                tAx = ctx["tA"]
                xldv = xlds.pop(sub, None)
                if xldv is None:
                    xldv = issue_xld(sub)
                for rnd in range(2):
                    for c4 in range(4):
                        c = 4 * rnd + c4
                        w2 = []
                        if c4 == 0:
                            w2 = [(qTH[par], xldv), ("sDV", st[ctx["bkfree"]])]
                            if st["x1T_free"][par]:
                                w2.append(("sPE", st["x1T_free"][par]))
                        last = tr(PS[:, base + 128 * c4:base + 128 * (c4 + 1)],
                                  h2[:, 1024 * par + 128 * c:1024 * par + 128 * (c + 1)],
                                  waits=w2)
                    dst = x1T2[:, 0:2048].rearrange("p (g c t) -> p g c t", g=2, t=128)[
                        :, par, 4 * rnd:4 * rnd + 4, :]
                    e = cp(dst, BKx.rearrange("p (c t) -> p c t", t=128),
                           waits=[("sPE", last["sPE"])])
                    st[ctx["bkfree"]] = e["sDV"]
                    yield
                st["xT_evac"] = e["sDV"]
                for c in range(8):
                    w2 = [("sDV", st["xT_evac"])] if c == 0 else []
                    m = mm(PS[0:16, base:base + 128], a_sb[:, 16 * c:16 * (c + 1)],
                           x1T2[:, 1024 * par + 128 * c:1024 * par + 128 * (c + 1)],
                           waits=w2, start=(c == 0), stop=(c == 7))
                zw = [("sPE", m["sPE"])]
                if st.get("lora_done"):
                    zw.append(("sPE", st["lora_done"]))
                ze = cp(zb_sb[0:16, 0:128], PS[0:16, base:base + 128], waits=zw)
                st[ctx["bkfree"]] = ze["sDV"]
                yield
                e1d = 0
                for half in range(2):
                    for c in range(8):
                        cp_ = c // 2
                        rhs, wwait = ctx["wf"](cp_, c, half)
                        w2 = [wwait]
                        if c == 0:
                            w2.append(("sDV", st[ctx["bkfree"]]))
                        m = mm(BKx,
                               x1T2[:, 1024 * par + 128 * c:1024 * par + 128 * (c + 1)],
                               rhs, waits=w2, start=(c == 0), stop=False)
                        if c % 2 == 1:
                            if sub == 3 and half == 1 and qt + 1 < NQT:
                                st["wbuf_free"][cp_] = m["sPE"]
                                if st["wpre"] is None:
                                    st["wpre"] = {}
                                st["wpre"][cp_] = dma(
                                    "gpsimd", wbuf[:, 2048 * cp_:2048 * (cp_ + 1)],
                                    wo_p[:, 2048 * cp_:2048 * (cp_ + 1)],
                                    sem=qB[cp_],
                                    waits=[("sPE", m["sPE"])])[qB[cp_]]
                            yield
                    m = mm(BKx, zb_sb[0:32, 0:128],
                           bpad_sb[0:32, 512 * half:512 * (half + 1)],
                           start=False, stop=True)
                    st["lora_done"] = m["sPE"]
                    e1w = [("sPE", m["sPE"]), ("qBB", ctx["bfld"]),
                           (qTH[par], st["h2_qh"][par])]
                    if st[ctx["tAq"]] and half == 0:
                        e1w.append(st[ctx["tAq"]])
                    e1 = tt(tAx[:, 512 * half:512 * (half + 1)], BKx,
                            h2[:, 1024 * par + 512 * half:1024 * par + 512 * (half + 1)],
                            ALU.add, waits=e1w + ([("sDV", e1d)] if e1d else []))
                    e1d = e1["sDV"]
                    st[ctx["bkfree"]] = e1["sDV"]
                    yield
                st["x1T_free"][par] = m["sPE"]
                e2 = tt(tAx[:, :], tAx[:, :], ctx["bbuf"][:, :], ALU.add,
                        waits=[("sDV", e1d), ("qBB", ctx["bfld"])])
                if ctx["bbuf"] is bbuf:
                    st["bbuf_free"] = e2["sDV"]
                yield
                for _ in ln_chain_gen(e2["sDV"], par, tAx, ctx["cells"]):
                    yield
                st["h2_free"][par] = st["sq_free"]
                if not last_qt and sub + 2 <= 3:
                    xlds[sub + 2] = issue_xld(sub + 2)
                od = dma("scalar", out_d[tau + 128 * sub:tau + 128 * (sub + 1), :],
                         tAx[:, :], sem="qO", waits=[("sDV", st["ln_out"])])
                st[ctx["tAq"]] = ("qO", od["qO"])
                yield

            for sub in range(4):
                par = sub % 2
                w0 = [("sDV", st["h2_free"][par])] if st["h2_free"][par] else []
                if st["h2_qh"][par]:
                    w0.append((qTH[par], st["h2_qh"][par]))
                if qt == 0 and sub < 2:
                    w0.append(("sPE", hload_seen[par]))
                hld = dma("sync", h2[:, 1024 * par:1024 * (par + 1)],
                          dr["h"][tau + 128 * sub:tau + 128 * (sub + 1), :],
                          sem=qTH[par], waits=w0)
                st["h2_qh"][par] = hld[qTH[par]]
                e1d = 0
                for half in range(2):
                    for c in range(8):
                        cp_ = c // 2
                        if sub == 0 and half == 0:
                            nidx = min(qt * NP + c + 1, NQT * NP - 1)
                            while not norm_[nidx]:
                                yield "spin"
                            w0 = [("qOD", norm_[qt * NP + c])]
                            if c == 0 and st["t1_mm_done"]:
                                w0.append(("sPE", st["t1_mm_done"]))
                            if c > 0:
                                w0.append(("qOL", oldv[c - 1]))
                            ol = dma("sync", osb[:, 512 * c:512 * (c + 1)],
                                     o_dram[(qt * NP + c) * 128:(qt * NP + c + 1) * 128, :],
                                     sem="qOL", waits=w0)
                            oldv[c] = ol["qOL"]
                        w2 = [(qB[cp_], lds[cp_])]
                        if sub == 0 and half == 0:
                            w2.append(("qOL", oldv[c]))
                        if c == 0:
                            w2 += [("sDV", st["bank7_free"])]
                        m = mm(BK7,
                               osb[:, 512 * c + 128 * sub:512 * c + 128 * (sub + 1)],
                               wbuf[:, 2048 * cp_ + 1024 * (c % 2) + 512 * half:
                                    2048 * cp_ + 1024 * (c % 2) + 512 * (half + 1)],
                               waits=w2, start=(c == 0), stop=(c == 7))
                        if c % 2 == 1:
                            if sub == 3 and half == 1 and not last_qt:
                                st["wbuf_free"][cp_] = m["sPE"]
                                t2lds[cp_] = dma(
                                    "gpsimd", wbuf[:, 2048 * cp_:2048 * (cp_ + 1)],
                                    wf_p[:, 2048 * cp_:2048 * (cp_ + 1)],
                                    sem=qB[cp_],
                                    waits=[("sPE", m["sPE"])])[qB[cp_]]
                            yield
                    e1w = [("sPE", m["sPE"]), (qTH[par], st["h2_qh"][par])]
                    if st["tA_q"] and half == 0:
                        e1w.append(st["tA_q"])
                    if qt == 0 and sub == 0 and half == 0:
                        e1w.append(("sDV", phase0_dv))
                    e1 = tt(tA[:, 512 * half:512 * (half + 1)], BK7,
                            h2[:, 1024 * par + 512 * half:1024 * par + 512 * (half + 1)],
                            ALU.add, waits=e1w + ([("sDV", e1d)] if e1d else []))
                    e1d = e1["sDV"]
                    st["bank7_free"] = e1["sDV"]
                    yield
                if sub == 3:
                    st["t1_mm_done"] = m["sPE"]
                e2 = tt(tA[:, :], tA[:, :], bbuf[:, :], ALU.add,
                        waits=[("sDV", e1d), ("qBB", st["bo_ld"])])
                st["bbuf_free"] = e2["sDV"]
                yield
                for _ in ln_chain_gen(e2["sDV"], par):
                    yield
                st["h2_free"][par] = st["sq_free"]
                xw = dma("gpsimd", x1_scr[128 * sub:128 * (sub + 1), :], tA[:, :],
                         sem="qX", waits=[("sDV", st["ln_out"])])
                st["tA_q"] = ("qX", xw["qX"])
                x1w_s[sub] = xw["qX"]
                yield
                if last_qt:
                    for _ in t2_sub_gen(sub, ctxL):
                        yield
            if not last_qt:
                bb = dma("gpsimd", bbuf[:, :], dr["b_ffn"].partition_broadcast(128),
                         sem="qBB", waits=[("sDV", st["bbuf_free"])])
                st["bf_ld"] = bb["qBB"]
                ctxS = dict(base=3584, tA=tA, cells=None, bbuf=bbuf,
                            bfld=bb["qBB"], tAq="tA_q", bkfree="bank7_free",
                            wf=lambda cp_, c, half: (
                                wbuf[:, 2048 * cp_ + 1024 * (c % 2) + 512 * half:
                                     2048 * cp_ + 1024 * (c % 2) + 512 * (half + 1)],
                                (qB[cp_], t2lds[cp_])))
                xlds[0] = issue_xld(0)
                xlds[1] = issue_xld(1)
                for sub in range(4):
                    for _ in t2_sub_gen(sub, ctxS):
                        yield

        bg_tail.append(tail_gen())
        NGQ = NP * NKC
        for g in range(NGQ):
            p, kc = g // NKC, g % NKC
            gctr[0] = qt * NGQ + g
            S_.gpos = (qt, p, kc)
            emit_S(p, kc)
            emit_exp(p, kc)
            if g >= 2:
                emit_pv((g - 2) // NKC, (g - 2) % NKC)
            if kc == 1 and qt * NP + p > 0:
                P2 = qt * NP + p - 1
                e_dv, c2_dv = ep_start(P2)
                bg_ep.append(ep_chain_gen(P2, e_dv, c2_dv))
            if bg_ep:
                try:
                    next(bg_ep[0])
                except StopIteration:
                    bg_ep.pop(0)
            if kc == NKC // 2 and qt + 1 < NQT and p in (3, 7):
                if p == 3:
                    bg_qb.append(qbuild_gen(qt + 1, [0, 1, 2], True))
                else:
                    bg_qb.append(qbuild_gen(qt + 1, [3, 4, 5, 6], False))
            for q_ in (() if kc in (0, 1, 2) else (bg_qb, bg_tail)):
                if not q_:
                    continue
                for _spin in range(64):
                    try:
                        v = next(q_[0])
                    except StopIteration:
                        q_.pop(0)
                        break
                    if v != "spin":
                        break
        emit_pv(NP - 1, NKC - 2)
        emit_pv(NP - 1, NKC - 1)
        if qt == NQT - 1:
            P2 = qt * NP + NP - 1
            e_dv, c2_dv = ep_start(P2)
            bg_ep.append(ep_chain_gen(P2, e_dv, c2_dv))
        if qt + 1 < NQT:
            bg_qb.append(qbuild_gen(qt + 1, [7], False))
        while bg_qb:
            try:
                next(bg_qb[0])
            except StopIteration:
                bg_qb.pop(0)

        if qt == NQT - 1:
            for p_ in range(NP):
                pass
    gctr[0] = 1 << 60
    for q_ in (bg_ep, bg_qb, bg_tail):
        while q_:
            try:
                next(q_[0])
            except StopIteration:
                q_.pop(0)
    wonly("gpsimd", [("qO", 16 * 4 * NQT)])

    from contextlib import ExitStack
    with ExitStack() as ctx:
        sems = {}
        for s in S_.cnt:
            sems[s] = ctx.enter_context(nc.semaphore(s))
        block = ctx.enter_context(nc.Block())

        def mk(eng_name):
            def emit(eng):
                for fn, waits, incs in S_.events[eng_name]:
                    for s, v in waits:
                        if v > 0:
                            eng.wait_ge(sems[s], v)
                    if fn is None:
                        continue
                    ins = fn(eng)
                    for s, a in incs:
                        ins = ins.then_inc(sems[s], a)
            return emit
        block.sync(mk("sync"))
        block.gpsimd(mk("gpsimd"))
        block.tensor(mk("tensor"))
        block.scalar(mk("scalar"))
        block.vector(mk("vector"))
    return nc


def kernel(**inputs):
    h_all = np.asarray(inputs["L0_attn_h"], np.float32)
    B, S, _ = h_all.shape
    const = _host_constants(inputs, S)
    nc = build_program(S)
    shared = {k: np.ascontiguousarray(v) for k, v in const.items()
              if k not in ("W_dkv", "W_dq", "W_uv", "W_o", "W_ffn", "a_l")}
    shared.update({
        "W_dkv": const["W_dkv"], "W_dq": const["W_dq"], "W_uv": const["W_uv"],
        "W_o": const["W_o"], "W_ffn": const["W_ffn"], "a_l": const["a_l"],
    })
    in_maps = []
    for c in range(B):
        m = dict(shared)
        m["h"] = np.ascontiguousarray(h_all[c])
        in_maps.append(m)
    res = run_bass_kernel_spmd(nc, in_maps, core_ids=list(range(B)))
    out = np.stack([np.asarray(r["out"]) for r in res.results], axis=0)
    return out.astype(np.float32)
